# version 18
# baseline (speedup 1.0000x reference)
import contextlib
import numpy as np
import ml_dtypes
import concourse.bass as bass
import concourse.mybir as mybir
from concourse.bass_utils import run_bass_kernel_spmd

F32 = mybir.dt.float32
BF16 = mybir.dt.bfloat16
I32 = mybir.dt.int32
AF = mybir.ActivationFunctionType
ALU = mybir.AluOpType
BF = ml_dtypes.bfloat16

P = 128
T = 2048
DM = 1024
DC = 8
NST = 16
HNST = NST // 2
R = 64
H = 3072
MC = 24
TC = 512
NCH = T // TC
FH = TC // 2
S127 = float(np.float32(1.0) / np.float32(127.0))
EPS = 1e-5


class Chain:
    def __init__(self, sem, step=1):
        self.sem, self.step, self.n = sem, step, 0

    def tick(self, ins):
        ins.then_inc(self.sem, self.step)
        self.n += self.step
        return self.n


def build_program():
    nc = bass.Bass("TRN2", target_bir_lowering=False)

    u_in = nc.dram_tensor("u", [T, DM], F32, kind="ExternalInput").ap()
    wg_in = nc.dram_tensor("wg_prep", [MC, P, DM], I32, kind="ExternalInput").ap()
    wu_in = nc.dram_tensor("wu_prep", [MC, P, DM], I32, kind="ExternalInput").ap()
    wd_in = nc.dram_tensor("wd_prep", [DC, P, H], I32, kind="ExternalInput").ap()
    wxdtT_in = nc.dram_tensor("wxdtT", [P, DC * R], F32, kind="ExternalInput").ap()
    wdtT_in = nc.dram_tensor("wdtT", [R, DM], F32, kind="ExternalInput").ap()
    wbT_in = nc.dram_tensor("wbT", [P, DC * NST], F32, kind="ExternalInput").ap()
    wcT_in = nc.dram_tensor("wcT", [P, DC * NST], F32, kind="ExternalInput").ap()
    alog_in = nc.dram_tensor("alog_p", [P, DC * NST], F32, kind="ExternalInput").ap()
    bdt_in = nc.dram_tensor("bdt_p", [P, DC], F32, kind="ExternalInput").ap()
    dsk_in = nc.dram_tensor("d_p", [P, DC], F32, kind="ExternalInput").ap()
    bxdt_in = nc.dram_tensor("bxdt", [R, 1], F32, kind="ExternalInput").ap()
    bb_in = nc.dram_tensor("bb", [NST, 1], F32, kind="ExternalInput").ap()
    bc_in = nc.dram_tensor("bc", [NST, 1], F32, kind="ExternalInput").ap()
    idf_in = nc.dram_tensor("idf", [P, P], F32, kind="ExternalInput").ap()
    onesb_in = nc.dram_tensor("ones_b", [P, 1], BF16, kind="ExternalInput").ap()
    onesrow_in = nc.dram_tensor("ones_row", [1, P], BF16, kind="ExternalInput").ap()

    wg_scr = nc.dram_tensor("wg_scr", [MC, P, DM], BF16, kind="Internal").ap()
    wu_scr = nc.dram_tensor("wu_scr", [MC, P, DM], BF16, kind="Internal").ap()
    wd_scr = nc.dram_tensor("wd_scr", [DC, P, H], BF16, kind="Internal").ap()
    hid_hbm = nc.dram_tensor("hid_scr", [MC, P, TC], BF16, kind="Internal").ap()
    y_out = nc.dram_tensor("y_out", [T, DM], F32, kind="ExternalOutput").ap()

    es = contextlib.ExitStack()

    def sb(name, shape, dtype):
        return es.enter_context(nc.sbuf_tensor(name, shape, dtype))

    def psum(name, shape, dtype=F32):
        return es.enter_context(nc.psum_tensor(name, shape, dtype))

    xT = sb("xT", [P, DC * T], F32)
    x2n = sb("x2n", [P, 2 * DC * TC], BF16)
    dtb = sb("dtb", [P, 2 * TC], BF16)
    unb = sb("unb", [P, DC * TC], BF16)
    mam = sb("mam", [P, (2 * NST + HNST) * TC], BF16)
    bbc = sb("bbc", [P, NST * TC], BF16)
    cbc = sb("cbc", [P, NST * TC], BF16)
    aux = sb("aux", [P, 2 * TC], BF16)
    rows = sb("rows", [1, 3 * TC], F32)
    rowb = sb("rowb", [1, TC], BF16)
    rbc = sb("rbc", [P, 2 * TC], BF16)
    smw = sb("smw", [P, 2 * DC * NST + DC * R], BF16)
    wdtTb = sb("wdtTb", [R, DM], BF16)
    xdtb = sb("xdtb", [R, TC], BF16)
    bscb = sb("bscb", [NST, 2 * TC], BF16)
    aneg = sb("aneg", [P, DC * NST], F32)
    bdtp = sb("bdtp", [P, DC], F32)
    dskp = sb("dskp", [P, DC], F32)
    bxdtc = sb("bxdtc", [R, 1], F32)
    bbcol = sb("bbcol", [NST, 1], F32)
    bccol = sb("bccol", [NST, 1], F32)
    idf = sb("idf_sb", [P, P], F32)
    onesb = sb("onesb_sb", [P, 1], BF16)
    onesrow = sb("onesrow_sb", [1, P], BF16)
    hlast = sb("hlast", [P, DC * NST], BF16)
    fix16 = sb("fix16", [P, NST], BF16)
    cst = sb("cst", [P, 2], F32)
    hidt = sb("hidt", [P, MC * FH], BF16)
    wgu = sb("wgu", [P, 4 * DM], BF16)
    sgub = sb("sgub", [P, 4 * TC], BF16)
    hidm = sb("hidm", [P, 2 * TC], BF16)
    wdm2 = wgu[:][:, 0:H]

    auxv = aux[:]
    t1s = auxv[:, 0:TC]
    etx = auxv[:, TC:2 * TC]

    smwv = smw[:]
    wbTb = smwv[:, 0:DC * NST]
    wcTb = smwv[:, DC * NST:2 * DC * NST]
    wxdtTb = smwv[:, 2 * DC * NST:2 * DC * NST + DC * R]

    mamx = mam[:]
    abar = mamx[:, 0:NST * TC]
    abar_h0 = mamx[:, 0:HNST * TC]
    abar_h1 = mamx[:, HNST * TC:NST * TC]
    xbuf = mamx[:, NST * TC:2 * NST * TC]
    hbuf = mamx[:, 2 * NST * TC:(2 * NST + HNST) * TC]
    rstd2_bc = mamx[:, HNST * TC:HNST * TC + TC]
    mr2_bc = mamx[:, HNST * TC + TC:HNST * TC + 2 * TC]
    uld = mam[:].bitcast(F32)[:, 0:2 * DM]

    def slotv(c):
        return x2n[:][:, (c % 2) * DC * TC:((c % 2) + 1) * DC * TC]


    ps_g = psum("ps_g", [P, TC])
    ps_u = psum("ps_u", [P, TC])
    ps_d = psum("ps_d", [P, TC])
    ps_s1 = psum("ps_s1", [P, TC])
    ps_s2 = psum("ps_s2", [P, TC])
    ps_t1f = psum("ps_t1", [P, TC])
    ps_t2f = psum("ps_t2", [P, TC])
    ps_p = psum("ps_p", [P, TC])
    ps_t1 = ps_t1f[:, 0:P]
    ps_t2 = ps_t2f[:, 0:P]

    def sem(name):
        return es.enter_context(nc.semaphore(name))

    sem_pe, sem_act, sem_dve, sem_po = sem("s_pe"), sem("s_act"), sem("s_dve"), sem("s_po")
    sem_sy, sem_gp = sem("s_sy"), sem("s_gp")
    sem_uld0, sem_uld1 = sem("s_uld0"), sem("s_uld1")
    sem_bbc, sem_cbc = sem("s_bbc"), sem("s_cbc")
    sem_wc0, sem_wc1 = sem("s_wc0"), sem("s_wc1")
    sem_wsc = sem("s_wsc")
    sem_wv0, sem_wv1 = sem("s_wv0"), sem("s_wv1")
    sem_wd = sem("s_wd")
    sem_wd2 = sem("s_wd2")
    sem_sgub = sem("s_sgub")
    sem_hidt = sem("s_hidt")
    sem_out = sem("s_out")
    blk = es.enter_context(nc.Block())

    pe, act, dve, po = Chain(sem_pe), Chain(sem_act), Chain(sem_dve), Chain(sem_po)
    sy, gp = Chain(sem_sy, 16), Chain(sem_gp, 16)
    uldc = [Chain(sem_uld0, 16), Chain(sem_uld1, 16)]
    bbcc, cbcc = Chain(sem_bbc, 16), Chain(sem_cbc, 16)
    wcc = [Chain(sem_wc0, 16), Chain(sem_wc1, 16)]
    wscc = Chain(sem_wsc, 16)
    wvc = [Chain(sem_wv0, 16), Chain(sem_wv1, 16)]
    wdc = Chain(sem_wd, 16)
    wdc2 = Chain(sem_wd2, 16)
    sguc = Chain(sem_sgub, 16)
    hidtc = Chain(sem_hidt, 16)
    outc = Chain(sem_out, 16)

    TEN, ACT, DVE, SY, GP = (blk.tensor, blk.scalar, blk.vector, blk.sync,
                             blk.gpsimd)
    POOL = blk.gpsimd

    def load_act_set(e, set_id):
        e.add_instruction(mybir.InstLoadActFuncSet(
            name=nc.get_next_instruction_name(), act_func_set_id=set_id,
            ins=[], outs=[]))

    psum_free = {}

    def pwait(e, name):
        ch_m = psum_free.get(name)
        if ch_m:
            e.wait_ge(ch_m[0].sem, ch_m[1])

    def pfree(name, chain):
        psum_free[name] = (chain, chain.n)

    rows_s = rows[:][:, 0:TC]
    rows_q = rows[:][:, TC:2 * TC]
    rows_a1 = rows[:][:, 2 * TC:3 * TC]
    rows_a2 = rows_q
    rowb_1 = rowb[:][:, 0:TC]
    rowb2 = hidm[:].bitcast(BF16)[0:1, TC:2 * TC]
    rstd_bc = rbc[:][:, 0:TC]
    mr_bc = rbc[:][:, TC:2 * TC]
    cst_eps = cst[:][:, 0:1]
    cst_one = cst[:][:, 1:2]

    def _(e):
        for ap_out, ap_in in [(idf[:], idf_in), (onesb[:], onesb_in),
                              (onesrow[:], onesrow_in), (bdtp[:], bdt_in),
                              (dskp[:], dsk_in), (bxdtc[:], bxdt_in),
                              (bbcol[:], bb_in), (bccol[:], bc_in)]:
            sy.tick(e.dma_start(out=ap_out, in_=ap_in))
    SY(_)
    sy_setup = sy.n

    def _(e):
        gp.tick(e.dma_start(out=wbTb, in_=wbT_in))
        gp.tick(e.dma_start(out=wcTb, in_=wcT_in))
        gp.tick(e.dma_start(out=wxdtTb, in_=wxdtT_in))
        gp.tick(e.dma_start(out=wdtTb[:], in_=wdtT_in))
        gp.tick(e.dma_start(out=aneg[:], in_=alog_in))
    GP(_)
    gp_small = gp.n

    def _(e):
        load_act_set(e, 6)
        e.wait_ge(sem_gp, gp_small)
        act.tick(e.activation(out=aneg[:], in_=aneg[:], func=AF.Exp))
    ACT(_)
    act_aexp = act.n

    def _(e):
        e.wait_ge(sem_act, act_aexp)
        e.memset(cst_eps, EPS)
        e.memset(cst_one, 1.0)
        dve.tick(e.tensor_scalar_mul(aneg[:], aneg[:], -1.0))
    DVE(_)

    u_v = u_in.rearrange("(tt p) d -> p tt d", p=P)
    pe_tile_marks = {}
    act_u_chunk = {}
    uld2 = wgu[:].bitcast(F32)

    def emit_u_tile(tt):
        par = tt % 2
        base = uld if tt < 4 else uld2
        buf = base[:, par * DM:(par + 1) * DM]
        uch = uldc[par]

        def _(e, tt=tt, buf=buf, uch=uch):
            e.wait_ge(sem_sy, sy_setup)
            m = pe_tile_marks.get(tt - 2, 0)
            if m:
                e.wait_ge(sem_pe, m)
            uch.tick(e.dma_start(out=buf, in_=u_v[:, tt, :]))
        SY(_)
        u_mark = uch.n
        act_base = act.n

        def _(e, tt=tt, buf=buf, u_mark=u_mark, uch=uch, act_base=act_base):
            e.wait_ge(uch.sem, u_mark)
            if tt > 0:
                e.wait_ge(sem_act, act_base)
            for dc in range(DC):
                pst = ps_t1 if dc % 2 == 0 else ps_t2
                if dc >= 2:
                    e.wait_ge(sem_act, act_base + dc - 1)
                pe.tick(e.transpose(pst[:], buf[:, dc * P:(dc + 1) * P], idf[:]))
        TEN(_)
        pe_tile_marks[tt] = pe.n
        pe_base = pe.n - DC

        def _(e, tt=tt, pe_base=pe_base):
            for dc in range(DC):
                pst = ps_t1 if dc % 2 == 0 else ps_t2
                e.wait_ge(sem_pe, pe_base + dc + 1)
                act.tick(e.copy(
                    out=xT[:][:, dc * T + tt * P:dc * T + (tt + 1) * P],
                    in_=pst[:]))
        ACT(_)
        if tt % 4 == 3:
            act_u_chunk[tt // 4] = act.n
        pfree("ps_t1", act)
        pfree("ps_t2", act)

    for tt in range(4):
        emit_u_tile(tt)

    def emit_stats_rows_bcast(src_getter, extra_marks, src_bf16=False):
        pe_slot = {}
        for i in range(DC):
            src = src_getter(i)
            qslot = dtb[:][:, (i % 2) * TC:(i % 2 + 1) * TC]

            def _(e, i=i, src=src, qslot=qslot):
                if i == 0:
                    for s_, v in extra_marks:
                        if v > 0:
                            e.wait_ge(s_, v)
                w = pe_slot.get(i - 2 if src_bf16 else i - 1)
                if w:
                    e.wait_ge(sem_pe, w)
                if not src_bf16:
                    e.activation(out=etx, in_=src, func=AF.Copy)
                act.tick(e.activation(out=qslot, in_=src, func=AF.Square))
            ACT(_)
            a_i = act.n

            def _(e, i=i, src=src, qslot=qslot, a_i=a_i):
                e.wait_ge(sem_act, a_i)
                if i == 0:
                    pwait(e, "ps_s1")
                    pwait(e, "ps_s2")
                e.matmul(ps_s1[0:1, :], onesb[:], src if src_bf16 else etx,
                         start=(i == 0), stop=(i == DC - 1))
                pe.tick(e.matmul(ps_s2[0:1, :], onesb[:], qslot,
                                 start=(i == 0), stop=(i == DC - 1)))
            TEN(_)
            pe_slot[i] = pe.n
        pe_stats = pe.n

        def _(e, pe_stats=pe_stats):
            e.wait_ge(sem_pe, pe_stats)
            e.copy(out=rows_s, in_=ps_s1[0:1, :])
            e.copy(out=rows_q, in_=ps_s2[0:1, :])
            e.mul(out=rows_s, in_=rows_s, mul=1.0 / DM)
            e.mul(out=rows_q, in_=rows_q, mul=1.0 / DM)
            act.tick(e.activation(out=rows_a1, in_=rows_s, func=AF.Square))
        ACT(_)
        a_m = act.n

        def _(e, a_m=a_m):
            e.wait_ge(sem_act, a_m)
            dve.tick(e.tensor_sub(rows_a2, rows_q, rows_a1))
        DVE(_)
        d_v = dve.n

        def _(e, d_v=d_v):
            e.wait_ge(sem_dve, d_v)
            e.activation(out=rows_a1, in_=rows_a2, func=AF.Ln,
                         bias=cst_eps[0:1, :])
            act.tick(e.activation(out=rows_a1, in_=rows_a1, func=AF.Exp,
                                  scale=-0.5))
        ACT(_)
        a_r = act.n

        def _(e, a_r=a_r):
            e.wait_ge(sem_act, a_r)
            dve.tick(e.tensor_mul(rows_a2, rows_s, rows_a1))
        DVE(_)
        d_mr = dve.n
        dve_now = dve.n

        def _(e, d_mr=d_mr):
            e.wait_ge(sem_dve, d_mr)
            e.wait_ge(sem_sgub, sguc.n)
            e.activation(out=rowb_1, in_=rows_a1, func=AF.Copy)
            act.tick(e.activation(out=rowb2, in_=rows_a2, func=AF.Copy))
        ACT(_)
        a_rb = act.n

        def _(e, a_rb=a_rb):
            e.wait_ge(sem_act, a_rb)
            pwait(e, "ps_p")
            pwait(e, "ps_s2")
            e.matmul(ps_p[:], onesrow[:], rowb_1, start=True, stop=True)
            pe.tick(e.matmul(ps_s2[:], onesrow[:], rowb2, start=True,
                             stop=True))
        TEN(_)
        pe_b1 = pe.n

        def _(e, pe_b1=pe_b1, dve_now=dve_now):
            e.wait_ge(sem_pe, pe_b1)
            e.wait_ge(sem_dve, dve_now)
            e.copy(out=rstd_bc, in_=ps_p[:])
            act.tick(e.copy(out=mr_bc, in_=ps_s2[:]))
        ACT(_)
        pfree("ps_p", act)
        pfree("ps_s2", act)
        return act.n

    gu_state = {"x2n_ready": 0}
    DNPS = [(ps_g, "ps_g"), (ps_u, "ps_u"), (ps_t1f, "ps_t1"),
            (ps_t2f, "ps_t2")]

    def emit_gu_batch(wc, m0, m1):
        for m in range(m0, m1):
            par = m % 2
            wch = wvc[par]
            wg_sb = wgu[:][:, par * 2 * DM:par * 2 * DM + DM]
            wu_sb = wgu[:][:, par * 2 * DM + DM:(par + 1) * 2 * DM]
            psg, psg_n = (ps_g, "ps_g") if par == 0 else (ps_t1f, "ps_t1")
            psu, psu_n = (ps_u, "ps_u") if par == 0 else (ps_t2f, "ps_t2")
            sg_sb = sgub[:][:, par * 2 * TC:par * 2 * TC + TC]
            ub_sb = sgub[:][:, par * 2 * TC + TC:(par + 1) * 2 * TC]

            pre = gu_state.get(("wvmark", wc, m))
            if pre is None:
                def _(e, m=m, wc=wc, wch=wch, wg_sb=wg_sb, wu_sb=wu_sb):
                    e.wait_ge(sem_wsc, wsc_all)
                    if wc == 0 and m < 2:
                        e.wait_ge(sem_act, act_u_chunk[3])
                    mm = gu_state.get(("pe_w", m - 2))
                    if mm:
                        e.wait_ge(sem_pe, mm)
                    mm2 = gu_state.get("down_done")
                    if mm2 and m < 2:
                        e.wait_ge(sem_pe, mm2)
                    wch.tick(e.dma_start(out=wg_sb, in_=wg_scr[m]))
                    wch.tick(e.dma_start(out=wu_sb, in_=wu_scr[m]))
                SY(_)
                wv_mark = wch.n
            else:
                wv_mark = pre

            def _(e, m=m, wc=wc, wch=wch, wv_mark=wv_mark, wg_sb=wg_sb,
                  wu_sb=wu_sb, psg=psg, psu=psu, psg_n=psg_n, psu_n=psu_n):
                e.wait_ge(wch.sem, wv_mark)
                e.wait_ge(sem_dve, gu_state["x2n_ready"])
                pwait(e, psg_n)
                pwait(e, psu_n)
                sl = slotv(wc)
                for k in range(DC):
                    e.matmul(psg[:], wg_sb[:, k * P:(k + 1) * P],
                             sl[:, k * TC:(k + 1) * TC],
                             start=(k == 0), stop=(k == DC - 1))
                for k in range(DC):
                    ins = e.matmul(psu[:], wu_sb[:, k * P:(k + 1) * P],
                                   sl[:, k * TC:(k + 1) * TC],
                                   start=(k == 0), stop=(k == DC - 1))
                    if k == DC - 1:
                        pe.tick(ins)
            TEN(_)
            gu_state[("pe_w", m)] = pe.n
            pe_gu = pe.n

            def _(e, m=m, m0=m0, m1=m1, wc=wc, pe_gu=pe_gu, psg=psg, psu=psu,
                  sg_sb=sg_sb, ub_sb=ub_sb):
                if m == m0:
                    load_act_set(e, 18)
                e.wait_ge(sem_pe, pe_gu)
                if m < 2:
                    oo = gu_state.get(("otok_done", 0 if wc < 1 else wc - 1), 0) \
                        if wc >= 1 else 0
                    if oo:
                        e.wait_ge(sem_out, oo)
                mm = gu_state.get(("hidtt", m - 2))
                if mm:
                    e.wait_ge(sem_po, mm)
                e.activation(out=sg_sb, in_=psg[:], func=AF.Silu, scale=S127)
                act.tick(e.activation(out=ub_sb, in_=psu[:], func=AF.Copy,
                                      scale=S127))
                if m == m1 - 1:
                    load_act_set(e, 6)
            ACT(_)
            pfree(psg_n, act)
            pfree(psu_n, act)
            a_gu = act.n

            def _(e, m=m, a_gu=a_gu, sg_sb=sg_sb, ub_sb=ub_sb):
                e.wait_ge(sem_act, a_gu)
                mm = gu_state.get(("hidst", m - 2))
                if mm:
                    e.wait_ge(sem_sgub, mm)
                par = m % 2
                po.tick(e.tensor_mul(hidm[:][:, par * TC:(par + 1) * TC],
                                     sg_sb, ub_sb))
            POOL(_)
            gu_state[("hidtt", m)] = po.n
            po_h = po.n

            def _(e, m=m, po_h=po_h):
                e.wait_ge(sem_po, po_h)
                par = m % 2
                sguc.tick(e.dma_start(out=hid_hbm[m],
                                      in_=hidm[:][:, par * TC:(par + 1) * TC]))
            GP(_)
            gu_state[("sgst", m)] = sguc.n
            gu_state[("hidst", m)] = sguc.n
        gu_state[("gudone", wc)] = pe.n

    def emit_down_mgroup(dn, fh, m2):
        wc = dn["wc"]
        dwnb, sq1 = dn["dwnb"], dn["sq1"]
        sy_hidt = dn["sy_hidt"]
        half_marks, st_marks, a_dn_marks = (dn["half_marks"], dn["st_marks"],
                                            dn["a_dn_marks"])
        psd, psd_n = DNPS[m2 % 4]
        HC = MC // 2
        HCW = HC * P
        for h in range(2):
            wch2 = wdc if h == 0 else wdc2
            prev = half_marks.get((fh, m2 - 1, h))
            pe_now = pe.n

            def _(e, m2=m2, fh=fh, h=h, wch2=wch2, prev=prev, pe_now=pe_now):
                if prev is not None:
                    e.wait_ge(sem_pe, prev)
                elif fh > 0 or gu_state.get("wd_used"):
                    e.wait_ge(sem_pe, pe_now)
                wch2.tick(e.dma_start(
                    out=wdm2[:, h * HCW:(h + 1) * HCW],
                    in_=wd_scr[m2][:, h * HCW:(h + 1) * HCW]))
            GP(_)
        gu_state["wd_used"] = True
        for h in range(2):
            wch2 = wdc if h == 0 else wdc2
            gp_wd = wch2.n

            def _(e, m2=m2, fh=fh, h=h, sy_hidt=sy_hidt, gp_wd=gp_wd,
                  psd=psd, psd_n=psd_n, wch2=wch2):
                if h == 0:
                    e.wait_ge(sem_hidt, sy_hidt)
                    pwait(e, psd_n)
                e.wait_ge(wch2.sem, gp_wd)
                for k2 in range(HC):
                    kk = h * HC + k2
                    ins = e.matmul(psd[:, 0:FH],
                                   wdm2[:, kk * P:(kk + 1) * P],
                                   hidt[:][:, kk * FH:(kk + 1) * FH],
                                   start=(kk == 0), stop=(kk == MC - 1))
                    if k2 == HC - 1:
                        pe.tick(ins)
            TEN(_)
            half_marks[(fh, m2, h)] = pe.n
        pe_dn = pe.n
        if m2 >= 2:
            emit_dn_stats(dn, fh, m2 - 2)
        st_guard = st_marks.get((fh, m2 - 2), 0)
        po_guard = dn["po_guard"]
        sg_done = dn["sg_done"]

        def _(e, m2=m2, fh=fh, pe_dn=pe_dn, psd=psd,
              po_guard=po_guard, st_guard=st_guard, sg_done=sg_done):
            e.wait_ge(sem_pe, pe_dn)
            if m2 == 0 and fh == 0:
                if po_guard:
                    e.wait_ge(sem_pe, po_guard)
                e.wait_ge(sem_sgub, sg_done)
            if st_guard:
                e.wait_ge(sem_pe, st_guard)
            dsl = dwnb[:, m2 * TC + fh * FH:m2 * TC + fh * FH + FH]
            e.activation(out=dsl, in_=psd[:, 0:FH], func=AF.Copy,
                         scale=S127)
            act.tick(e.activation(
                out=sq1[:, (m2 % 2) * FH:(m2 % 2 + 1) * FH],
                in_=psd[:, 0:FH], func=AF.Square, scale=S127))
        ACT(_)
        pfree(psd_n, act)
        a_dn_marks[(fh, m2)] = act.n

    def emit_dn_stats(dn, fh, k):
        dwnb, sq1 = dn["dwnb"], dn["sq1"]

        def _(e, k=k, fh=fh, a_dn=dn["a_dn_marks"][(fh, k)]):
            e.wait_ge(sem_act, a_dn)
            if k == 0 and fh == 0:
                pwait(e, "ps_s1")
                pwait(e, "ps_s2")
            e.matmul(ps_s1[0:1, fh * FH:(fh + 1) * FH], onesb[:],
                     dwnb[:, k * TC + fh * FH:k * TC + fh * FH + FH],
                     start=(k == 0), stop=(k == DC - 1))
            pe.tick(e.matmul(
                ps_s2[0:1, fh * FH:(fh + 1) * FH], onesb[:],
                sq1[:, (k % 2) * FH:(k % 2 + 1) * FH],
                start=(k == 0), stop=(k == DC - 1)))
        TEN(_)
        dn["st_marks"][(fh, k)] = pe.n

    def emit_dn_hidt(dn, fh):
        sg_done = dn["sg_done"]
        pe_hidt_guard = pe.n if fh == 1 else max(
            gu_state.get("down_done", 0),
            gu_state.get(("midstats_pe", dn["wc"] + 2), 0))

        def _(e, fh=fh, sg_done=sg_done, pe_hidt_guard=pe_hidt_guard):
            e.wait_ge(sem_sgub, sg_done)
            if pe_hidt_guard:
                e.wait_ge(sem_pe, pe_hidt_guard)
            hidtc.tick(e.dma_start(
                out=hidt[:].rearrange("p (a t) -> p a t", t=FH),
                in_=hid_hbm[:, :, fh * FH:(fh + 1) * FH].transpose([1, 0, 2])))
        GP(_)
        dn["sy_hidt"] = hidtc.n

    def emit_down_part1(wc, po_guard):
        dn = {"wc": wc, "po_guard": po_guard,
              "dwnb": slotv(wc), "sq1": hidm[:],
              "sg_done": gu_state[("sgst", MC - 1)],
              "half_marks": {}, "st_marks": {}, "a_dn_marks": {}}
        gu_state[("dn", wc)] = dn
        emit_dn_hidt(dn, 0)
        for m2 in range(4):
            emit_down_mgroup(dn, 0, m2)

    def emit_down_part1b(wc):
        dn = gu_state[("dn", wc)]
        for m2 in range(4, DC):
            emit_down_mgroup(dn, 0, m2)
        emit_dn_stats(dn, 0, DC - 2)
        emit_dn_stats(dn, 0, DC - 1)
        emit_dn_hidt(dn, 1)

    def emit_down_part2a(wc):
        emit_down_part1b(wc)
        dn = gu_state[("dn", wc)]
        dwnb = dn["dwnb"]
        for m2 in range(DC):
            emit_down_mgroup(dn, 1, m2)
        emit_dn_stats(dn, 1, DC - 2)
        emit_dn_stats(dn, 1, DC - 1)
        gu_state["down_done"] = pe.n
        gu_state[("ddone", wc)] = pe.n
        nwc = wc + 1
        if nwc < NCH:
            for m in (0, 1):
                par = m % 2
                wch = wvc[par]
                wg_sb = wgu[:][:, par * 2 * DM:par * 2 * DM + DM]
                wu_sb = wgu[:][:, par * 2 * DM + DM:(par + 1) * 2 * DM]

                def _(e, m=m, wch=wch, wg_sb=wg_sb, wu_sb=wu_sb,
                      dd=gu_state["down_done"]):
                    e.wait_ge(sem_wsc, wsc_all)
                    e.wait_ge(sem_pe, dd)
                    wch.tick(e.dma_start(out=wg_sb, in_=wg_scr[m]))
                    wch.tick(e.dma_start(out=wu_sb, in_=wu_scr[m]))
                SY(_)
                gu_state[("wvmark", nwc, m)] = wch.n
        pe_stats_dn = pe.n
        def _(e, pe_stats_dn=pe_stats_dn):
            e.wait_ge(sem_pe, pe_stats_dn)
            e.copy(out=rows_s, in_=ps_s1[0:1, :])
            e.copy(out=rows_q, in_=ps_s2[0:1, :])
            e.mul(out=rows_s, in_=rows_s, mul=1.0 / DM)
            e.mul(out=rows_q, in_=rows_q, mul=1.0 / DM)
            act.tick(e.activation(out=rows_a1, in_=rows_s, func=AF.Square))
        ACT(_)
        a_m4 = act.n

        def _(e, a_m4=a_m4):
            e.wait_ge(sem_act, a_m4)
            dve.tick(e.tensor_sub(rows_a2, rows_q, rows_a1))
        DVE(_)
        d_v4 = dve.n

        def _(e, d_v4=d_v4):
            e.wait_ge(sem_dve, d_v4)
            e.activation(out=rows_a1, in_=rows_a2, func=AF.Ln,
                         bias=cst_eps[0:1, :])
            act.tick(e.activation(out=rows_a1, in_=rows_a1, func=AF.Exp,
                                  scale=-0.5))
        ACT(_)
        a_r4 = act.n

        def _(e, a_r4=a_r4):
            e.wait_ge(sem_act, a_r4)
            dve.tick(e.tensor_mul(rows_a2, rows_s, rows_a1))
        DVE(_)
        d_mr4 = dve.n
        dve_now4 = dve.n

        def _(e, d_mr4=d_mr4):
            e.wait_ge(sem_dve, d_mr4)
            e.wait_ge(sem_sgub, sguc.n)
            e.activation(out=rowb_1, in_=rows_a1, func=AF.Copy)
            act.tick(e.activation(out=rowb2, in_=rows_a2, func=AF.Copy))
        ACT(_)
        a_rb4 = act.n

        def _(e, a_rb4=a_rb4):
            e.wait_ge(sem_act, a_rb4)
            pwait(e, "ps_p")
            pwait(e, "ps_s2")
            e.matmul(ps_p[:], onesrow[:], rowb_1, start=True, stop=True)
            pe.tick(e.matmul(ps_s2[:], onesrow[:], rowb2, start=True,
                             stop=True))
        TEN(_)
        pe_b14 = pe.n

        def _(e, pe_b14=pe_b14, dve_now4=dve_now4):
            e.wait_ge(sem_pe, pe_b14)
            e.wait_ge(sem_dve, dve_now4)
            e.copy(out=rstd_bc, in_=ps_p[:])
            act.tick(e.copy(out=mr_bc, in_=ps_s2[:]))
        ACT(_)
        pfree("ps_p", act)
        pfree("ps_s2", act)
        bc4 = act.n

        def _(e, bc4=bc4, wc=wc):
            e.wait_ge(sem_act, bc4)
            for dc in range(DC):
                e.tensor_mul(t1s, dwnb[:, dc * TC:(dc + 1) * TC], rstd_bc)
                e.tensor_sub(t1s, t1s, mr_bc)
                xsl = xT[:][:, dc * T + wc * TC:dc * T + (wc + 1) * TC]
                ins = e.tensor_tensor(out=xsl, in0=xsl, in1=t1s, op=ALU.add)
                if dc == DC - 1:
                    dve.tick(ins)
        DVE(_)
        dve_x3 = dve.n
        gu_state["x3_dve"] = dve_x3


    def emit_down_part2out(wc):
        dn = gu_state[("dn", wc)]
        dve_x3 = gu_state["x3_dve"]
        otok = sgub[:].bitcast(F32)[:, 0:DM]
        for tt in range(TC // P):
            act_base = act.n
            sy_now = outc.n

            def _(e, tt=tt, wc=wc, dve_x3=dve_x3, act_base=act_base):
                e.wait_ge(sem_dve, dve_x3)
                pwait(e, "ps_t1")
                pwait(e, "ps_t2")
                if tt > 0:
                    e.wait_ge(sem_act, act_base)
                for dc in range(DC):
                    pst = ps_t1 if dc % 2 == 0 else ps_t2
                    if dc >= 2:
                        e.wait_ge(sem_act, act_base + dc - 1)
                    pe.tick(e.transpose(
                        pst[:],
                        xT[:][:, dc * T + wc * TC + tt * P:
                              dc * T + wc * TC + (tt + 1) * P],
                        idf[:]))
            TEN(_)
            pe_base = pe.n - DC

            def _(e, tt=tt, pe_base=pe_base, sy_now=sy_now, otok=otok,
                  tt_guard=gu_state[("hidtt", MC - 1)]):
                for dc in range(DC):
                    pst = ps_t1 if dc % 2 == 0 else ps_t2
                    e.wait_ge(sem_pe, pe_base + dc + 1)
                    if dc == 0:
                        e.wait_ge(sem_out, sy_now)
                        if tt == 0:
                            e.wait_ge(sem_po, tt_guard)
                    act.tick(e.copy(out=otok[:, dc * P:(dc + 1) * P],
                                    in_=pst[:]))
            ACT(_)
            a_ot = act.n

            def _(e, tt=tt, wc=wc, a_ot=a_ot, otok=otok):
                e.wait_ge(sem_act, a_ot)
                outc.tick(e.dma_start(
                    out=y_out[wc * TC + tt * P:wc * TC + (tt + 1) * P, :],
                    in_=otok))
            SY(_)
        pfree("ps_t1", act)
        pfree("ps_t2", act)
        gu_state[("otok_done", wc)] = outc.n

    head_state = {}
    rbs = bscb[:][0:1, 0:2 * TC]

    def emit_head_stats_mid(cn):
        hA = [hidt[:][:, 0:TC], hidt[:][:, TC:2 * TC]]
        hB = [hidt[:][:, 2 * TC:3 * TC], hidt[:][:, 3 * TC:4 * TC]]
        c0 = cn * TC
        dd_guard = gu_state.get(("ddone", cn - 3), 0)
        pe_slot = {}
        for i in range(DC):
            src = xT[:][:, i * T + c0:i * T + c0 + TC]

            def _(e, i=i, src=src, dd_guard=dd_guard):
                if i == 0:
                    e.wait_ge(sem_act, act_u_chunk[cn])
                    e.wait_ge(sem_wsc, wsc_all)
                    if dd_guard:
                        e.wait_ge(sem_pe, dd_guard)
                w = pe_slot.get(i - 2)
                if w:
                    e.wait_ge(sem_pe, w)
                e.activation(out=hA[i % 2], in_=src, func=AF.Copy)
                act.tick(e.activation(out=hB[i % 2], in_=src, func=AF.Square))
            ACT(_)
            a_i = act.n

            def _(e, i=i, a_i=a_i):
                e.wait_ge(sem_act, a_i)
                if i == 0:
                    pwait(e, "ps_s1")
                    pwait(e, "ps_s2")
                e.matmul(ps_s1[0:1, :], onesb[:], hA[i % 2],
                         start=(i == 0), stop=(i == DC - 1))
                pe.tick(e.matmul(ps_s2[0:1, :], onesb[:], hB[i % 2],
                                 start=(i == 0), stop=(i == DC - 1)))
            TEN(_)
            pe_slot[i] = pe.n
        pe_stats = pe.n
        gu_state[("midstats_pe", cn)] = pe.n

        def _(e, pe_stats=pe_stats):
            e.wait_ge(sem_pe, pe_stats)
            e.copy(out=rows_s, in_=ps_s1[0:1, :])
            e.copy(out=rows_q, in_=ps_s2[0:1, :])
            e.mul(out=rows_s, in_=rows_s, mul=1.0 / DM)
            e.mul(out=rows_q, in_=rows_q, mul=1.0 / DM)
            act.tick(e.activation(out=rows_a1, in_=rows_s, func=AF.Square))
        ACT(_)
        pfree("ps_s1", act)
        pfree("ps_s2", act)
        a_m = act.n

        def _(e, a_m=a_m):
            e.wait_ge(sem_act, a_m)
            dve.tick(e.tensor_sub(rows_a2, rows_q, rows_a1))
        DVE(_)
        d_v = dve.n

        def _(e, d_v=d_v):
            e.wait_ge(sem_dve, d_v)
            e.activation(out=rows_a1, in_=rows_a2, func=AF.Ln,
                         bias=cst_eps[0:1, :])
            act.tick(e.activation(out=rows_a1, in_=rows_a1, func=AF.Exp,
                                  scale=-0.5))
        ACT(_)
        a_r = act.n

        def _(e, a_r=a_r):
            e.wait_ge(sem_act, a_r)
            dve.tick(e.tensor_mul(rows_a2, rows_s, rows_a1))
        DVE(_)
        d_mr = dve.n
        bb_g = head_state.get((cn - 1, "gp_bbc"), 0)
        cb_g = head_state.get((cn - 1, "sy_cbc"), 0)

        def _(e, d_mr=d_mr, bb_g=bb_g, cb_g=cb_g):
            e.wait_ge(sem_dve, d_mr)
            if bb_g:
                e.wait_ge(sem_bbc, bb_g)
            if cb_g:
                e.wait_ge(sem_cbc, cb_g)
            e.activation(out=rbs[:, 0:TC], in_=rows_a1, func=AF.Copy)
            act.tick(e.activation(out=rbs[:, TC:2 * TC], in_=rows_a2,
                                  func=AF.Copy))
        ACT(_)
        gu_state[("midrows_act", cn)] = act.n

    def emit_head_bcast(cn, scan_mark):
        mid_a = gu_state[("midrows_act", cn)]

        def _(e, mid_a=mid_a, scan_mark=scan_mark):
            e.wait_ge(sem_act, mid_a)
            e.wait_ge(sem_dve, scan_mark)
            pwait(e, "ps_p")
            pwait(e, "ps_d")
            e.matmul(ps_p[:], onesrow[:], rbs[:, 0:TC], start=True, stop=True)
            pe.tick(e.matmul(ps_d[:], onesrow[:], rbs[:, TC:2 * TC],
                             start=True, stop=True))
        TEN(_)
        pe_b = pe.n

        def _(e, pe_b=pe_b):
            e.wait_ge(sem_pe, pe_b)
            e.copy(out=rstd2_bc, in_=ps_p[:])
            act.tick(e.copy(out=mr2_bc, in_=ps_d[:]))
        ACT(_)
        pfree("ps_p", act)
        pfree("ps_d", act)
        bc1 = act.n
        c0 = cn * TC

        def _(e, bc1=bc1, c0=c0):
            e.wait_ge(sem_act, bc1)
            for dc in range(DC):
                e.tensor_mul(t1s, xT[:][:, dc * T + c0:dc * T + c0 + TC],
                             rstd2_bc)
                ins = e.tensor_sub(unb[:][:, dc * TC:(dc + 1) * TC],
                                   t1s, mr2_bc)
                if dc == DC - 1:
                    dve.tick(ins)
        DVE(_)
        head_state[(cn, "dve_unorm")] = dve.n


    def emit_head_stats(c, dve_chunk_end):
        c0 = c * TC
        bc1 = emit_stats_rows_bcast(
            lambda i: xT[:][:, i * T + c0:i * T + c0 + TC],
            [(sem_act, act_u_chunk[c]), (sem_dve, dve_chunk_end)])

        def _(e, bc1=bc1, c0=c0):
            e.wait_ge(sem_act, bc1)
            for dc in range(DC):
                e.tensor_mul(t1s, xT[:][:, dc * T + c0:dc * T + c0 + TC],
                             rstd_bc)
                ins = e.tensor_sub(unb[:][:, dc * TC:(dc + 1) * TC],
                                   t1s, mr_bc)
                if dc == DC - 1:
                    dve.tick(ins)
        DVE(_)
        head_state[(c, "dve_unorm")] = dve.n

    def emit_head_proj(c, dve_chunk_end):
        dve_unorm = head_state[(c, "dve_unorm")]
        act_now = act.n

        def _(e, dve_unorm=dve_unorm, act_now=act_now):
            e.wait_ge(sem_dve, dve_unorm)
            e.wait_ge(sem_gp, gp_small)
            e.wait_ge(sem_act, act_now)
            pwait(e, "ps_p")
            for dc in range(DC):
                ins = e.matmul(ps_p[0:R, :], wxdtTb[:, dc * R:(dc + 1) * R],
                               unb[:][:, dc * TC:(dc + 1) * TC],
                               start=(dc == 0), stop=(dc == DC - 1))
                if dc == DC - 1:
                    pe.tick(ins)
        TEN(_)
        pe_xdt = pe.n

        def _(e, pe_xdt=pe_xdt):
            e.wait_ge(sem_pe, pe_xdt)
            act.tick(e.activation(out=xdtb[:], in_=ps_p[0:R, :],
                                  func=AF.Identity, bias=bxdtc[:]))
        ACT(_)
        pfree("ps_p", act)
        head_state[(c, "act_xdt")] = act.n

        def _(e, dve_unorm=dve_unorm):
            e.wait_ge(sem_dve, dve_unorm)
            pwait(e, "ps_p")
            pwait(e, "ps_d")
            for dc in range(DC):
                e.matmul(ps_p[0:NST, :], wbTb[:, dc * NST:(dc + 1) * NST],
                         unb[:][:, dc * TC:(dc + 1) * TC],
                         start=(dc == 0), stop=(dc == DC - 1))
            for dc in range(DC):
                ins = e.matmul(ps_d[0:NST, :], wcTb[:, dc * NST:(dc + 1) * NST],
                               unb[:][:, dc * TC:(dc + 1) * TC],
                               start=(dc == 0), stop=(dc == DC - 1))
                if dc == DC - 1:
                    pe.tick(ins)
        TEN(_)
        pe_bcproj = pe.n

        def _(e, pe_bcproj=pe_bcproj):
            e.wait_ge(sem_pe, pe_bcproj)
            e.activation(out=bscb[:, 0:TC], in_=ps_p[0:NST, :],
                         func=AF.Identity, bias=bbcol[:])
            act.tick(e.activation(out=bscb[:, TC:2 * TC], in_=ps_d[0:NST, :],
                                  func=AF.Identity, bias=bccol[:]))
        ACT(_)
        pfree("ps_p", act)
        pfree("ps_d", act)
        act_bc = act.n

        dve_prev_reader = dve_chunk_end if c > 0 else 0

        def _(e, act_bc=act_bc, dve_prev_reader=dve_prev_reader):
            e.wait_ge(sem_act, act_bc)
            if dve_prev_reader:
                e.wait_ge(sem_dve, dve_prev_reader)
            bbcc.tick(e.dma_start(
                out=bbc[:][0:1, :].rearrange("o (n t) -> o n t", t=TC),
                in_=bscb[0:NST, 0:TC]))
            k = 1
            while k < P:
                kk = min(k, P - k)
                e.wait_ge(sem_bbc, bbcc.n)
                bbcc.tick(e.dma_start(out=bbc[:][k:k + kk, :],
                                      in_=bbc[:][0:kk, :]))
                k += kk
        SY(_)
        head_state[(c, "gp_bbc")] = bbcc.n

        def _(e, act_bc=act_bc, dve_prev_reader=dve_prev_reader):
            e.wait_ge(sem_act, act_bc)
            if dve_prev_reader:
                e.wait_ge(sem_dve, dve_prev_reader)
            cbcc.tick(e.dma_start(
                out=cbc[:][0:1, :].rearrange("o (n t) -> o n t", t=TC),
                in_=bscb[0:NST, TC:2 * TC]))
            k = 1
            while k < P:
                kk = min(k, P - k)
                e.wait_ge(sem_cbc, cbcc.n)
                cbcc.tick(e.dma_start(out=cbc[:][k:k + kk, :],
                                      in_=cbc[:][0:kk, :]))
                k += kk
        GP(_)
        head_state[(c, "sy_cbc")] = cbcc.n

    emit_head_stats(0, 0)
    emit_head_proj(0, 0)

    stage = [hidt[:][:, 0:H], hidt[:][:, H:2 * H]]
    cast_jobs = ([("wg", m) for m in range(MC)] + [("wu", m) for m in range(MC)]
                 + [("wd", m) for m in range(DC)])
    wc_marks = {}
    for j, (kind, m) in enumerate(cast_jobs):
        par = j % 2
        wch = wcc[par]
        src = {"wg": wg_in, "wu": wu_in, "wd": wd_in}[kind][m]
        width = H if kind == "wd" else DM
        sbuf = stage[par][:, 0:width]

        def _(e, j=j, wch=wch, src=src, sbuf=sbuf):
            if j >= 2:
                e.wait_ge(sem_wsc, (j - 1) * 16)
            wch.tick(e.dma_start(out=sbuf, in_=src))
        GP(_)
        wc_marks[j] = wch.n

    def emit_cast_stores(j0, j1):
        for j in range(j0, min(j1, len(cast_jobs))):
            kind, m = cast_jobs[j]
            par = j % 2
            wch = wcc[par]
            dst = {"wg": wg_scr, "wu": wu_scr, "wd": wd_scr}[kind][m]
            width = H if kind == "wd" else DM
            sbuf = stage[par][:, 0:width]

            def _(e, wch=wch, wc_mark=wc_marks[j], dst=dst, sbuf=sbuf):
                e.wait_ge(wch.sem, wc_mark)
                wscc.tick(e.dma_start(out=dst, in_=sbuf))
            SY(_)
    WSC_ALL_N = len(cast_jobs) * 16
    wsc_all = WSC_ALL_N

    dve_chunk_end = 0
    for c in range(NCH):
        c0 = c * TC
        slot = slotv(c)
        ybfv = slot
        slot_guard_pe = max(gu_state.get(("gudone", c - 2), 0),
                            gu_state.get(("ddone", c - 2), 0))
        slot_guard_out = 0
        dve_unorm = head_state[(c, "dve_unorm")]
        act_xdt = head_state[(c, "act_xdt")]
        gp_bbc = head_state[(c, "gp_bbc")]
        sy_cbc = head_state[(c, "sy_cbc")]
        dve_prev_reader = dve_chunk_end if c > 0 else 0

        t_scans_prev = 0
        t_ym0_prev = 0
        dve_scan_prev = 0
        for dc in range(DC):
            if c >= 1 and dc < 6:
                emit_gu_batch(c - 1, 4 * dc, 4 * dc + 4)
            dtsl = dtb[:][:, (dc % 2) * TC:(dc % 2 + 1) * TC]
            A0R = abar_h0 if dc % 2 == 0 else hbuf
            A1R = abar_h1
            OUT0R = hbuf if dc % 2 == 0 else abar_h0

            def _(e, dc=dc, act_xdt=act_xdt):
                e.wait_ge(sem_act, act_xdt)
                pwait(e, "ps_d")
                pe.tick(e.matmul(ps_d[:], wdtTb[:][:, dc * P:(dc + 1) * P],
                                 xdtb[:], start=True, stop=True))
            TEN(_)
            pe_dt = pe.n

            def _(e, dc=dc, c=c, pe_dt=pe_dt, dtsl=dtsl, A0R=A0R, A1R=A1R,
                  t_scans_prev=t_scans_prev, t_ym0_prev=t_ym0_prev,
                  dve_prev_reader=dve_prev_reader):
                e.wait_ge(sem_pe, pe_dt)
                if dc >= 1:
                    e.wait_ge(sem_dve, t_scans_prev)
                elif dve_prev_reader:
                    e.wait_ge(sem_dve, dve_prev_reader)
                e.activation(out=etx, in_=ps_d[:], func=AF.Exp,
                             bias=bdtp[:][:, dc:dc + 1])
                e.activation(out=dtsl, in_=etx, func=AF.Ln, bias=cst_one)
                for n in range(HNST, NST):
                    e.activation(
                        out=A1R[:, (n - HNST) * TC:(n - HNST + 1) * TC],
                        in_=dtsl, func=AF.Exp,
                        scale=aneg[:][:, dc * NST + n:dc * NST + n + 1])
                if dc >= 1:
                    e.wait_ge(sem_dve, t_ym0_prev)
                for n in range(HNST):
                    ins = e.activation(
                        out=A0R[:, n * TC:(n + 1) * TC],
                        in_=dtsl, func=AF.Exp,
                        scale=aneg[:][:, dc * NST + n:dc * NST + n + 1])
                    if n == HNST - 1:
                        act.tick(ins)
            ACT(_)
            pfree("ps_d", act)
            act_abar = act.n

            ysl = ybfv[:, dc * TC:(dc + 1) * TC]

            def _(e, dc=dc, c=c, act_abar=act_abar, gp_bbc=gp_bbc,
                  sy_cbc=sy_cbc, dtsl=dtsl, ysl=ysl, A0R=A0R, A1R=A1R,
                  OUT0R=OUT0R, slot_guard_pe=slot_guard_pe,
                  slot_guard_out=slot_guard_out):
                e.wait_ge(sem_act, act_abar)
                e.wait_ge(sem_bbc, gp_bbc)
                av0 = A0R.rearrange("p (n t) -> p n t", t=TC)
                av1 = A1R.rearrange("p (n t) -> p n t", t=TC)
                if c > 0:
                    hl = hlast[:][:, dc * NST:(dc + 1) * NST]
                    e.tensor_mul(fix16[:][:, 0:HNST], av0[:, :, 0],
                                 hl[:, 0:HNST])
                    e.tensor_mul(fix16[:][:, HNST:NST], av1[:, :, 0],
                                 hl[:, HNST:NST])
                dtu = t1s
                e.tensor_mul(dtu, dtsl, unb[:][:, dc * TC:(dc + 1) * TC])
                dtu_b = dtu.unsqueeze(1).broadcast_to([P, NST, TC])
                bbc_v = bbc[:].rearrange("p (n t) -> p n t", t=TC)
                xbuf_v = xbuf.rearrange("p (n t) -> p n t", t=TC)
                e.tensor_tensor(out=xbuf_v, in0=dtu_b, in1=bbc_v, op=ALU.mult)
                if c > 0:
                    e.tensor_tensor(out=xbuf_v[:, :, 0], in0=xbuf_v[:, :, 0],
                                    in1=fix16[:], op=ALU.add)
                e.memset(av0[:, :, 0:1], 0.0)
                dve.tick(e.memset(av1[:, :, 0:1], 0.0))
                e.wait_ge(sem_dve, dve.n)
                e.tensor_tensor_scan(
                    out=OUT0R, data0=A0R, data1=xbuf[:, 0:HNST * TC],
                    initial=0.0, op0=ALU.mult, op1=ALU.add)
                dve.tick(e.tensor_tensor_scan(
                    out=A0R, data0=A1R, data1=xbuf[:, HNST * TC:],
                    initial=0.0, op0=ALU.mult, op1=ALU.add))
                e.wait_ge(sem_cbc, sy_cbc)
                if dc == 0:
                    if slot_guard_pe:
                        e.wait_ge(sem_pe, slot_guard_pe)
                    if slot_guard_out:
                        e.wait_ge(sem_out, slot_guard_out)
                hv0 = OUT0R.rearrange("p (n t) -> p n t", t=TC)
                cb0 = cbc[:][:, 0:HNST * TC].rearrange("p (n t) -> p n t", t=TC)
                xb0 = xbuf[:, 0:HNST * TC]
                e.tensor_tensor(
                    out=xb0.rearrange("p (n t) -> p n t", t=TC),
                    in0=hv0, in1=cb0, op=ALU.mult)
                dve.tick(e.tensor_copy(
                    out=hlast[:][:, dc * NST:dc * NST + HNST],
                    in_=hv0[:, :, TC - 1]))
                hv1 = A0R.rearrange("p (n t) -> p n t", t=TC)
                cb1 = cbc[:][:, HNST * TC:].rearrange("p (n t) -> p n t", t=TC)
                xb1 = xbuf[:, HNST * TC:]
                e.tensor_tensor(out=xb1.rearrange("p (n t) -> p n t", t=TC),
                                in0=hv1, in1=cb1, op=ALU.mult)
                vin, width = xbuf, NST * TC
                while width > 2 * TC:
                    vv = vin.rearrange("p (j two t) -> p j two t", two=2, t=TC)
                    outw = width // 2
                    vout = xbuf[:, 0:outw]
                    vo = vout.rearrange("p (j t) -> p j t", t=TC)
                    e.tensor_tensor(out=vo, in0=vv[:, :, 0, :],
                                    in1=vv[:, :, 1, :], op=ALU.add)
                    vin, width = vout, outw
                vv = vin.rearrange("p (j two t) -> p j two t", two=2, t=TC)
                e.tensor_tensor(out=ysl, in0=vv[:, 0, 0, :],
                                in1=vv[:, 0, 1, :], op=ALU.add)
                e.tensor_copy(
                    out=hlast[:][:, dc * NST + HNST:(dc + 1) * NST],
                    in_=hv1[:, :, TC - 1])
                dve.tick(e.scalar_tensor_tensor(
                    out=ysl, in0=unb[:][:, dc * TC:(dc + 1) * TC],
                    scalar=dskp[:][:, dc:dc + 1], in1=ysl,
                    op0=ALU.mult, op1=ALU.add))
            DVE(_)
            dve_scan_prev = dve.n
            t_ym0_prev = dve.n - 1
            t_scans_prev = dve.n - 2
            if c == 0 and dc >= 2:
                emit_u_tile(2 * dc)
                emit_u_tile(2 * dc + 1)
                emit_cast_stores((dc - 2) * 10, (dc - 1) * 10)
            if c + 1 < NCH and dc == (7 if c == 0 else 2):
                emit_head_stats_mid(c + 1)
            if c >= 1 and dc == 6:
                emit_down_part1(c - 1, gu_state[("gudone", c - 1)])
        dve_mamba = dve.n

        if c + 1 < NCH:
            emit_head_bcast(c + 1, t_scans_prev)
            emit_head_proj(c + 1, dve_mamba)
        if c >= 1:
            emit_down_part2a(c - 1)

        bc2 = emit_stats_rows_bcast(
            lambda i: ybfv[:, i * TC:(i + 1) * TC],
            [(sem_dve, dve_mamba)], src_bf16=True)

        def _(e, bc2=bc2, c0=c0, ybfv=ybfv, slot=slot):
            e.wait_ge(sem_act, bc2)
            for dc in range(DC):
                e.tensor_mul(t1s, ybfv[:, dc * TC:(dc + 1) * TC], rstd_bc)
                e.tensor_sub(t1s, t1s, mr_bc)
                xsl = xT[:][:, dc * T + c0:dc * T + c0 + TC]
                e.tensor_tensor(out=xsl, in0=xsl, in1=t1s, op=ALU.add)
                ins = e.tensor_copy(out=slot[:, dc * TC:(dc + 1) * TC],
                                    in_=xsl)
                if dc == DC - 1:
                    dve.tick(ins)
        DVE(_)
        dve_x2 = dve.n

        bc3 = emit_stats_rows_bcast(
            lambda i: slot[:, i * TC:(i + 1) * TC],
            [(sem_dve, dve_x2)], src_bf16=True)

        def _(e, bc3=bc3, slot=slot):
            e.wait_ge(sem_act, bc3)
            for dc in range(DC):
                e.tensor_mul(t1s, slot[:, dc * TC:(dc + 1) * TC], rstd_bc)
                ins = e.tensor_sub(slot[:, dc * TC:(dc + 1) * TC], t1s, mr_bc)
                if dc == DC - 1:
                    dve.tick(ins)
        DVE(_)
        dve_chunk_end = dve.n
        gu_state["x2n_ready"] = dve.n

        if c >= 1:
            emit_down_part2out(c - 1)

    emit_gu_batch(NCH - 1, 0, MC)
    emit_down_part1(NCH - 1, gu_state[("gudone", NCH - 1)])
    emit_down_part2a(NCH - 1)
    emit_down_part2out(NCH - 1)

    def _(e):
        e.wait_ge(sem_out, outc.n)
    SY(_)

    es.close()
    return nc


_PROG = None


def _get_prog():
    global _PROG
    if _PROG is None:
        _PROG = build_program()
    return _PROG


def _prep_host(inputs):
    W_xdt = inputs["L0_attn_W_xdt"]
    W_dt = inputs["L0_attn_W_dt"]
    W_B = inputs["L0_attn_W_B"]
    W_C = inputs["L0_attn_W_C"]
    A_log = inputs["L0_attn_A_log"]
    b_dt = inputs["L0_attn_b_dt"]
    D = inputs["L0_attn_D"]
    wg = inputs["L0_ffn_w_gate"]
    wu = inputs["L0_ffn_w_up"]
    wd = inputs["L0_ffn_w_down"]

    def dchunk_cols(a2d, inner):
        return np.ascontiguousarray(
            a2d.reshape(DC, P, inner).transpose(1, 0, 2).reshape(P, DC * inner))

    wxdtT = dchunk_cols(np.ascontiguousarray(W_xdt.T), R).astype(np.float32)
    wdtT = np.ascontiguousarray(W_dt.T).astype(np.float32)
    wbT = dchunk_cols(np.ascontiguousarray(W_B.T), NST).astype(np.float32)
    wcT = dchunk_cols(np.ascontiguousarray(W_C.T), NST).astype(np.float32)
    alog_p = dchunk_cols(A_log, NST).astype(np.float32)
    bdt_p = np.ascontiguousarray(b_dt.reshape(DC, P).T).astype(np.float32)
    d_p = np.ascontiguousarray(D.reshape(DC, P).T).astype(np.float32)

    def prep_lhsT(aT, kc, mc):
        a4 = aT.reshape(kc, P, mc, P)
        return np.ascontiguousarray(
            a4.transpose(2, 1, 0, 3).reshape(mc, P, kc * P))

    wg_prep = prep_lhsT(np.ascontiguousarray(wg.T), DC, MC).astype(np.int32)
    wu_prep = prep_lhsT(np.ascontiguousarray(wu.T), DC, MC).astype(np.int32)
    wd_prep = prep_lhsT(np.ascontiguousarray(wd.T), MC, DC).astype(np.int32)

    return dict(
        wg_prep=wg_prep, wu_prep=wu_prep, wd_prep=wd_prep,
        wxdtT=wxdtT, wdtT=wdtT, wbT=wbT, wcT=wcT, alog_p=alog_p,
        bdt_p=bdt_p, d_p=d_p,
        bxdt=np.ascontiguousarray(
            inputs["L0_attn_b_xdt"].reshape(R, 1)).astype(np.float32),
        bb=np.ascontiguousarray(
            inputs["L0_attn_b_B"].reshape(NST, 1)).astype(np.float32),
        bc=np.ascontiguousarray(
            inputs["L0_attn_b_C"].reshape(NST, 1)).astype(np.float32),
        idf=np.eye(P, dtype=np.float32),
        ones_b=np.ones((P, 1), dtype=BF),
        ones_row=np.ones((1, P), dtype=BF),
    )


def kernel(**inputs):
    u = np.asarray(inputs["L0_attn_u"], dtype=np.float32)
    common = _prep_host({k: np.asarray(v) for k, v in inputs.items()})
    nc = _get_prog()
    in_maps = [dict(common, u=np.ascontiguousarray(u[b])) for b in range(8)]
    res = run_bass_kernel_spmd(nc, in_maps, core_ids=list(range(8)))
    return np.stack([np.asarray(r["y_out"], dtype=np.float32)
                     for r in res.results], axis=0)


# version 19
# speedup vs baseline: 1.0311x; 1.0311x over previous
import contextlib
import numpy as np
import ml_dtypes
import concourse.bass as bass
import concourse.mybir as mybir
from concourse.bass_utils import run_bass_kernel_spmd

F32 = mybir.dt.float32
BF16 = mybir.dt.bfloat16
I32 = mybir.dt.int32
AF = mybir.ActivationFunctionType
ALU = mybir.AluOpType
BF = ml_dtypes.bfloat16

P = 128
T = 2048
DM = 1024
DC = 8
NST = 16
HNST = NST // 2
R = 64
H = 3072
MC = 24
TC = 512
NCH = T // TC
FH = TC // 2
S127 = float(np.float32(1.0) / np.float32(127.0))
EPS = 1e-5


class Chain:
    def __init__(self, sem, step=1):
        self.sem, self.step, self.n = sem, step, 0

    def tick(self, ins):
        ins.then_inc(self.sem, self.step)
        self.n += self.step
        return self.n


def build_program():
    nc = bass.Bass("TRN2", target_bir_lowering=False)

    u_in = nc.dram_tensor("u", [T, DM], F32, kind="ExternalInput").ap()
    wg_in = nc.dram_tensor("wg_prep", [MC, P, DM], I32, kind="ExternalInput").ap()
    wu_in = nc.dram_tensor("wu_prep", [MC, P, DM], I32, kind="ExternalInput").ap()
    wd_in = nc.dram_tensor("wd_prep", [DC, P, H], I32, kind="ExternalInput").ap()
    wxdtT_in = nc.dram_tensor("wxdtT", [P, DC * R], F32, kind="ExternalInput").ap()
    wdtT_in = nc.dram_tensor("wdtT", [R, DM], F32, kind="ExternalInput").ap()
    wbT_in = nc.dram_tensor("wbT", [P, DC * NST], F32, kind="ExternalInput").ap()
    wcT_in = nc.dram_tensor("wcT", [P, DC * NST], F32, kind="ExternalInput").ap()
    alog_in = nc.dram_tensor("alog_p", [P, DC * NST], F32, kind="ExternalInput").ap()
    bdt_in = nc.dram_tensor("bdt_p", [P, DC], F32, kind="ExternalInput").ap()
    dsk_in = nc.dram_tensor("d_p", [P, DC], F32, kind="ExternalInput").ap()
    bxdt_in = nc.dram_tensor("bxdt", [R, 1], F32, kind="ExternalInput").ap()
    bb_in = nc.dram_tensor("bb", [NST, 1], F32, kind="ExternalInput").ap()
    bc_in = nc.dram_tensor("bc", [NST, 1], F32, kind="ExternalInput").ap()
    idf_in = nc.dram_tensor("idf", [P, P], F32, kind="ExternalInput").ap()
    onesb_in = nc.dram_tensor("ones_b", [P, 1], BF16, kind="ExternalInput").ap()
    onesrow_in = nc.dram_tensor("ones_row", [1, P], BF16, kind="ExternalInput").ap()

    wg_scr = nc.dram_tensor("wg_scr", [MC, P, DM], BF16, kind="Internal").ap()
    wu_scr = nc.dram_tensor("wu_scr", [MC, P, DM], BF16, kind="Internal").ap()
    wd_scr = nc.dram_tensor("wd_scr", [DC, P, H], BF16, kind="Internal").ap()
    hid_hbm = nc.dram_tensor("hid_scr", [MC, P, TC], BF16, kind="Internal").ap()
    y_out = nc.dram_tensor("y_out", [T, DM], F32, kind="ExternalOutput").ap()

    es = contextlib.ExitStack()

    def sb(name, shape, dtype):
        return es.enter_context(nc.sbuf_tensor(name, shape, dtype))

    def psum(name, shape, dtype=F32):
        return es.enter_context(nc.psum_tensor(name, shape, dtype))

    xT = sb("xT", [P, DC * T], F32)
    x2n = sb("x2n", [P, 2 * DC * TC], BF16)
    dtb = sb("dtb", [P, 2 * TC], BF16)
    unb = sb("unb", [P, DC * TC], BF16)
    mam = sb("mam", [P, (2 * NST + HNST) * TC], BF16)
    bbc = sb("bbc", [P, NST * TC], BF16)
    cbc = sb("cbc", [P, NST * TC], BF16)
    aux = sb("aux", [P, 2 * TC], BF16)
    rows = sb("rows", [1, 3 * TC], F32)
    rowb = sb("rowb", [1, TC], BF16)
    rbc = sb("rbc", [P, 2 * TC], BF16)
    smw = sb("smw", [P, 2 * DC * NST + DC * R], BF16)
    wdtTb = sb("wdtTb", [R, DM], BF16)
    xdtb = sb("xdtb", [R, TC], BF16)
    bscb = sb("bscb", [NST, 2 * TC], BF16)
    aneg = sb("aneg", [P, DC * NST], F32)
    bdtp = sb("bdtp", [P, DC], F32)
    dskp = sb("dskp", [P, DC], F32)
    bxdtc = sb("bxdtc", [R, 1], F32)
    bbcol = sb("bbcol", [NST, 1], F32)
    bccol = sb("bccol", [NST, 1], F32)
    idf = sb("idf_sb", [P, P], F32)
    onesb = sb("onesb_sb", [P, 1], BF16)
    onesrow = sb("onesrow_sb", [1, P], BF16)
    hlast = sb("hlast", [P, DC * NST], BF16)
    fix16 = sb("fix16", [P, NST], BF16)
    cst = sb("cst", [P, 2], F32)
    hidt = sb("hidt", [P, MC * FH], BF16)
    wgu = sb("wgu", [P, 4 * DM], BF16)
    sgub = sb("sgub", [P, 4 * TC], BF16)
    hidm = sb("hidm", [P, 2 * TC], BF16)
    wdm2 = wgu[:][:, 0:H]

    auxv = aux[:]
    t1s = auxv[:, 0:TC]
    etx = auxv[:, TC:2 * TC]

    smwv = smw[:]
    wbTb = smwv[:, 0:DC * NST]
    wcTb = smwv[:, DC * NST:2 * DC * NST]
    wxdtTb = smwv[:, 2 * DC * NST:2 * DC * NST + DC * R]

    mamx = mam[:]
    abar = mamx[:, 0:NST * TC]
    abar_h0 = mamx[:, 0:HNST * TC]
    abar_h1 = mamx[:, HNST * TC:NST * TC]
    xbuf = mamx[:, NST * TC:2 * NST * TC]
    hbuf = mamx[:, 2 * NST * TC:(2 * NST + HNST) * TC]
    rstd2_bc = mamx[:, HNST * TC:HNST * TC + TC]
    mr2_bc = mamx[:, HNST * TC + TC:HNST * TC + 2 * TC]
    uld = mam[:].bitcast(F32)[:, 0:2 * DM]

    def slotv(c):
        return x2n[:][:, (c % 2) * DC * TC:((c % 2) + 1) * DC * TC]


    ps_g = psum("ps_g", [P, TC])
    ps_u = psum("ps_u", [P, TC])
    ps_d = psum("ps_d", [P, TC])
    ps_s1 = psum("ps_s1", [P, TC])
    ps_s2 = psum("ps_s2", [P, TC])
    ps_t1f = psum("ps_t1", [P, TC])
    ps_t2f = psum("ps_t2", [P, TC])
    ps_p = psum("ps_p", [P, TC])
    ps_t1 = ps_t1f[:, 0:P]
    ps_t2 = ps_t2f[:, 0:P]

    def sem(name):
        return es.enter_context(nc.semaphore(name))

    sem_pe, sem_act, sem_dve, sem_po = sem("s_pe"), sem("s_act"), sem("s_dve"), sem("s_po")
    sem_sy, sem_gp = sem("s_sy"), sem("s_gp")
    sem_uld0, sem_uld1 = sem("s_uld0"), sem("s_uld1")
    sem_bbc, sem_cbc = sem("s_bbc"), sem("s_cbc")
    sem_wc0, sem_wc1 = sem("s_wc0"), sem("s_wc1")
    sem_wsc = sem("s_wsc")
    sem_wv0, sem_wv1 = sem("s_wv0"), sem("s_wv1")
    sem_wd = sem("s_wd")
    sem_wd2 = sem("s_wd2")
    sem_sgub = sem("s_sgub")
    sem_hidt = sem("s_hidt")
    sem_out = sem("s_out")
    blk = es.enter_context(nc.Block())

    pe, act, dve, po = Chain(sem_pe), Chain(sem_act), Chain(sem_dve), Chain(sem_po)
    sy, gp = Chain(sem_sy, 16), Chain(sem_gp, 16)
    uldc = [Chain(sem_uld0, 16), Chain(sem_uld1, 16)]
    bbcc, cbcc = Chain(sem_bbc, 16), Chain(sem_cbc, 16)
    wcc = [Chain(sem_wc0, 16), Chain(sem_wc1, 16)]
    wscc = Chain(sem_wsc, 16)
    wvc = [Chain(sem_wv0, 16), Chain(sem_wv1, 16)]
    wdc = Chain(sem_wd, 16)
    wdc2 = Chain(sem_wd2, 16)
    sguc = Chain(sem_sgub, 16)
    hidtc = Chain(sem_hidt, 16)
    outc = Chain(sem_out, 16)

    TEN, ACT, DVE, SY, GP = (blk.tensor, blk.scalar, blk.vector, blk.sync,
                             blk.gpsimd)
    POOL = blk.gpsimd

    def load_act_set(e, set_id):
        e.add_instruction(mybir.InstLoadActFuncSet(
            name=nc.get_next_instruction_name(), act_func_set_id=set_id,
            ins=[], outs=[]))

    psum_free = {}

    def pwait(e, name):
        ch_m = psum_free.get(name)
        if ch_m:
            e.wait_ge(ch_m[0].sem, ch_m[1])

    def pfree(name, chain):
        psum_free[name] = (chain, chain.n)

    rows_s = rows[:][:, 0:TC]
    rows_q = rows[:][:, TC:2 * TC]
    rows_a1 = rows[:][:, 2 * TC:3 * TC]
    rows_a2 = rows_q
    rowb_1 = rowb[:][:, 0:TC]
    rowb2 = hidm[:].bitcast(BF16)[0:1, TC:2 * TC]
    rstd_bc = rbc[:][:, 0:TC]
    mr_bc = rbc[:][:, TC:2 * TC]
    cst_eps = cst[:][:, 0:1]
    cst_one = cst[:][:, 1:2]

    def _(e):
        for ap_out, ap_in in [(idf[:], idf_in), (onesb[:], onesb_in),
                              (onesrow[:], onesrow_in), (bdtp[:], bdt_in),
                              (dskp[:], dsk_in), (bxdtc[:], bxdt_in),
                              (bbcol[:], bb_in), (bccol[:], bc_in)]:
            sy.tick(e.dma_start(out=ap_out, in_=ap_in))
    SY(_)
    sy_setup = sy.n

    def _(e):
        gp.tick(e.dma_start(out=wbTb, in_=wbT_in))
        gp.tick(e.dma_start(out=wcTb, in_=wcT_in))
        gp.tick(e.dma_start(out=wxdtTb, in_=wxdtT_in))
        gp.tick(e.dma_start(out=wdtTb[:], in_=wdtT_in))
        gp.tick(e.dma_start(out=aneg[:], in_=alog_in))
    GP(_)
    gp_small = gp.n

    def _(e):
        load_act_set(e, 6)
        e.wait_ge(sem_gp, gp_small)
        act.tick(e.activation(out=aneg[:], in_=aneg[:], func=AF.Exp))
    ACT(_)
    act_aexp = act.n

    def _(e):
        e.wait_ge(sem_act, act_aexp)
        e.memset(cst_eps, EPS)
        e.memset(cst_one, 1.0)
        dve.tick(e.tensor_scalar_mul(aneg[:], aneg[:], -1.0))
    DVE(_)

    u_v = u_in.rearrange("(tt p) d -> p tt d", p=P)
    pe_tile_marks = {}
    act_u_chunk = {}
    uld2 = wgu[:].bitcast(F32)

    def emit_u_tile(tt):
        par = tt % 2
        base = uld if tt < 4 else uld2
        buf = base[:, par * DM:(par + 1) * DM]
        uch = uldc[par]

        def _(e, tt=tt, buf=buf, uch=uch):
            e.wait_ge(sem_sy, sy_setup)
            m = pe_tile_marks.get(tt - 2, 0)
            if m:
                e.wait_ge(sem_pe, m)
            uch.tick(e.dma_start(out=buf, in_=u_v[:, tt, :]))
        SY(_)
        u_mark = uch.n
        act_base = act.n

        def _(e, tt=tt, buf=buf, u_mark=u_mark, uch=uch, act_base=act_base):
            e.wait_ge(uch.sem, u_mark)
            if tt > 0:
                e.wait_ge(sem_act, act_base)
            for dc in range(DC):
                pst = ps_t1 if dc % 2 == 0 else ps_t2
                if dc >= 2:
                    e.wait_ge(sem_act, act_base + dc - 1)
                pe.tick(e.transpose(pst[:], buf[:, dc * P:(dc + 1) * P], idf[:]))
        TEN(_)
        pe_tile_marks[tt] = pe.n
        pe_base = pe.n - DC

        def _(e, tt=tt, pe_base=pe_base):
            for dc in range(DC):
                pst = ps_t1 if dc % 2 == 0 else ps_t2
                e.wait_ge(sem_pe, pe_base + dc + 1)
                act.tick(e.copy(
                    out=xT[:][:, dc * T + tt * P:dc * T + (tt + 1) * P],
                    in_=pst[:]))
        ACT(_)
        if tt % 4 == 3:
            act_u_chunk[tt // 4] = act.n
        pfree("ps_t1", act)
        pfree("ps_t2", act)

    for tt in range(4):
        emit_u_tile(tt)

    def emit_stats_rows_bcast(src_getter, extra_marks, src_bf16=False):
        pe_slot = {}
        for i in range(DC):
            src = src_getter(i)
            qslot = dtb[:][:, (i % 2) * TC:(i % 2 + 1) * TC]

            def _(e, i=i, src=src, qslot=qslot):
                if i == 0:
                    for s_, v in extra_marks:
                        if v > 0:
                            e.wait_ge(s_, v)
                w = pe_slot.get(i - 2 if src_bf16 else i - 1)
                if w:
                    e.wait_ge(sem_pe, w)
                if not src_bf16:
                    e.activation(out=etx, in_=src, func=AF.Copy)
                act.tick(e.activation(out=qslot, in_=src, func=AF.Square))
            ACT(_)
            a_i = act.n

            def _(e, i=i, src=src, qslot=qslot, a_i=a_i):
                e.wait_ge(sem_act, a_i)
                if i == 0:
                    pwait(e, "ps_s1")
                    pwait(e, "ps_s2")
                e.matmul(ps_s1[0:1, :], onesb[:], src if src_bf16 else etx,
                         start=(i == 0), stop=(i == DC - 1))
                pe.tick(e.matmul(ps_s2[0:1, :], onesb[:], qslot,
                                 start=(i == 0), stop=(i == DC - 1)))
            TEN(_)
            pe_slot[i] = pe.n
        pe_stats = pe.n

        def _(e, pe_stats=pe_stats):
            e.wait_ge(sem_pe, pe_stats)
            e.copy(out=rows_s, in_=ps_s1[0:1, :])
            e.copy(out=rows_q, in_=ps_s2[0:1, :])
            e.mul(out=rows_s, in_=rows_s, mul=1.0 / DM)
            e.mul(out=rows_q, in_=rows_q, mul=1.0 / DM)
            act.tick(e.activation(out=rows_a1, in_=rows_s, func=AF.Square))
        ACT(_)
        a_m = act.n

        def _(e, a_m=a_m):
            e.wait_ge(sem_act, a_m)
            dve.tick(e.tensor_sub(rows_a2, rows_q, rows_a1))
        DVE(_)
        d_v = dve.n

        def _(e, d_v=d_v):
            e.wait_ge(sem_dve, d_v)
            e.activation(out=rows_a1, in_=rows_a2, func=AF.Ln,
                         bias=cst_eps[0:1, :])
            act.tick(e.activation(out=rows_a1, in_=rows_a1, func=AF.Exp,
                                  scale=-0.5))
        ACT(_)
        a_r = act.n

        def _(e, a_r=a_r):
            e.wait_ge(sem_act, a_r)
            dve.tick(e.tensor_mul(rows_a2, rows_s, rows_a1))
        DVE(_)
        d_mr = dve.n
        dve_now = dve.n

        def _(e, d_mr=d_mr):
            e.wait_ge(sem_dve, d_mr)
            e.wait_ge(sem_sgub, sguc.n)
            e.activation(out=rowb_1, in_=rows_a1, func=AF.Copy)
            act.tick(e.activation(out=rowb2, in_=rows_a2, func=AF.Copy))
        ACT(_)
        a_rb = act.n

        def _(e, a_rb=a_rb):
            e.wait_ge(sem_act, a_rb)
            pwait(e, "ps_p")
            pwait(e, "ps_s2")
            e.matmul(ps_p[:], onesrow[:], rowb_1, start=True, stop=True)
            pe.tick(e.matmul(ps_s2[:], onesrow[:], rowb2, start=True,
                             stop=True))
        TEN(_)
        pe_b1 = pe.n

        def _(e, pe_b1=pe_b1, dve_now=dve_now):
            e.wait_ge(sem_pe, pe_b1)
            e.wait_ge(sem_dve, dve_now)
            e.copy(out=rstd_bc, in_=ps_p[:])
            act.tick(e.copy(out=mr_bc, in_=ps_s2[:]))
        ACT(_)
        pfree("ps_p", act)
        pfree("ps_s2", act)
        return act.n

    gu_state = {"x2n_ready": 0}
    DNPS = [(ps_g, "ps_g"), (ps_u, "ps_u"), (ps_t1f, "ps_t1"),
            (ps_t2f, "ps_t2")]

    def emit_gu_batch(wc, m0, m1):
        for m in range(m0, m1):
            par = m % 2
            wch = wvc[par]
            wg_sb = wgu[:][:, par * 2 * DM:par * 2 * DM + DM]
            wu_sb = wgu[:][:, par * 2 * DM + DM:(par + 1) * 2 * DM]
            psg, psg_n = (ps_g, "ps_g") if par == 0 else (ps_t1f, "ps_t1")
            psu, psu_n = (ps_u, "ps_u") if par == 0 else (ps_t2f, "ps_t2")
            sg_sb = sgub[:][:, par * 2 * TC:par * 2 * TC + TC]
            ub_sb = sgub[:][:, par * 2 * TC + TC:(par + 1) * 2 * TC]

            pre = gu_state.get(("wvmark", wc, m))
            if pre is None:
                def _(e, m=m, wc=wc, wch=wch, wg_sb=wg_sb, wu_sb=wu_sb):
                    e.wait_ge(sem_wsc, wsc_all)
                    if wc == 0 and m < 2:
                        e.wait_ge(sem_act, act_u_chunk[3])
                    mm = gu_state.get(("pe_w", m - 2))
                    if mm:
                        e.wait_ge(sem_pe, mm)
                    mm2 = gu_state.get("down_done")
                    if mm2 and m < 2:
                        e.wait_ge(sem_pe, mm2)
                    wch.tick(e.dma_start(out=wg_sb, in_=wg_scr[m]))
                    wch.tick(e.dma_start(out=wu_sb, in_=wu_scr[m]))
                SY(_)
                wv_mark = wch.n
            else:
                wv_mark = pre

            def _(e, m=m, wc=wc, wch=wch, wv_mark=wv_mark, wg_sb=wg_sb,
                  wu_sb=wu_sb, psg=psg, psu=psu, psg_n=psg_n, psu_n=psu_n):
                e.wait_ge(wch.sem, wv_mark)
                e.wait_ge(sem_dve, gu_state["x2n_ready"])
                pwait(e, psg_n)
                pwait(e, psu_n)
                sl = slotv(wc)
                for k in range(DC):
                    e.matmul(psg[:], wg_sb[:, k * P:(k + 1) * P],
                             sl[:, k * TC:(k + 1) * TC],
                             start=(k == 0), stop=(k == DC - 1))
                for k in range(DC):
                    ins = e.matmul(psu[:], wu_sb[:, k * P:(k + 1) * P],
                                   sl[:, k * TC:(k + 1) * TC],
                                   start=(k == 0), stop=(k == DC - 1))
                    if k == DC - 1:
                        pe.tick(ins)
            TEN(_)
            gu_state[("pe_w", m)] = pe.n
            pe_gu = pe.n

            def _(e, m=m, m0=m0, m1=m1, wc=wc, pe_gu=pe_gu, psg=psg, psu=psu,
                  sg_sb=sg_sb, ub_sb=ub_sb):
                if m == m0:
                    load_act_set(e, 18)
                e.wait_ge(sem_pe, pe_gu)
                if m < 2:
                    oo = gu_state.get(("otok_done", 0 if wc < 1 else wc - 1), 0) \
                        if wc >= 1 else 0
                    if oo:
                        e.wait_ge(sem_out, oo)
                mm = gu_state.get(("hidtt", m - 2))
                if mm:
                    e.wait_ge(sem_po, mm)
                e.activation(out=sg_sb, in_=psg[:], func=AF.Silu, scale=S127)
                act.tick(e.activation(out=ub_sb, in_=psu[:], func=AF.Copy,
                                      scale=S127))
                if m == m1 - 1:
                    load_act_set(e, 6)
            ACT(_)
            pfree(psg_n, act)
            pfree(psu_n, act)
            a_gu = act.n

            def _(e, m=m, a_gu=a_gu, sg_sb=sg_sb, ub_sb=ub_sb):
                e.wait_ge(sem_act, a_gu)
                mm = gu_state.get(("hidst", m - 2))
                if mm:
                    e.wait_ge(sem_sgub, mm)
                par = m % 2
                po.tick(e.tensor_mul(hidm[:][:, par * TC:(par + 1) * TC],
                                     sg_sb, ub_sb))
            POOL(_)
            gu_state[("hidtt", m)] = po.n
            po_h = po.n

            def _(e, m=m, po_h=po_h):
                e.wait_ge(sem_po, po_h)
                par = m % 2
                sguc.tick(e.dma_start(out=hid_hbm[m],
                                      in_=hidm[:][:, par * TC:(par + 1) * TC]))
            GP(_)
            gu_state[("sgst", m)] = sguc.n
            gu_state[("hidst", m)] = sguc.n
        gu_state[("gudone", wc)] = pe.n

    def emit_down_mgroup(dn, fh, m2):
        wc = dn["wc"]
        dwnb, sq1 = dn["dwnb"], dn["sq1"]
        sy_hidt = dn["sy_hidt"]
        half_marks, st_marks, a_dn_marks = (dn["half_marks"], dn["st_marks"],
                                            dn["a_dn_marks"])
        psd, psd_n = DNPS[m2 % 4]
        HC = MC // 2
        HCW = HC * P
        for h in range(2):
            wch2 = wdc if h == 0 else wdc2
            prev = half_marks.get((fh, m2 - 1, h))
            pe_now = pe.n

            def _(e, m2=m2, fh=fh, h=h, wch2=wch2, prev=prev, pe_now=pe_now):
                if prev is not None:
                    e.wait_ge(sem_pe, prev)
                elif fh > 0 or gu_state.get("wd_used"):
                    e.wait_ge(sem_pe, pe_now)
                wch2.tick(e.dma_start(
                    out=wdm2[:, h * HCW:(h + 1) * HCW],
                    in_=wd_scr[m2][:, h * HCW:(h + 1) * HCW]))
            GP(_)
        gu_state["wd_used"] = True
        for h in range(2):
            wch2 = wdc if h == 0 else wdc2
            gp_wd = wch2.n

            def _(e, m2=m2, fh=fh, h=h, sy_hidt=sy_hidt, gp_wd=gp_wd,
                  psd=psd, psd_n=psd_n, wch2=wch2):
                if h == 0:
                    e.wait_ge(sem_hidt, sy_hidt)
                    pwait(e, psd_n)
                e.wait_ge(wch2.sem, gp_wd)
                for k2 in range(HC):
                    kk = h * HC + k2
                    ins = e.matmul(psd[:, 0:FH],
                                   wdm2[:, kk * P:(kk + 1) * P],
                                   hidt[:][:, kk * FH:(kk + 1) * FH],
                                   start=(kk == 0), stop=(kk == MC - 1))
                    if k2 == HC - 1:
                        pe.tick(ins)
            TEN(_)
            half_marks[(fh, m2, h)] = pe.n
        pe_dn = pe.n
        if m2 >= 2:
            emit_dn_stats(dn, fh, m2 - 2)
        st_guard = st_marks.get((fh, m2 - 2), 0)
        po_guard = dn["po_guard"]
        sg_done = dn["sg_done"]

        def _(e, m2=m2, fh=fh, pe_dn=pe_dn, psd=psd,
              po_guard=po_guard, st_guard=st_guard, sg_done=sg_done):
            e.wait_ge(sem_pe, pe_dn)
            if m2 == 0 and fh == 0:
                if po_guard:
                    e.wait_ge(sem_pe, po_guard)
                e.wait_ge(sem_sgub, sg_done)
            if st_guard:
                e.wait_ge(sem_pe, st_guard)
            dsl = dwnb[:, m2 * TC + fh * FH:m2 * TC + fh * FH + FH]
            e.activation(out=dsl, in_=psd[:, 0:FH], func=AF.Copy,
                         scale=S127)
            act.tick(e.activation(
                out=sq1[:, (m2 % 2) * FH:(m2 % 2 + 1) * FH],
                in_=psd[:, 0:FH], func=AF.Square, scale=S127))
        ACT(_)
        pfree(psd_n, act)
        a_dn_marks[(fh, m2)] = act.n

    def emit_dn_stats(dn, fh, k):
        dwnb, sq1 = dn["dwnb"], dn["sq1"]

        def _(e, k=k, fh=fh, a_dn=dn["a_dn_marks"][(fh, k)]):
            e.wait_ge(sem_act, a_dn)
            if k == 0 and fh == 0:
                pwait(e, "ps_s1")
                pwait(e, "ps_s2")
            e.matmul(ps_s1[0:1, fh * FH:(fh + 1) * FH], onesb[:],
                     dwnb[:, k * TC + fh * FH:k * TC + fh * FH + FH],
                     start=(k == 0), stop=(k == DC - 1))
            pe.tick(e.matmul(
                ps_s2[0:1, fh * FH:(fh + 1) * FH], onesb[:],
                sq1[:, (k % 2) * FH:(k % 2 + 1) * FH],
                start=(k == 0), stop=(k == DC - 1)))
        TEN(_)
        dn["st_marks"][(fh, k)] = pe.n

    def emit_dn_hidt(dn, fh):
        sg_done = dn["sg_done"]
        pe_hidt_guard = pe.n if fh == 1 else max(
            gu_state.get("down_done", 0),
            gu_state.get(("midstats_pe", dn["wc"] + 2), 0))

        def _(e, fh=fh, sg_done=sg_done, pe_hidt_guard=pe_hidt_guard):
            e.wait_ge(sem_sgub, sg_done)
            if pe_hidt_guard:
                e.wait_ge(sem_pe, pe_hidt_guard)
            hidtc.tick(e.dma_start(
                out=hidt[:].rearrange("p (a t) -> p a t", t=FH),
                in_=hid_hbm[:, :, fh * FH:(fh + 1) * FH].transpose([1, 0, 2])))
        GP(_)
        dn["sy_hidt"] = hidtc.n

    def emit_down_part1(wc, po_guard):
        dn = {"wc": wc, "po_guard": po_guard,
              "dwnb": slotv(wc), "sq1": hidm[:],
              "sg_done": gu_state[("sgst", MC - 1)],
              "half_marks": {}, "st_marks": {}, "a_dn_marks": {}}
        gu_state[("dn", wc)] = dn
        emit_dn_hidt(dn, 0)
        for m2 in range(4):
            emit_down_mgroup(dn, 0, m2)

    def emit_down_part1b(wc):
        dn = gu_state[("dn", wc)]
        for m2 in range(4, DC):
            emit_down_mgroup(dn, 0, m2)
        emit_dn_stats(dn, 0, DC - 2)
        emit_dn_stats(dn, 0, DC - 1)
        emit_dn_hidt(dn, 1)

    def emit_down_part2a(wc):
        dn = gu_state[("dn", wc)]
        dwnb = dn["dwnb"]
        for m2 in range(DC):
            emit_down_mgroup(dn, 1, m2)
        emit_dn_stats(dn, 1, DC - 2)
        emit_dn_stats(dn, 1, DC - 1)
        gu_state["down_done"] = pe.n
        gu_state[("ddone", wc)] = pe.n
        nwc = wc + 1
        if nwc < NCH:
            for m in (0, 1):
                par = m % 2
                wch = wvc[par]
                wg_sb = wgu[:][:, par * 2 * DM:par * 2 * DM + DM]
                wu_sb = wgu[:][:, par * 2 * DM + DM:(par + 1) * 2 * DM]

                def _(e, m=m, wch=wch, wg_sb=wg_sb, wu_sb=wu_sb,
                      dd=gu_state["down_done"]):
                    e.wait_ge(sem_wsc, wsc_all)
                    e.wait_ge(sem_pe, dd)
                    wch.tick(e.dma_start(out=wg_sb, in_=wg_scr[m]))
                    wch.tick(e.dma_start(out=wu_sb, in_=wu_scr[m]))
                SY(_)
                gu_state[("wvmark", nwc, m)] = wch.n
        pe_stats_dn = pe.n
        def _(e, pe_stats_dn=pe_stats_dn):
            e.wait_ge(sem_pe, pe_stats_dn)
            e.copy(out=rows_s, in_=ps_s1[0:1, :])
            e.copy(out=rows_q, in_=ps_s2[0:1, :])
            e.mul(out=rows_s, in_=rows_s, mul=1.0 / DM)
            e.mul(out=rows_q, in_=rows_q, mul=1.0 / DM)
            act.tick(e.activation(out=rows_a1, in_=rows_s, func=AF.Square))
        ACT(_)
        a_m4 = act.n

        def _(e, a_m4=a_m4):
            e.wait_ge(sem_act, a_m4)
            dve.tick(e.tensor_sub(rows_a2, rows_q, rows_a1))
        DVE(_)
        d_v4 = dve.n

        def _(e, d_v4=d_v4):
            e.wait_ge(sem_dve, d_v4)
            e.activation(out=rows_a1, in_=rows_a2, func=AF.Ln,
                         bias=cst_eps[0:1, :])
            act.tick(e.activation(out=rows_a1, in_=rows_a1, func=AF.Exp,
                                  scale=-0.5))
        ACT(_)
        a_r4 = act.n

        def _(e, a_r4=a_r4):
            e.wait_ge(sem_act, a_r4)
            dve.tick(e.tensor_mul(rows_a2, rows_s, rows_a1))
        DVE(_)
        d_mr4 = dve.n
        dve_now4 = dve.n

        def _(e, d_mr4=d_mr4):
            e.wait_ge(sem_dve, d_mr4)
            e.wait_ge(sem_sgub, sguc.n)
            e.activation(out=rowb_1, in_=rows_a1, func=AF.Copy)
            act.tick(e.activation(out=rowb2, in_=rows_a2, func=AF.Copy))
        ACT(_)
        a_rb4 = act.n

        def _(e, a_rb4=a_rb4):
            e.wait_ge(sem_act, a_rb4)
            pwait(e, "ps_p")
            pwait(e, "ps_s2")
            e.matmul(ps_p[:], onesrow[:], rowb_1, start=True, stop=True)
            pe.tick(e.matmul(ps_s2[:], onesrow[:], rowb2, start=True,
                             stop=True))
        TEN(_)
        pe_b14 = pe.n

        def _(e, pe_b14=pe_b14, dve_now4=dve_now4):
            e.wait_ge(sem_pe, pe_b14)
            e.wait_ge(sem_dve, dve_now4)
            e.copy(out=rstd_bc, in_=ps_p[:])
            act.tick(e.copy(out=mr_bc, in_=ps_s2[:]))
        ACT(_)
        pfree("ps_p", act)
        pfree("ps_s2", act)
        bc4 = act.n

        def _(e, bc4=bc4, wc=wc):
            e.wait_ge(sem_act, bc4)
            for dc in range(DC):
                e.tensor_mul(t1s, dwnb[:, dc * TC:(dc + 1) * TC], rstd_bc)
                e.tensor_sub(t1s, t1s, mr_bc)
                xsl = xT[:][:, dc * T + wc * TC:dc * T + (wc + 1) * TC]
                ins = e.tensor_tensor(out=xsl, in0=xsl, in1=t1s, op=ALU.add)
                if dc == DC - 1:
                    dve.tick(ins)
        DVE(_)
        dve_x3 = dve.n
        gu_state["x3_dve"] = dve_x3


    def emit_down_part2out(wc):
        dn = gu_state[("dn", wc)]
        dve_x3 = gu_state["x3_dve"]
        otok = sgub[:].bitcast(F32)[:, 0:DM]
        for tt in range(TC // P):
            act_base = act.n
            sy_now = outc.n

            def _(e, tt=tt, wc=wc, dve_x3=dve_x3, act_base=act_base):
                e.wait_ge(sem_dve, dve_x3)
                pwait(e, "ps_t1")
                pwait(e, "ps_t2")
                if tt > 0:
                    e.wait_ge(sem_act, act_base)
                for dc in range(DC):
                    pst = ps_t1 if dc % 2 == 0 else ps_t2
                    if dc >= 2:
                        e.wait_ge(sem_act, act_base + dc - 1)
                    pe.tick(e.transpose(
                        pst[:],
                        xT[:][:, dc * T + wc * TC + tt * P:
                              dc * T + wc * TC + (tt + 1) * P],
                        idf[:]))
            TEN(_)
            pe_base = pe.n - DC

            def _(e, tt=tt, pe_base=pe_base, sy_now=sy_now, otok=otok,
                  tt_guard=gu_state[("hidtt", MC - 1)]):
                for dc in range(DC):
                    pst = ps_t1 if dc % 2 == 0 else ps_t2
                    e.wait_ge(sem_pe, pe_base + dc + 1)
                    if dc == 0:
                        e.wait_ge(sem_out, sy_now)
                        if tt == 0:
                            e.wait_ge(sem_po, tt_guard)
                    act.tick(e.copy(out=otok[:, dc * P:(dc + 1) * P],
                                    in_=pst[:]))
            ACT(_)
            a_ot = act.n

            def _(e, tt=tt, wc=wc, a_ot=a_ot, otok=otok):
                e.wait_ge(sem_act, a_ot)
                outc.tick(e.dma_start(
                    out=y_out[wc * TC + tt * P:wc * TC + (tt + 1) * P, :],
                    in_=otok))
            SY(_)
        pfree("ps_t1", act)
        pfree("ps_t2", act)
        gu_state[("otok_done", wc)] = outc.n

    head_state = {}
    rbs = bscb[:][0:1, 0:2 * TC]

    def emit_head_stats_mid(cn):
        hA = [hidt[:][:, 0:TC], hidt[:][:, TC:2 * TC]]
        hB = [hidt[:][:, 2 * TC:3 * TC], hidt[:][:, 3 * TC:4 * TC]]
        c0 = cn * TC
        dd_guard = gu_state.get(("ddone", cn - 3), 0)
        pe_slot = {}
        for i in range(DC):
            src = xT[:][:, i * T + c0:i * T + c0 + TC]

            def _(e, i=i, src=src, dd_guard=dd_guard):
                if i == 0:
                    e.wait_ge(sem_act, act_u_chunk[cn])
                    e.wait_ge(sem_wsc, wsc_all)
                    if dd_guard:
                        e.wait_ge(sem_pe, dd_guard)
                w = pe_slot.get(i - 2)
                if w:
                    e.wait_ge(sem_pe, w)
                e.activation(out=hA[i % 2], in_=src, func=AF.Copy)
                act.tick(e.activation(out=hB[i % 2], in_=src, func=AF.Square))
            ACT(_)
            a_i = act.n

            def _(e, i=i, a_i=a_i):
                e.wait_ge(sem_act, a_i)
                if i == 0:
                    pwait(e, "ps_s1")
                    pwait(e, "ps_s2")
                e.matmul(ps_s1[0:1, :], onesb[:], hA[i % 2],
                         start=(i == 0), stop=(i == DC - 1))
                pe.tick(e.matmul(ps_s2[0:1, :], onesb[:], hB[i % 2],
                                 start=(i == 0), stop=(i == DC - 1)))
            TEN(_)
            pe_slot[i] = pe.n
        pe_stats = pe.n
        gu_state[("midstats_pe", cn)] = pe.n

        def _(e, pe_stats=pe_stats):
            e.wait_ge(sem_pe, pe_stats)
            e.copy(out=rows_s, in_=ps_s1[0:1, :])
            e.copy(out=rows_q, in_=ps_s2[0:1, :])
            e.mul(out=rows_s, in_=rows_s, mul=1.0 / DM)
            e.mul(out=rows_q, in_=rows_q, mul=1.0 / DM)
            act.tick(e.activation(out=rows_a1, in_=rows_s, func=AF.Square))
        ACT(_)
        pfree("ps_s1", act)
        pfree("ps_s2", act)
        a_m = act.n

        def _(e, a_m=a_m):
            e.wait_ge(sem_act, a_m)
            dve.tick(e.tensor_sub(rows_a2, rows_q, rows_a1))
        DVE(_)
        d_v = dve.n

        def _(e, d_v=d_v):
            e.wait_ge(sem_dve, d_v)
            e.activation(out=rows_a1, in_=rows_a2, func=AF.Ln,
                         bias=cst_eps[0:1, :])
            act.tick(e.activation(out=rows_a1, in_=rows_a1, func=AF.Exp,
                                  scale=-0.5))
        ACT(_)
        a_r = act.n

        def _(e, a_r=a_r):
            e.wait_ge(sem_act, a_r)
            dve.tick(e.tensor_mul(rows_a2, rows_s, rows_a1))
        DVE(_)
        d_mr = dve.n
        bb_g = head_state.get((cn - 1, "gp_bbc"), 0)
        cb_g = head_state.get((cn - 1, "sy_cbc"), 0)

        def _(e, d_mr=d_mr, bb_g=bb_g, cb_g=cb_g):
            e.wait_ge(sem_dve, d_mr)
            if bb_g:
                e.wait_ge(sem_bbc, bb_g)
            if cb_g:
                e.wait_ge(sem_cbc, cb_g)
            e.activation(out=rbs[:, 0:TC], in_=rows_a1, func=AF.Copy)
            act.tick(e.activation(out=rbs[:, TC:2 * TC], in_=rows_a2,
                                  func=AF.Copy))
        ACT(_)
        gu_state[("midrows_act", cn)] = act.n

    def emit_head_bcast(cn, scan_mark):
        mid_a = gu_state[("midrows_act", cn)]

        def _(e, mid_a=mid_a, scan_mark=scan_mark):
            e.wait_ge(sem_act, mid_a)
            e.wait_ge(sem_dve, scan_mark)
            pwait(e, "ps_p")
            pwait(e, "ps_d")
            e.matmul(ps_p[:], onesrow[:], rbs[:, 0:TC], start=True, stop=True)
            pe.tick(e.matmul(ps_d[:], onesrow[:], rbs[:, TC:2 * TC],
                             start=True, stop=True))
        TEN(_)
        pe_b = pe.n

        def _(e, pe_b=pe_b):
            e.wait_ge(sem_pe, pe_b)
            e.copy(out=rstd2_bc, in_=ps_p[:])
            act.tick(e.copy(out=mr2_bc, in_=ps_d[:]))
        ACT(_)
        pfree("ps_p", act)
        pfree("ps_d", act)
        bc1 = act.n
        c0 = cn * TC

        def _(e, bc1=bc1, c0=c0):
            e.wait_ge(sem_act, bc1)
            for dc in range(DC):
                e.tensor_mul(t1s, xT[:][:, dc * T + c0:dc * T + c0 + TC],
                             rstd2_bc)
                ins = e.tensor_sub(unb[:][:, dc * TC:(dc + 1) * TC],
                                   t1s, mr2_bc)
                if dc == DC - 1:
                    dve.tick(ins)
        DVE(_)
        head_state[(cn, "dve_unorm")] = dve.n


    def emit_head_stats(c, dve_chunk_end):
        c0 = c * TC
        bc1 = emit_stats_rows_bcast(
            lambda i: xT[:][:, i * T + c0:i * T + c0 + TC],
            [(sem_act, act_u_chunk[c]), (sem_dve, dve_chunk_end)])

        def _(e, bc1=bc1, c0=c0):
            e.wait_ge(sem_act, bc1)
            for dc in range(DC):
                e.tensor_mul(t1s, xT[:][:, dc * T + c0:dc * T + c0 + TC],
                             rstd_bc)
                ins = e.tensor_sub(unb[:][:, dc * TC:(dc + 1) * TC],
                                   t1s, mr_bc)
                if dc == DC - 1:
                    dve.tick(ins)
        DVE(_)
        head_state[(c, "dve_unorm")] = dve.n

    def emit_head_proj(c, dve_chunk_end):
        dve_unorm = head_state[(c, "dve_unorm")]
        act_now = act.n

        def _(e, dve_unorm=dve_unorm, act_now=act_now):
            e.wait_ge(sem_dve, dve_unorm)
            e.wait_ge(sem_gp, gp_small)
            e.wait_ge(sem_act, act_now)
            pwait(e, "ps_p")
            for dc in range(DC):
                ins = e.matmul(ps_p[0:R, :], wxdtTb[:, dc * R:(dc + 1) * R],
                               unb[:][:, dc * TC:(dc + 1) * TC],
                               start=(dc == 0), stop=(dc == DC - 1))
                if dc == DC - 1:
                    pe.tick(ins)
        TEN(_)
        pe_xdt = pe.n

        def _(e, pe_xdt=pe_xdt):
            e.wait_ge(sem_pe, pe_xdt)
            act.tick(e.activation(out=xdtb[:], in_=ps_p[0:R, :],
                                  func=AF.Identity, bias=bxdtc[:]))
        ACT(_)
        pfree("ps_p", act)
        head_state[(c, "act_xdt")] = act.n

        def _(e, dve_unorm=dve_unorm):
            e.wait_ge(sem_dve, dve_unorm)
            pwait(e, "ps_p")
            pwait(e, "ps_d")
            for dc in range(DC):
                e.matmul(ps_p[0:NST, :], wbTb[:, dc * NST:(dc + 1) * NST],
                         unb[:][:, dc * TC:(dc + 1) * TC],
                         start=(dc == 0), stop=(dc == DC - 1))
            for dc in range(DC):
                ins = e.matmul(ps_d[0:NST, :], wcTb[:, dc * NST:(dc + 1) * NST],
                               unb[:][:, dc * TC:(dc + 1) * TC],
                               start=(dc == 0), stop=(dc == DC - 1))
                if dc == DC - 1:
                    pe.tick(ins)
        TEN(_)
        pe_bcproj = pe.n

        def _(e, pe_bcproj=pe_bcproj):
            e.wait_ge(sem_pe, pe_bcproj)
            e.activation(out=bscb[:, 0:TC], in_=ps_p[0:NST, :],
                         func=AF.Identity, bias=bbcol[:])
            act.tick(e.activation(out=bscb[:, TC:2 * TC], in_=ps_d[0:NST, :],
                                  func=AF.Identity, bias=bccol[:]))
        ACT(_)
        pfree("ps_p", act)
        pfree("ps_d", act)
        act_bc = act.n

        dve_prev_reader = dve_chunk_end if c > 0 else 0

        def _(e, act_bc=act_bc, dve_prev_reader=dve_prev_reader):
            e.wait_ge(sem_act, act_bc)
            if dve_prev_reader:
                e.wait_ge(sem_dve, dve_prev_reader)
            bbcc.tick(e.dma_start(
                out=bbc[:][0:1, :].rearrange("o (n t) -> o n t", t=TC),
                in_=bscb[0:NST, 0:TC]))
            k = 1
            while k < P:
                kk = min(k, P - k)
                e.wait_ge(sem_bbc, bbcc.n)
                bbcc.tick(e.dma_start(out=bbc[:][k:k + kk, :],
                                      in_=bbc[:][0:kk, :]))
                k += kk
        SY(_)
        head_state[(c, "gp_bbc")] = bbcc.n

        def _(e, act_bc=act_bc, dve_prev_reader=dve_prev_reader):
            e.wait_ge(sem_act, act_bc)
            if dve_prev_reader:
                e.wait_ge(sem_dve, dve_prev_reader)
            cbcc.tick(e.dma_start(
                out=cbc[:][0:1, :].rearrange("o (n t) -> o n t", t=TC),
                in_=bscb[0:NST, TC:2 * TC]))
            k = 1
            while k < P:
                kk = min(k, P - k)
                e.wait_ge(sem_cbc, cbcc.n)
                cbcc.tick(e.dma_start(out=cbc[:][k:k + kk, :],
                                      in_=cbc[:][0:kk, :]))
                k += kk
        GP(_)
        head_state[(c, "sy_cbc")] = cbcc.n

    emit_head_stats(0, 0)
    emit_head_proj(0, 0)

    stage = [hidt[:][:, 0:H], hidt[:][:, H:2 * H]]
    cast_jobs = ([("wg", m) for m in range(MC)] + [("wu", m) for m in range(MC)]
                 + [("wd", m) for m in range(DC)])
    wc_marks = {}
    for j, (kind, m) in enumerate(cast_jobs):
        par = j % 2
        wch = wcc[par]
        src = {"wg": wg_in, "wu": wu_in, "wd": wd_in}[kind][m]
        width = H if kind == "wd" else DM
        sbuf = stage[par][:, 0:width]

        def _(e, j=j, wch=wch, src=src, sbuf=sbuf):
            if j >= 2:
                e.wait_ge(sem_wsc, (j - 1) * 16)
            wch.tick(e.dma_start(out=sbuf, in_=src))
        GP(_)
        wc_marks[j] = wch.n

    def emit_cast_stores(j0, j1):
        for j in range(j0, min(j1, len(cast_jobs))):
            kind, m = cast_jobs[j]
            par = j % 2
            wch = wcc[par]
            dst = {"wg": wg_scr, "wu": wu_scr, "wd": wd_scr}[kind][m]
            width = H if kind == "wd" else DM
            sbuf = stage[par][:, 0:width]

            def _(e, wch=wch, wc_mark=wc_marks[j], dst=dst, sbuf=sbuf):
                e.wait_ge(wch.sem, wc_mark)
                wscc.tick(e.dma_start(out=dst, in_=sbuf))
            SY(_)
    WSC_ALL_N = len(cast_jobs) * 16
    wsc_all = WSC_ALL_N

    dve_chunk_end = 0
    for c in range(NCH):
        c0 = c * TC
        slot = slotv(c)
        ybfv = slot
        slot_guard_pe = max(gu_state.get(("gudone", c - 2), 0),
                            gu_state.get(("ddone", c - 2), 0))
        slot_guard_out = 0
        dve_unorm = head_state[(c, "dve_unorm")]
        act_xdt = head_state[(c, "act_xdt")]
        gp_bbc = head_state[(c, "gp_bbc")]
        sy_cbc = head_state[(c, "sy_cbc")]
        dve_prev_reader = dve_chunk_end if c > 0 else 0

        t_scans_prev = 0
        t_ym0_prev = 0
        dve_scan_prev = 0
        for dc in range(DC):
            if c >= 1 and dc < 6:
                emit_gu_batch(c - 1, 4 * dc, 4 * dc + 4)
            dtsl = dtb[:][:, (dc % 2) * TC:(dc % 2 + 1) * TC]
            A0R = abar_h0 if dc % 2 == 0 else hbuf
            A1R = abar_h1
            OUT0R = hbuf if dc % 2 == 0 else abar_h0

            def _(e, dc=dc, act_xdt=act_xdt):
                e.wait_ge(sem_act, act_xdt)
                pwait(e, "ps_d")
                pe.tick(e.matmul(ps_d[:], wdtTb[:][:, dc * P:(dc + 1) * P],
                                 xdtb[:], start=True, stop=True))
            TEN(_)
            pe_dt = pe.n

            def _(e, dc=dc, c=c, pe_dt=pe_dt, dtsl=dtsl, A0R=A0R, A1R=A1R,
                  t_scans_prev=t_scans_prev, t_ym0_prev=t_ym0_prev,
                  dve_prev_reader=dve_prev_reader):
                e.wait_ge(sem_pe, pe_dt)
                if dc >= 1:
                    e.wait_ge(sem_dve, t_scans_prev)
                elif dve_prev_reader:
                    e.wait_ge(sem_dve, dve_prev_reader)
                e.activation(out=etx, in_=ps_d[:], func=AF.Exp,
                             bias=bdtp[:][:, dc:dc + 1])
                e.activation(out=dtsl, in_=etx, func=AF.Ln, bias=cst_one)
                for n in range(HNST, NST):
                    e.activation(
                        out=A1R[:, (n - HNST) * TC:(n - HNST + 1) * TC],
                        in_=dtsl, func=AF.Exp,
                        scale=aneg[:][:, dc * NST + n:dc * NST + n + 1])
                if dc >= 1:
                    e.wait_ge(sem_dve, t_ym0_prev)
                for n in range(HNST):
                    ins = e.activation(
                        out=A0R[:, n * TC:(n + 1) * TC],
                        in_=dtsl, func=AF.Exp,
                        scale=aneg[:][:, dc * NST + n:dc * NST + n + 1])
                    if n == HNST - 1:
                        act.tick(ins)
            ACT(_)
            pfree("ps_d", act)
            act_abar = act.n

            ysl = ybfv[:, dc * TC:(dc + 1) * TC]

            def _(e, dc=dc, c=c, act_abar=act_abar, gp_bbc=gp_bbc,
                  sy_cbc=sy_cbc, dtsl=dtsl, ysl=ysl, A0R=A0R, A1R=A1R,
                  OUT0R=OUT0R, slot_guard_pe=slot_guard_pe,
                  slot_guard_out=slot_guard_out):
                e.wait_ge(sem_act, act_abar)
                e.wait_ge(sem_bbc, gp_bbc)
                av0 = A0R.rearrange("p (n t) -> p n t", t=TC)
                av1 = A1R.rearrange("p (n t) -> p n t", t=TC)
                if c > 0:
                    hl = hlast[:][:, dc * NST:(dc + 1) * NST]
                    e.tensor_mul(fix16[:][:, 0:HNST], av0[:, :, 0],
                                 hl[:, 0:HNST])
                    e.tensor_mul(fix16[:][:, HNST:NST], av1[:, :, 0],
                                 hl[:, HNST:NST])
                dtu = t1s
                e.tensor_mul(dtu, dtsl, unb[:][:, dc * TC:(dc + 1) * TC])
                dtu_b = dtu.unsqueeze(1).broadcast_to([P, NST, TC])
                bbc_v = bbc[:].rearrange("p (n t) -> p n t", t=TC)
                xbuf_v = xbuf.rearrange("p (n t) -> p n t", t=TC)
                e.tensor_tensor(out=xbuf_v, in0=dtu_b, in1=bbc_v, op=ALU.mult)
                if c > 0:
                    e.tensor_tensor(out=xbuf_v[:, :, 0], in0=xbuf_v[:, :, 0],
                                    in1=fix16[:], op=ALU.add)
                e.memset(av0[:, :, 0:1], 0.0)
                dve.tick(e.memset(av1[:, :, 0:1], 0.0))
                e.wait_ge(sem_dve, dve.n)
                e.tensor_tensor_scan(
                    out=OUT0R, data0=A0R, data1=xbuf[:, 0:HNST * TC],
                    initial=0.0, op0=ALU.mult, op1=ALU.add)
                dve.tick(e.tensor_tensor_scan(
                    out=A0R, data0=A1R, data1=xbuf[:, HNST * TC:],
                    initial=0.0, op0=ALU.mult, op1=ALU.add))
                e.wait_ge(sem_cbc, sy_cbc)
                if dc == 0:
                    if slot_guard_pe:
                        e.wait_ge(sem_pe, slot_guard_pe)
                    if slot_guard_out:
                        e.wait_ge(sem_out, slot_guard_out)
                hv0 = OUT0R.rearrange("p (n t) -> p n t", t=TC)
                cb0 = cbc[:][:, 0:HNST * TC].rearrange("p (n t) -> p n t", t=TC)
                xb0 = xbuf[:, 0:HNST * TC]
                e.tensor_tensor(
                    out=xb0.rearrange("p (n t) -> p n t", t=TC),
                    in0=hv0, in1=cb0, op=ALU.mult)
                dve.tick(e.tensor_copy(
                    out=hlast[:][:, dc * NST:dc * NST + HNST],
                    in_=hv0[:, :, TC - 1]))
                hv1 = A0R.rearrange("p (n t) -> p n t", t=TC)
                cb1 = cbc[:][:, HNST * TC:].rearrange("p (n t) -> p n t", t=TC)
                xb1 = xbuf[:, HNST * TC:]
                e.tensor_tensor(out=xb1.rearrange("p (n t) -> p n t", t=TC),
                                in0=hv1, in1=cb1, op=ALU.mult)
                vin, width = xbuf, NST * TC
                while width > 2 * TC:
                    vv = vin.rearrange("p (j two t) -> p j two t", two=2, t=TC)
                    outw = width // 2
                    vout = xbuf[:, 0:outw]
                    vo = vout.rearrange("p (j t) -> p j t", t=TC)
                    e.tensor_tensor(out=vo, in0=vv[:, :, 0, :],
                                    in1=vv[:, :, 1, :], op=ALU.add)
                    vin, width = vout, outw
                vv = vin.rearrange("p (j two t) -> p j two t", two=2, t=TC)
                e.tensor_tensor(out=ysl, in0=vv[:, 0, 0, :],
                                in1=vv[:, 0, 1, :], op=ALU.add)
                e.tensor_copy(
                    out=hlast[:][:, dc * NST + HNST:(dc + 1) * NST],
                    in_=hv1[:, :, TC - 1])
                dve.tick(e.scalar_tensor_tensor(
                    out=ysl, in0=unb[:][:, dc * TC:(dc + 1) * TC],
                    scalar=dskp[:][:, dc:dc + 1], in1=ysl,
                    op0=ALU.mult, op1=ALU.add))
            DVE(_)
            dve_scan_prev = dve.n
            t_ym0_prev = dve.n - 1
            t_scans_prev = dve.n - 2
            if c == 0 and dc >= 2:
                emit_u_tile(2 * dc)
                emit_u_tile(2 * dc + 1)
                emit_cast_stores((dc - 2) * 10, (dc - 1) * 10)
            if c + 1 < NCH and dc == (7 if c == 0 else 2):
                emit_head_stats_mid(c + 1)
            if c >= 1 and dc == 6:
                emit_down_part1(c - 1, gu_state[("gudone", c - 1)])
            if c >= 1 and dc == 7:
                emit_down_part1b(c - 1)
        dve_mamba = dve.n

        if c + 1 < NCH:
            emit_head_bcast(c + 1, t_scans_prev)
            emit_head_proj(c + 1, dve_mamba)
        if c >= 1:
            emit_down_part2a(c - 1)

        bc2 = emit_stats_rows_bcast(
            lambda i: ybfv[:, i * TC:(i + 1) * TC],
            [(sem_dve, dve_mamba)], src_bf16=True)

        def _(e, bc2=bc2, c0=c0, ybfv=ybfv, slot=slot):
            e.wait_ge(sem_act, bc2)
            for dc in range(DC):
                e.tensor_mul(t1s, ybfv[:, dc * TC:(dc + 1) * TC], rstd_bc)
                e.tensor_sub(t1s, t1s, mr_bc)
                xsl = xT[:][:, dc * T + c0:dc * T + c0 + TC]
                e.tensor_tensor(out=xsl, in0=xsl, in1=t1s, op=ALU.add)
                ins = e.tensor_copy(out=slot[:, dc * TC:(dc + 1) * TC],
                                    in_=xsl)
                if dc == DC - 1:
                    dve.tick(ins)
        DVE(_)
        dve_x2 = dve.n

        bc3 = emit_stats_rows_bcast(
            lambda i: slot[:, i * TC:(i + 1) * TC],
            [(sem_dve, dve_x2)], src_bf16=True)

        def _(e, bc3=bc3, slot=slot):
            e.wait_ge(sem_act, bc3)
            for dc in range(DC):
                e.tensor_mul(t1s, slot[:, dc * TC:(dc + 1) * TC], rstd_bc)
                ins = e.tensor_sub(slot[:, dc * TC:(dc + 1) * TC], t1s, mr_bc)
                if dc == DC - 1:
                    dve.tick(ins)
        DVE(_)
        dve_chunk_end = dve.n
        gu_state["x2n_ready"] = dve.n

        if c >= 1:
            emit_down_part2out(c - 1)

    emit_gu_batch(NCH - 1, 0, MC)
    emit_down_part1(NCH - 1, gu_state[("gudone", NCH - 1)])
    emit_down_part1b(NCH - 1)
    emit_down_part2a(NCH - 1)
    emit_down_part2out(NCH - 1)

    def _(e):
        e.wait_ge(sem_out, outc.n)
    SY(_)

    es.close()
    return nc


_PROG = None


def _get_prog():
    global _PROG
    if _PROG is None:
        _PROG = build_program()
    return _PROG


def _prep_host(inputs):
    W_xdt = inputs["L0_attn_W_xdt"]
    W_dt = inputs["L0_attn_W_dt"]
    W_B = inputs["L0_attn_W_B"]
    W_C = inputs["L0_attn_W_C"]
    A_log = inputs["L0_attn_A_log"]
    b_dt = inputs["L0_attn_b_dt"]
    D = inputs["L0_attn_D"]
    wg = inputs["L0_ffn_w_gate"]
    wu = inputs["L0_ffn_w_up"]
    wd = inputs["L0_ffn_w_down"]

    def dchunk_cols(a2d, inner):
        return np.ascontiguousarray(
            a2d.reshape(DC, P, inner).transpose(1, 0, 2).reshape(P, DC * inner))

    wxdtT = dchunk_cols(np.ascontiguousarray(W_xdt.T), R).astype(np.float32)
    wdtT = np.ascontiguousarray(W_dt.T).astype(np.float32)
    wbT = dchunk_cols(np.ascontiguousarray(W_B.T), NST).astype(np.float32)
    wcT = dchunk_cols(np.ascontiguousarray(W_C.T), NST).astype(np.float32)
    alog_p = dchunk_cols(A_log, NST).astype(np.float32)
    bdt_p = np.ascontiguousarray(b_dt.reshape(DC, P).T).astype(np.float32)
    d_p = np.ascontiguousarray(D.reshape(DC, P).T).astype(np.float32)

    def prep_lhsT(aT, kc, mc):
        a4 = aT.reshape(kc, P, mc, P)
        return np.ascontiguousarray(
            a4.transpose(2, 1, 0, 3).reshape(mc, P, kc * P))

    wg_prep = prep_lhsT(np.ascontiguousarray(wg.T), DC, MC).astype(np.int32)
    wu_prep = prep_lhsT(np.ascontiguousarray(wu.T), DC, MC).astype(np.int32)
    wd_prep = prep_lhsT(np.ascontiguousarray(wd.T), MC, DC).astype(np.int32)

    return dict(
        wg_prep=wg_prep, wu_prep=wu_prep, wd_prep=wd_prep,
        wxdtT=wxdtT, wdtT=wdtT, wbT=wbT, wcT=wcT, alog_p=alog_p,
        bdt_p=bdt_p, d_p=d_p,
        bxdt=np.ascontiguousarray(
            inputs["L0_attn_b_xdt"].reshape(R, 1)).astype(np.float32),
        bb=np.ascontiguousarray(
            inputs["L0_attn_b_B"].reshape(NST, 1)).astype(np.float32),
        bc=np.ascontiguousarray(
            inputs["L0_attn_b_C"].reshape(NST, 1)).astype(np.float32),
        idf=np.eye(P, dtype=np.float32),
        ones_b=np.ones((P, 1), dtype=BF),
        ones_row=np.ones((1, P), dtype=BF),
    )


def kernel(**inputs):
    u = np.asarray(inputs["L0_attn_u"], dtype=np.float32)
    common = _prep_host({k: np.asarray(v) for k, v in inputs.items()})
    nc = _get_prog()
    in_maps = [dict(common, u=np.ascontiguousarray(u[b])) for b in range(8)]
    res = run_bass_kernel_spmd(nc, in_maps, core_ids=list(range(8)))
    return np.stack([np.asarray(r["y_out"], dtype=np.float32)
                     for r in res.results], axis=0)


# version 20
# speedup vs baseline: 1.0409x; 1.0095x over previous
import contextlib
import numpy as np
import ml_dtypes
import concourse.bass as bass
import concourse.mybir as mybir
from concourse.bass_utils import run_bass_kernel_spmd

F32 = mybir.dt.float32
BF16 = mybir.dt.bfloat16
I32 = mybir.dt.int32
AF = mybir.ActivationFunctionType
ALU = mybir.AluOpType
BF = ml_dtypes.bfloat16

P = 128
T = 2048
DM = 1024
DC = 8
NST = 16
HNST = NST // 2
R = 64
H = 3072
MC = 24
TC = 512
NCH = T // TC
FH = TC // 2
S127 = float(np.float32(1.0) / np.float32(127.0))
EPS = 1e-5


class Chain:
    def __init__(self, sem, step=1):
        self.sem, self.step, self.n = sem, step, 0

    def tick(self, ins):
        ins.then_inc(self.sem, self.step)
        self.n += self.step
        return self.n


def build_program():
    nc = bass.Bass("TRN2", target_bir_lowering=False)

    u_in = nc.dram_tensor("u", [T, DM], F32, kind="ExternalInput").ap()
    wg_in = nc.dram_tensor("wg_prep", [MC, P, DM], I32, kind="ExternalInput").ap()
    wu_in = nc.dram_tensor("wu_prep", [MC, P, DM], I32, kind="ExternalInput").ap()
    wd_in = nc.dram_tensor("wd_prep", [DC, P, H], I32, kind="ExternalInput").ap()
    wxdtT_in = nc.dram_tensor("wxdtT", [P, DC * R], F32, kind="ExternalInput").ap()
    wdtT_in = nc.dram_tensor("wdtT", [R, DM], F32, kind="ExternalInput").ap()
    wbT_in = nc.dram_tensor("wbT", [P, DC * NST], F32, kind="ExternalInput").ap()
    wcT_in = nc.dram_tensor("wcT", [P, DC * NST], F32, kind="ExternalInput").ap()
    alog_in = nc.dram_tensor("alog_p", [P, DC * NST], F32, kind="ExternalInput").ap()
    bdt_in = nc.dram_tensor("bdt_p", [P, DC], F32, kind="ExternalInput").ap()
    dsk_in = nc.dram_tensor("d_p", [P, DC], F32, kind="ExternalInput").ap()
    bxdt_in = nc.dram_tensor("bxdt", [R, 1], F32, kind="ExternalInput").ap()
    bb_in = nc.dram_tensor("bb", [NST, 1], F32, kind="ExternalInput").ap()
    bc_in = nc.dram_tensor("bc", [NST, 1], F32, kind="ExternalInput").ap()
    idf_in = nc.dram_tensor("idf", [P, P], F32, kind="ExternalInput").ap()
    onesb_in = nc.dram_tensor("ones_b", [P, 1], BF16, kind="ExternalInput").ap()
    onesrow_in = nc.dram_tensor("ones_row", [1, P], BF16, kind="ExternalInput").ap()

    wg_scr = nc.dram_tensor("wg_scr", [MC, P, DM], BF16, kind="Internal").ap()
    wu_scr = nc.dram_tensor("wu_scr", [MC, P, DM], BF16, kind="Internal").ap()
    wd_scr = nc.dram_tensor("wd_scr", [DC, P, H], BF16, kind="Internal").ap()
    hid_hbm = nc.dram_tensor("hid_scr", [MC, P, TC], BF16, kind="Internal").ap()
    y_out = nc.dram_tensor("y_out", [T, DM], F32, kind="ExternalOutput").ap()

    es = contextlib.ExitStack()

    def sb(name, shape, dtype):
        return es.enter_context(nc.sbuf_tensor(name, shape, dtype))

    def psum(name, shape, dtype=F32):
        return es.enter_context(nc.psum_tensor(name, shape, dtype))

    xT = sb("xT", [P, DC * T], F32)
    x2n = sb("x2n", [P, 2 * DC * TC], BF16)
    dtb = sb("dtb", [P, 2 * TC], BF16)
    unb = sb("unb", [P, DC * TC], BF16)
    mam = sb("mam", [P, (2 * NST + HNST) * TC], BF16)
    bbc = sb("bbc", [P, NST * TC], BF16)
    cbc = sb("cbc", [P, NST * TC], BF16)
    aux = sb("aux", [P, 2 * TC], BF16)
    rows = sb("rows", [1, 3 * TC], F32)
    rowb = sb("rowb", [1, TC], BF16)
    rbc = sb("rbc", [P, 2 * TC], BF16)
    smw = sb("smw", [P, 2 * DC * NST + DC * R], BF16)
    wdtTb = sb("wdtTb", [R, DM], BF16)
    xdtb = sb("xdtb", [R, TC], BF16)
    bscb = sb("bscb", [NST, 2 * TC], BF16)
    aneg = sb("aneg", [P, DC * NST], F32)
    bdtp = sb("bdtp", [P, DC], F32)
    dskp = sb("dskp", [P, DC], F32)
    bxdtc = sb("bxdtc", [R, 1], F32)
    bbcol = sb("bbcol", [NST, 1], F32)
    bccol = sb("bccol", [NST, 1], F32)
    idf = sb("idf_sb", [P, P], F32)
    onesb = sb("onesb_sb", [P, 1], BF16)
    onesrow = sb("onesrow_sb", [1, P], BF16)
    hlast = sb("hlast", [P, DC * NST], BF16)
    fix16 = sb("fix16", [P, NST], BF16)
    cst = sb("cst", [P, 2], F32)
    hidt = sb("hidt", [P, MC * FH], BF16)
    wgu = sb("wgu", [P, 4 * DM], BF16)
    sgub = sb("sgub", [P, 4 * TC], BF16)
    hidm = sb("hidm", [P, 2 * TC], BF16)
    wdm2 = wgu[:][:, 0:H]

    auxv = aux[:]
    t1s = auxv[:, 0:TC]
    etx = auxv[:, TC:2 * TC]

    smwv = smw[:]
    wbTb = smwv[:, 0:DC * NST]
    wcTb = smwv[:, DC * NST:2 * DC * NST]
    wxdtTb = smwv[:, 2 * DC * NST:2 * DC * NST + DC * R]

    mamx = mam[:]
    abar = mamx[:, 0:NST * TC]
    abar_h0 = mamx[:, 0:HNST * TC]
    abar_h1 = mamx[:, HNST * TC:NST * TC]
    xbuf = mamx[:, NST * TC:2 * NST * TC]
    hbuf = mamx[:, 2 * NST * TC:(2 * NST + HNST) * TC]
    rstd2_bc = mamx[:, HNST * TC:HNST * TC + TC]
    mr2_bc = mamx[:, HNST * TC + TC:HNST * TC + 2 * TC]
    uld = mam[:].bitcast(F32)[:, 0:2 * DM]

    def slotv(c):
        return x2n[:][:, (c % 2) * DC * TC:((c % 2) + 1) * DC * TC]


    ps_g = psum("ps_g", [P, TC])
    ps_u = psum("ps_u", [P, TC])
    ps_d = psum("ps_d", [P, TC])
    ps_s1 = psum("ps_s1", [P, TC])
    ps_s2 = psum("ps_s2", [P, TC])
    ps_t1f = psum("ps_t1", [P, TC])
    ps_t2f = psum("ps_t2", [P, TC])
    ps_p = psum("ps_p", [P, TC])
    ps_t1 = ps_t1f[:, 0:P]
    ps_t2 = ps_t2f[:, 0:P]

    def sem(name):
        return es.enter_context(nc.semaphore(name))

    sem_pe, sem_act, sem_dve, sem_po = sem("s_pe"), sem("s_act"), sem("s_dve"), sem("s_po")
    sem_sy, sem_gp = sem("s_sy"), sem("s_gp")
    sem_uld0, sem_uld1 = sem("s_uld0"), sem("s_uld1")
    sem_bbc, sem_cbc = sem("s_bbc"), sem("s_cbc")
    sem_wc0, sem_wc1 = sem("s_wc0"), sem("s_wc1")
    sem_wsc = sem("s_wsc")
    sem_wv0, sem_wv1 = sem("s_wv0"), sem("s_wv1")
    sem_wd = sem("s_wd")
    sem_wd2 = sem("s_wd2")
    sem_sgub = sem("s_sgub")
    sem_hidt = sem("s_hidt")
    sem_out = sem("s_out")
    blk = es.enter_context(nc.Block())

    pe, act, dve, po = Chain(sem_pe), Chain(sem_act), Chain(sem_dve), Chain(sem_po)
    sy, gp = Chain(sem_sy, 16), Chain(sem_gp, 16)
    uldc = [Chain(sem_uld0, 16), Chain(sem_uld1, 16)]
    bbcc, cbcc = Chain(sem_bbc, 16), Chain(sem_cbc, 16)
    wcc = [Chain(sem_wc0, 16), Chain(sem_wc1, 16)]
    wscc = Chain(sem_wsc, 16)
    wvc = [Chain(sem_wv0, 16), Chain(sem_wv1, 16)]
    wdc = Chain(sem_wd, 16)
    wdc2 = Chain(sem_wd2, 16)
    sguc = Chain(sem_sgub, 16)
    hidtc = Chain(sem_hidt, 16)
    outc = Chain(sem_out, 16)

    TEN, ACT, DVE, SY, GP = (blk.tensor, blk.scalar, blk.vector, blk.sync,
                             blk.gpsimd)
    POOL = blk.gpsimd

    def load_act_set(e, set_id):
        e.add_instruction(mybir.InstLoadActFuncSet(
            name=nc.get_next_instruction_name(), act_func_set_id=set_id,
            ins=[], outs=[]))

    psum_free = {}

    def pwait(e, name):
        ch_m = psum_free.get(name)
        if ch_m:
            e.wait_ge(ch_m[0].sem, ch_m[1])

    def pfree(name, chain):
        psum_free[name] = (chain, chain.n)

    rows_s = rows[:][:, 0:TC]
    rows_q = rows[:][:, TC:2 * TC]
    rows_a1 = rows[:][:, 2 * TC:3 * TC]
    rows_a2 = rows_q
    rowb_1 = rowb[:][:, 0:TC]
    rowb2 = hidm[:].bitcast(BF16)[0:1, TC:2 * TC]
    rstd_bc = rbc[:][:, 0:TC]
    mr_bc = rbc[:][:, TC:2 * TC]
    cst_eps = cst[:][:, 0:1]
    cst_one = cst[:][:, 1:2]

    def _(e):
        for ap_out, ap_in in [(idf[:], idf_in), (onesb[:], onesb_in),
                              (onesrow[:], onesrow_in), (bdtp[:], bdt_in),
                              (dskp[:], dsk_in), (bxdtc[:], bxdt_in),
                              (bbcol[:], bb_in), (bccol[:], bc_in)]:
            sy.tick(e.dma_start(out=ap_out, in_=ap_in))
    SY(_)
    sy_setup = sy.n

    def _(e):
        gp.tick(e.dma_start(out=wbTb, in_=wbT_in))
        gp.tick(e.dma_start(out=wcTb, in_=wcT_in))
        gp.tick(e.dma_start(out=wxdtTb, in_=wxdtT_in))
        gp.tick(e.dma_start(out=wdtTb[:], in_=wdtT_in))
        gp.tick(e.dma_start(out=aneg[:], in_=alog_in))
    GP(_)
    gp_small = gp.n

    def _(e):
        load_act_set(e, 6)
        e.wait_ge(sem_gp, gp_small)
        act.tick(e.activation(out=aneg[:], in_=aneg[:], func=AF.Exp))
    ACT(_)
    act_aexp = act.n

    def _(e):
        e.wait_ge(sem_act, act_aexp)
        e.memset(cst_eps, EPS)
        e.memset(cst_one, 1.0)
        dve.tick(e.tensor_scalar_mul(aneg[:], aneg[:], -1.0))
    DVE(_)

    u_v = u_in.rearrange("(tt p) d -> p tt d", p=P)
    pe_tile_marks = {}
    act_u_chunk = {}
    uld2 = wgu[:].bitcast(F32)

    def emit_u_tile(tt):
        par = tt % 2
        base = uld if tt < 4 else uld2
        buf = base[:, par * DM:(par + 1) * DM]
        uch = uldc[par]

        def _(e, tt=tt, buf=buf, uch=uch):
            e.wait_ge(sem_sy, sy_setup)
            m = pe_tile_marks.get(tt - 2, 0)
            if m:
                e.wait_ge(sem_pe, m)
            uch.tick(e.dma_start(out=buf, in_=u_v[:, tt, :]))
        SY(_)
        u_mark = uch.n
        act_base = act.n

        def _(e, tt=tt, buf=buf, u_mark=u_mark, uch=uch, act_base=act_base):
            e.wait_ge(uch.sem, u_mark)
            if tt > 0:
                e.wait_ge(sem_act, act_base)
            for dc in range(DC):
                pst = ps_t1 if dc % 2 == 0 else ps_t2
                if dc >= 2:
                    e.wait_ge(sem_act, act_base + dc - 1)
                pe.tick(e.transpose(pst[:], buf[:, dc * P:(dc + 1) * P], idf[:]))
        TEN(_)
        pe_tile_marks[tt] = pe.n
        pe_base = pe.n - DC

        def _(e, tt=tt, pe_base=pe_base):
            for dc in range(DC):
                pst = ps_t1 if dc % 2 == 0 else ps_t2
                e.wait_ge(sem_pe, pe_base + dc + 1)
                act.tick(e.copy(
                    out=xT[:][:, dc * T + tt * P:dc * T + (tt + 1) * P],
                    in_=pst[:]))
        ACT(_)
        if tt % 4 == 3:
            act_u_chunk[tt // 4] = act.n
        pfree("ps_t1", act)
        pfree("ps_t2", act)

    for tt in range(4):
        emit_u_tile(tt)

    def emit_stats_rows_bcast(src_getter, extra_marks, src_bf16=False):
        pe_slot = {}
        for i in range(DC):
            src = src_getter(i)
            qslot = dtb[:][:, (i % 2) * TC:(i % 2 + 1) * TC]

            def _(e, i=i, src=src, qslot=qslot):
                if i == 0:
                    for s_, v in extra_marks:
                        if v > 0:
                            e.wait_ge(s_, v)
                w = pe_slot.get(i - 2 if src_bf16 else i - 1)
                if w:
                    e.wait_ge(sem_pe, w)
                if not src_bf16:
                    e.activation(out=etx, in_=src, func=AF.Copy)
                act.tick(e.activation(out=qslot, in_=src, func=AF.Square))
            ACT(_)
            a_i = act.n

            def _(e, i=i, src=src, qslot=qslot, a_i=a_i):
                e.wait_ge(sem_act, a_i)
                if i == 0:
                    pwait(e, "ps_s1")
                    pwait(e, "ps_s2")
                e.matmul(ps_s1[0:1, :], onesb[:], src if src_bf16 else etx,
                         start=(i == 0), stop=(i == DC - 1))
                pe.tick(e.matmul(ps_s2[0:1, :], onesb[:], qslot,
                                 start=(i == 0), stop=(i == DC - 1)))
            TEN(_)
            pe_slot[i] = pe.n
        pe_stats = pe.n

        def _(e, pe_stats=pe_stats):
            e.wait_ge(sem_pe, pe_stats)
            e.copy(out=rows_s, in_=ps_s1[0:1, :])
            e.copy(out=rows_q, in_=ps_s2[0:1, :])
            e.mul(out=rows_s, in_=rows_s, mul=1.0 / DM)
            e.mul(out=rows_q, in_=rows_q, mul=1.0 / DM)
            act.tick(e.activation(out=rows_a1, in_=rows_s, func=AF.Square))
        ACT(_)
        a_m = act.n

        def _(e, a_m=a_m):
            e.wait_ge(sem_act, a_m)
            dve.tick(e.tensor_sub(rows_a2, rows_q, rows_a1))
        DVE(_)
        d_v = dve.n

        def _(e, d_v=d_v):
            e.wait_ge(sem_dve, d_v)
            e.activation(out=rows_a1, in_=rows_a2, func=AF.Ln,
                         bias=cst_eps[0:1, :])
            act.tick(e.activation(out=rows_a1, in_=rows_a1, func=AF.Exp,
                                  scale=-0.5))
        ACT(_)
        a_r = act.n

        def _(e, a_r=a_r):
            e.wait_ge(sem_act, a_r)
            dve.tick(e.tensor_mul(rows_a2, rows_s, rows_a1))
        DVE(_)
        d_mr = dve.n
        dve_now = dve.n

        def _(e, d_mr=d_mr):
            e.wait_ge(sem_dve, d_mr)
            e.wait_ge(sem_sgub, sguc.n)
            e.activation(out=rowb_1, in_=rows_a1, func=AF.Copy)
            act.tick(e.activation(out=rowb2, in_=rows_a2, func=AF.Copy))
        ACT(_)
        a_rb = act.n

        def _(e, a_rb=a_rb):
            e.wait_ge(sem_act, a_rb)
            pwait(e, "ps_p")
            pwait(e, "ps_s2")
            e.matmul(ps_p[:], onesrow[:], rowb_1, start=True, stop=True)
            pe.tick(e.matmul(ps_s2[:], onesrow[:], rowb2, start=True,
                             stop=True))
        TEN(_)
        pe_b1 = pe.n

        def _(e, pe_b1=pe_b1, dve_now=dve_now):
            e.wait_ge(sem_pe, pe_b1)
            e.wait_ge(sem_dve, dve_now)
            e.copy(out=rstd_bc, in_=ps_p[:])
            act.tick(e.copy(out=mr_bc, in_=ps_s2[:]))
        ACT(_)
        pfree("ps_p", act)
        pfree("ps_s2", act)
        return act.n

    gu_state = {"x2n_ready": 0}
    DNPS = [(ps_g, "ps_g"), (ps_u, "ps_u"), (ps_t1f, "ps_t1"),
            (ps_t2f, "ps_t2")]

    def emit_gu_batch(wc, m0, m1):
        for m in range(m0, m1):
            par = m % 2
            wch = wvc[par]
            wg_sb = wgu[:][:, par * 2 * DM:par * 2 * DM + DM]
            wu_sb = wgu[:][:, par * 2 * DM + DM:(par + 1) * 2 * DM]
            psg, psg_n = (ps_g, "ps_g") if par == 0 else (ps_t1f, "ps_t1")
            psu, psu_n = (ps_u, "ps_u") if par == 0 else (ps_t2f, "ps_t2")
            sg_sb = sgub[:][:, par * 2 * TC:par * 2 * TC + TC]
            ub_sb = sgub[:][:, par * 2 * TC + TC:(par + 1) * 2 * TC]

            pre = gu_state.get(("wvmark", wc, m))
            if pre is None:
                def _(e, m=m, wc=wc, wch=wch, wg_sb=wg_sb, wu_sb=wu_sb):
                    e.wait_ge(sem_wsc, wsc_all)
                    if wc == 0 and m < 2:
                        e.wait_ge(sem_act, act_u_chunk[3])
                    mm = gu_state.get(("pe_w", m - 2))
                    if mm:
                        e.wait_ge(sem_pe, mm)
                    mm2 = gu_state.get("down_done")
                    if mm2 and m < 2:
                        e.wait_ge(sem_pe, mm2)
                    wch.tick(e.dma_start(out=wg_sb, in_=wg_scr[m]))
                    wch.tick(e.dma_start(out=wu_sb, in_=wu_scr[m]))
                SY(_)
                wv_mark = wch.n
            else:
                wv_mark = pre

            def _(e, m=m, wc=wc, wch=wch, wv_mark=wv_mark, wg_sb=wg_sb,
                  wu_sb=wu_sb, psg=psg, psu=psu, psg_n=psg_n, psu_n=psu_n):
                e.wait_ge(wch.sem, wv_mark)
                e.wait_ge(sem_dve, gu_state["x2n_ready"])
                pwait(e, psg_n)
                pwait(e, psu_n)
                sl = slotv(wc)
                for k in range(DC):
                    e.matmul(psg[:], wg_sb[:, k * P:(k + 1) * P],
                             sl[:, k * TC:(k + 1) * TC],
                             start=(k == 0), stop=(k == DC - 1))
                for k in range(DC):
                    ins = e.matmul(psu[:], wu_sb[:, k * P:(k + 1) * P],
                                   sl[:, k * TC:(k + 1) * TC],
                                   start=(k == 0), stop=(k == DC - 1))
                    if k == DC - 1:
                        pe.tick(ins)
            TEN(_)
            gu_state[("pe_w", m)] = pe.n
            pe_gu = pe.n

            def _(e, m=m, m0=m0, m1=m1, wc=wc, pe_gu=pe_gu, psg=psg, psu=psu,
                  sg_sb=sg_sb, ub_sb=ub_sb):
                if m == m0:
                    load_act_set(e, 18)
                e.wait_ge(sem_pe, pe_gu)
                if m < 2:
                    oo = gu_state.get(("otok_done", 0 if wc < 1 else wc - 1), 0) \
                        if wc >= 1 else 0
                    if oo:
                        e.wait_ge(sem_out, oo)
                mm = gu_state.get(("hidtt", m - 2))
                if mm:
                    e.wait_ge(sem_po, mm)
                e.activation(out=sg_sb, in_=psg[:], func=AF.Silu, scale=S127)
                act.tick(e.activation(out=ub_sb, in_=psu[:], func=AF.Copy,
                                      scale=S127))
                if m == m1 - 1:
                    load_act_set(e, 6)
            ACT(_)
            pfree(psg_n, act)
            pfree(psu_n, act)
            a_gu = act.n

            def _(e, m=m, a_gu=a_gu, sg_sb=sg_sb, ub_sb=ub_sb):
                e.wait_ge(sem_act, a_gu)
                mm = gu_state.get(("hidst", m - 2))
                if mm:
                    e.wait_ge(sem_sgub, mm)
                par = m % 2
                po.tick(e.tensor_mul(hidm[:][:, par * TC:(par + 1) * TC],
                                     sg_sb, ub_sb))
            POOL(_)
            gu_state[("hidtt", m)] = po.n
            po_h = po.n

            def _(e, m=m, po_h=po_h):
                e.wait_ge(sem_po, po_h)
                par = m % 2
                sguc.tick(e.dma_start(out=hid_hbm[m],
                                      in_=hidm[:][:, par * TC:(par + 1) * TC]))
            GP(_)
            gu_state[("sgst", m)] = sguc.n
            gu_state[("hidst", m)] = sguc.n
        gu_state[("gudone", wc)] = pe.n

    def emit_down_mgroup(dn, fh, m2):
        wc = dn["wc"]
        dwnb, sq1 = dn["dwnb"], dn["sq1"]
        sy_hidt = dn["sy_hidt"]
        half_marks, st_marks, a_dn_marks = (dn["half_marks"], dn["st_marks"],
                                            dn["a_dn_marks"])
        psd, psd_n = DNPS[m2 % 4]
        HC = MC // 2
        HCW = HC * P
        for h in range(2):
            wch2 = wdc if h == 0 else wdc2
            prev = half_marks.get((fh, m2 - 1, h))
            pe_now = pe.n

            def _(e, m2=m2, fh=fh, h=h, wch2=wch2, prev=prev, pe_now=pe_now):
                if prev is not None:
                    e.wait_ge(sem_pe, prev)
                elif fh > 0 or gu_state.get("wd_used"):
                    e.wait_ge(sem_pe, pe_now)
                wch2.tick(e.dma_start(
                    out=wdm2[:, h * HCW:(h + 1) * HCW],
                    in_=wd_scr[m2][:, h * HCW:(h + 1) * HCW]))
            GP(_)
        gu_state["wd_used"] = True
        for h in range(2):
            wch2 = wdc if h == 0 else wdc2
            gp_wd = wch2.n

            def _(e, m2=m2, fh=fh, h=h, sy_hidt=sy_hidt, gp_wd=gp_wd,
                  psd=psd, psd_n=psd_n, wch2=wch2):
                if h == 0:
                    e.wait_ge(sem_hidt, sy_hidt)
                    pwait(e, psd_n)
                e.wait_ge(wch2.sem, gp_wd)
                for k2 in range(HC):
                    kk = h * HC + k2
                    ins = e.matmul(psd[:, 0:FH],
                                   wdm2[:, kk * P:(kk + 1) * P],
                                   hidt[:][:, kk * FH:(kk + 1) * FH],
                                   start=(kk == 0), stop=(kk == MC - 1))
                    if k2 == HC - 1:
                        pe.tick(ins)
            TEN(_)
            half_marks[(fh, m2, h)] = pe.n
        pe_dn = pe.n
        if m2 >= 2:
            emit_dn_stats(dn, fh, m2 - 2)
        st_guard = st_marks.get((fh, m2 - 2), 0)
        po_guard = dn["po_guard"]
        sg_done = dn["sg_done"]

        def _(e, m2=m2, fh=fh, pe_dn=pe_dn, psd=psd,
              po_guard=po_guard, st_guard=st_guard, sg_done=sg_done):
            e.wait_ge(sem_pe, pe_dn)
            if m2 == 0 and fh == 0:
                if po_guard:
                    e.wait_ge(sem_pe, po_guard)
                e.wait_ge(sem_sgub, sg_done)
            if st_guard:
                e.wait_ge(sem_pe, st_guard)
            dsl = dwnb[:, m2 * TC + fh * FH:m2 * TC + fh * FH + FH]
            e.activation(out=dsl, in_=psd[:, 0:FH], func=AF.Copy,
                         scale=S127)
            act.tick(e.activation(
                out=sq1[:, (m2 % 2) * FH:(m2 % 2 + 1) * FH],
                in_=psd[:, 0:FH], func=AF.Square, scale=S127))
        ACT(_)
        pfree(psd_n, act)
        a_dn_marks[(fh, m2)] = act.n

    def emit_dn_stats(dn, fh, k):
        dwnb, sq1 = dn["dwnb"], dn["sq1"]

        def _(e, k=k, fh=fh, a_dn=dn["a_dn_marks"][(fh, k)]):
            e.wait_ge(sem_act, a_dn)
            if k == 0 and fh == 0:
                pwait(e, "ps_s1")
                pwait(e, "ps_s2")
            e.matmul(ps_s1[0:1, fh * FH:(fh + 1) * FH], onesb[:],
                     dwnb[:, k * TC + fh * FH:k * TC + fh * FH + FH],
                     start=(k == 0), stop=(k == DC - 1))
            pe.tick(e.matmul(
                ps_s2[0:1, fh * FH:(fh + 1) * FH], onesb[:],
                sq1[:, (k % 2) * FH:(k % 2 + 1) * FH],
                start=(k == 0), stop=(k == DC - 1)))
        TEN(_)
        dn["st_marks"][(fh, k)] = pe.n

    def emit_dn_hidt(dn, fh):
        sg_done = dn["sg_done"]
        pe_hidt_guard = pe.n if fh == 1 else max(
            gu_state.get("down_done", 0),
            gu_state.get(("midstats_pe", dn["wc"] + 2), 0))

        def _(e, fh=fh, sg_done=sg_done, pe_hidt_guard=pe_hidt_guard):
            e.wait_ge(sem_sgub, sg_done)
            if pe_hidt_guard:
                e.wait_ge(sem_pe, pe_hidt_guard)
            hidtc.tick(e.dma_start(
                out=hidt[:].rearrange("p (a t) -> p a t", t=FH),
                in_=hid_hbm[:, :, fh * FH:(fh + 1) * FH].transpose([1, 0, 2])))
        GP(_)
        dn["sy_hidt"] = hidtc.n

    def emit_down_part1(wc, po_guard):
        dn = {"wc": wc, "po_guard": po_guard,
              "dwnb": slotv(wc), "sq1": hidm[:],
              "sg_done": gu_state[("sgst", MC - 1)],
              "half_marks": {}, "st_marks": {}, "a_dn_marks": {}}
        gu_state[("dn", wc)] = dn
        emit_dn_hidt(dn, 0)
        for m2 in range(4):
            emit_down_mgroup(dn, 0, m2)

    def emit_down_part1b(wc):
        dn = gu_state[("dn", wc)]
        for m2 in range(4, DC):
            emit_down_mgroup(dn, 0, m2)
        emit_dn_stats(dn, 0, DC - 2)
        emit_dn_stats(dn, 0, DC - 1)
        emit_dn_hidt(dn, 1)

    def emit_down_part2a_pre(wc):
        dn = gu_state[("dn", wc)]
        for m2 in range(2):
            emit_down_mgroup(dn, 1, m2)

    def emit_down_part2a(wc):
        dn = gu_state[("dn", wc)]
        dwnb = dn["dwnb"]
        for m2 in range(2, DC):
            emit_down_mgroup(dn, 1, m2)
        emit_dn_stats(dn, 1, DC - 2)
        emit_dn_stats(dn, 1, DC - 1)
        gu_state["down_done"] = pe.n
        gu_state[("ddone", wc)] = pe.n
        nwc = wc + 1
        if nwc < NCH:
            for m in (0, 1):
                par = m % 2
                wch = wvc[par]
                wg_sb = wgu[:][:, par * 2 * DM:par * 2 * DM + DM]
                wu_sb = wgu[:][:, par * 2 * DM + DM:(par + 1) * 2 * DM]

                def _(e, m=m, wch=wch, wg_sb=wg_sb, wu_sb=wu_sb,
                      dd=gu_state["down_done"]):
                    e.wait_ge(sem_wsc, wsc_all)
                    e.wait_ge(sem_pe, dd)
                    wch.tick(e.dma_start(out=wg_sb, in_=wg_scr[m]))
                    wch.tick(e.dma_start(out=wu_sb, in_=wu_scr[m]))
                SY(_)
                gu_state[("wvmark", nwc, m)] = wch.n
        pe_stats_dn = pe.n
        def _(e, pe_stats_dn=pe_stats_dn):
            e.wait_ge(sem_pe, pe_stats_dn)
            e.copy(out=rows_s, in_=ps_s1[0:1, :])
            e.copy(out=rows_q, in_=ps_s2[0:1, :])
            e.mul(out=rows_s, in_=rows_s, mul=1.0 / DM)
            e.mul(out=rows_q, in_=rows_q, mul=1.0 / DM)
            act.tick(e.activation(out=rows_a1, in_=rows_s, func=AF.Square))
        ACT(_)
        a_m4 = act.n

        def _(e, a_m4=a_m4):
            e.wait_ge(sem_act, a_m4)
            dve.tick(e.tensor_sub(rows_a2, rows_q, rows_a1))
        DVE(_)
        d_v4 = dve.n

        def _(e, d_v4=d_v4):
            e.wait_ge(sem_dve, d_v4)
            e.activation(out=rows_a1, in_=rows_a2, func=AF.Ln,
                         bias=cst_eps[0:1, :])
            act.tick(e.activation(out=rows_a1, in_=rows_a1, func=AF.Exp,
                                  scale=-0.5))
        ACT(_)
        a_r4 = act.n

        def _(e, a_r4=a_r4):
            e.wait_ge(sem_act, a_r4)
            dve.tick(e.tensor_mul(rows_a2, rows_s, rows_a1))
        DVE(_)
        d_mr4 = dve.n
        dve_now4 = dve.n

        def _(e, d_mr4=d_mr4):
            e.wait_ge(sem_dve, d_mr4)
            e.wait_ge(sem_sgub, sguc.n)
            e.activation(out=rowb_1, in_=rows_a1, func=AF.Copy)
            act.tick(e.activation(out=rowb2, in_=rows_a2, func=AF.Copy))
        ACT(_)
        a_rb4 = act.n

        def _(e, a_rb4=a_rb4):
            e.wait_ge(sem_act, a_rb4)
            pwait(e, "ps_p")
            pwait(e, "ps_s2")
            e.matmul(ps_p[:], onesrow[:], rowb_1, start=True, stop=True)
            pe.tick(e.matmul(ps_s2[:], onesrow[:], rowb2, start=True,
                             stop=True))
        TEN(_)
        pe_b14 = pe.n

        def _(e, pe_b14=pe_b14, dve_now4=dve_now4):
            e.wait_ge(sem_pe, pe_b14)
            e.wait_ge(sem_dve, dve_now4)
            e.copy(out=rstd_bc, in_=ps_p[:])
            act.tick(e.copy(out=mr_bc, in_=ps_s2[:]))
        ACT(_)
        pfree("ps_p", act)
        pfree("ps_s2", act)
        bc4 = act.n

        def _(e, bc4=bc4, wc=wc):
            e.wait_ge(sem_act, bc4)
            for dc in range(DC):
                e.tensor_mul(t1s, dwnb[:, dc * TC:(dc + 1) * TC], rstd_bc)
                e.tensor_sub(t1s, t1s, mr_bc)
                xsl = xT[:][:, dc * T + wc * TC:dc * T + (wc + 1) * TC]
                ins = e.tensor_tensor(out=xsl, in0=xsl, in1=t1s, op=ALU.add)
                if dc == DC - 1:
                    dve.tick(ins)
        DVE(_)
        dve_x3 = dve.n
        gu_state["x3_dve"] = dve_x3


    def emit_down_part2out(wc):
        dn = gu_state[("dn", wc)]
        dve_x3 = gu_state["x3_dve"]
        otok = sgub[:].bitcast(F32)[:, 0:DM]
        for tt in range(TC // P):
            act_base = act.n
            sy_now = outc.n

            def _(e, tt=tt, wc=wc, dve_x3=dve_x3, act_base=act_base):
                e.wait_ge(sem_dve, dve_x3)
                pwait(e, "ps_t1")
                pwait(e, "ps_t2")
                if tt > 0:
                    e.wait_ge(sem_act, act_base)
                for dc in range(DC):
                    pst = ps_t1 if dc % 2 == 0 else ps_t2
                    if dc >= 2:
                        e.wait_ge(sem_act, act_base + dc - 1)
                    pe.tick(e.transpose(
                        pst[:],
                        xT[:][:, dc * T + wc * TC + tt * P:
                              dc * T + wc * TC + (tt + 1) * P],
                        idf[:]))
            TEN(_)
            pe_base = pe.n - DC

            def _(e, tt=tt, pe_base=pe_base, sy_now=sy_now, otok=otok,
                  tt_guard=gu_state[("hidtt", MC - 1)]):
                for dc in range(DC):
                    pst = ps_t1 if dc % 2 == 0 else ps_t2
                    e.wait_ge(sem_pe, pe_base + dc + 1)
                    if dc == 0:
                        e.wait_ge(sem_out, sy_now)
                        if tt == 0:
                            e.wait_ge(sem_po, tt_guard)
                    act.tick(e.copy(out=otok[:, dc * P:(dc + 1) * P],
                                    in_=pst[:]))
            ACT(_)
            a_ot = act.n

            def _(e, tt=tt, wc=wc, a_ot=a_ot, otok=otok):
                e.wait_ge(sem_act, a_ot)
                outc.tick(e.dma_start(
                    out=y_out[wc * TC + tt * P:wc * TC + (tt + 1) * P, :],
                    in_=otok))
            SY(_)
        pfree("ps_t1", act)
        pfree("ps_t2", act)
        gu_state[("otok_done", wc)] = outc.n

    head_state = {}
    rbs = bscb[:][0:1, 0:2 * TC]

    def emit_head_stats_mid(cn):
        hA = [hidt[:][:, 0:TC], hidt[:][:, TC:2 * TC]]
        hB = [hidt[:][:, 2 * TC:3 * TC], hidt[:][:, 3 * TC:4 * TC]]
        c0 = cn * TC
        dd_guard = gu_state.get(("ddone", cn - 3), 0)
        pe_slot = {}
        for i in range(DC):
            src = xT[:][:, i * T + c0:i * T + c0 + TC]

            def _(e, i=i, src=src, dd_guard=dd_guard):
                if i == 0:
                    e.wait_ge(sem_act, act_u_chunk[cn])
                    e.wait_ge(sem_wsc, wsc_all)
                    if dd_guard:
                        e.wait_ge(sem_pe, dd_guard)
                w = pe_slot.get(i - 2)
                if w:
                    e.wait_ge(sem_pe, w)
                e.activation(out=hA[i % 2], in_=src, func=AF.Copy)
                act.tick(e.activation(out=hB[i % 2], in_=src, func=AF.Square))
            ACT(_)
            a_i = act.n

            def _(e, i=i, a_i=a_i):
                e.wait_ge(sem_act, a_i)
                if i == 0:
                    pwait(e, "ps_s1")
                    pwait(e, "ps_s2")
                e.matmul(ps_s1[0:1, :], onesb[:], hA[i % 2],
                         start=(i == 0), stop=(i == DC - 1))
                pe.tick(e.matmul(ps_s2[0:1, :], onesb[:], hB[i % 2],
                                 start=(i == 0), stop=(i == DC - 1)))
            TEN(_)
            pe_slot[i] = pe.n
        pe_stats = pe.n
        gu_state[("midstats_pe", cn)] = pe.n

        def _(e, pe_stats=pe_stats):
            e.wait_ge(sem_pe, pe_stats)
            e.copy(out=rows_s, in_=ps_s1[0:1, :])
            e.copy(out=rows_q, in_=ps_s2[0:1, :])
            e.mul(out=rows_s, in_=rows_s, mul=1.0 / DM)
            e.mul(out=rows_q, in_=rows_q, mul=1.0 / DM)
            act.tick(e.activation(out=rows_a1, in_=rows_s, func=AF.Square))
        ACT(_)
        pfree("ps_s1", act)
        pfree("ps_s2", act)
        a_m = act.n

        def _(e, a_m=a_m):
            e.wait_ge(sem_act, a_m)
            dve.tick(e.tensor_sub(rows_a2, rows_q, rows_a1))
        DVE(_)
        d_v = dve.n

        def _(e, d_v=d_v):
            e.wait_ge(sem_dve, d_v)
            e.activation(out=rows_a1, in_=rows_a2, func=AF.Ln,
                         bias=cst_eps[0:1, :])
            act.tick(e.activation(out=rows_a1, in_=rows_a1, func=AF.Exp,
                                  scale=-0.5))
        ACT(_)
        a_r = act.n

        def _(e, a_r=a_r):
            e.wait_ge(sem_act, a_r)
            dve.tick(e.tensor_mul(rows_a2, rows_s, rows_a1))
        DVE(_)
        d_mr = dve.n
        bb_g = head_state.get((cn - 1, "gp_bbc"), 0)
        cb_g = head_state.get((cn - 1, "sy_cbc"), 0)

        def _(e, d_mr=d_mr, bb_g=bb_g, cb_g=cb_g):
            e.wait_ge(sem_dve, d_mr)
            if bb_g:
                e.wait_ge(sem_bbc, bb_g)
            if cb_g:
                e.wait_ge(sem_cbc, cb_g)
            e.activation(out=rbs[:, 0:TC], in_=rows_a1, func=AF.Copy)
            act.tick(e.activation(out=rbs[:, TC:2 * TC], in_=rows_a2,
                                  func=AF.Copy))
        ACT(_)
        gu_state[("midrows_act", cn)] = act.n

    def emit_head_bcast(cn, scan_mark):
        mid_a = gu_state[("midrows_act", cn)]

        def _(e, mid_a=mid_a, scan_mark=scan_mark):
            e.wait_ge(sem_act, mid_a)
            e.wait_ge(sem_dve, scan_mark)
            pwait(e, "ps_p")
            pwait(e, "ps_d")
            e.matmul(ps_p[:], onesrow[:], rbs[:, 0:TC], start=True, stop=True)
            pe.tick(e.matmul(ps_d[:], onesrow[:], rbs[:, TC:2 * TC],
                             start=True, stop=True))
        TEN(_)
        pe_b = pe.n

        def _(e, pe_b=pe_b):
            e.wait_ge(sem_pe, pe_b)
            e.copy(out=rstd2_bc, in_=ps_p[:])
            act.tick(e.copy(out=mr2_bc, in_=ps_d[:]))
        ACT(_)
        pfree("ps_p", act)
        pfree("ps_d", act)
        bc1 = act.n
        c0 = cn * TC

        def _(e, bc1=bc1, c0=c0):
            e.wait_ge(sem_act, bc1)
            for dc in range(DC):
                e.tensor_mul(t1s, xT[:][:, dc * T + c0:dc * T + c0 + TC],
                             rstd2_bc)
                ins = e.tensor_sub(unb[:][:, dc * TC:(dc + 1) * TC],
                                   t1s, mr2_bc)
                if dc == DC - 1:
                    dve.tick(ins)
        DVE(_)
        head_state[(cn, "dve_unorm")] = dve.n


    def emit_head_stats(c, dve_chunk_end):
        c0 = c * TC
        bc1 = emit_stats_rows_bcast(
            lambda i: xT[:][:, i * T + c0:i * T + c0 + TC],
            [(sem_act, act_u_chunk[c]), (sem_dve, dve_chunk_end)])

        def _(e, bc1=bc1, c0=c0):
            e.wait_ge(sem_act, bc1)
            for dc in range(DC):
                e.tensor_mul(t1s, xT[:][:, dc * T + c0:dc * T + c0 + TC],
                             rstd_bc)
                ins = e.tensor_sub(unb[:][:, dc * TC:(dc + 1) * TC],
                                   t1s, mr_bc)
                if dc == DC - 1:
                    dve.tick(ins)
        DVE(_)
        head_state[(c, "dve_unorm")] = dve.n

    def emit_head_proj(c, dve_chunk_end):
        dve_unorm = head_state[(c, "dve_unorm")]
        act_now = act.n

        def _(e, dve_unorm=dve_unorm, act_now=act_now):
            e.wait_ge(sem_dve, dve_unorm)
            e.wait_ge(sem_gp, gp_small)
            e.wait_ge(sem_act, act_now)
            pwait(e, "ps_p")
            for dc in range(DC):
                ins = e.matmul(ps_p[0:R, :], wxdtTb[:, dc * R:(dc + 1) * R],
                               unb[:][:, dc * TC:(dc + 1) * TC],
                               start=(dc == 0), stop=(dc == DC - 1))
                if dc == DC - 1:
                    pe.tick(ins)
        TEN(_)
        pe_xdt = pe.n

        def _(e, pe_xdt=pe_xdt):
            e.wait_ge(sem_pe, pe_xdt)
            act.tick(e.activation(out=xdtb[:], in_=ps_p[0:R, :],
                                  func=AF.Identity, bias=bxdtc[:]))
        ACT(_)
        pfree("ps_p", act)
        head_state[(c, "act_xdt")] = act.n

        def _(e, dve_unorm=dve_unorm):
            e.wait_ge(sem_dve, dve_unorm)
            pwait(e, "ps_p")
            pwait(e, "ps_d")
            for dc in range(DC):
                e.matmul(ps_p[0:NST, :], wbTb[:, dc * NST:(dc + 1) * NST],
                         unb[:][:, dc * TC:(dc + 1) * TC],
                         start=(dc == 0), stop=(dc == DC - 1))
            for dc in range(DC):
                ins = e.matmul(ps_d[0:NST, :], wcTb[:, dc * NST:(dc + 1) * NST],
                               unb[:][:, dc * TC:(dc + 1) * TC],
                               start=(dc == 0), stop=(dc == DC - 1))
                if dc == DC - 1:
                    pe.tick(ins)
        TEN(_)
        pe_bcproj = pe.n

        def _(e, pe_bcproj=pe_bcproj):
            e.wait_ge(sem_pe, pe_bcproj)
            e.activation(out=bscb[:, 0:TC], in_=ps_p[0:NST, :],
                         func=AF.Identity, bias=bbcol[:])
            act.tick(e.activation(out=bscb[:, TC:2 * TC], in_=ps_d[0:NST, :],
                                  func=AF.Identity, bias=bccol[:]))
        ACT(_)
        pfree("ps_p", act)
        pfree("ps_d", act)
        act_bc = act.n

        dve_prev_reader = dve_chunk_end if c > 0 else 0

        def _(e, act_bc=act_bc, dve_prev_reader=dve_prev_reader):
            e.wait_ge(sem_act, act_bc)
            if dve_prev_reader:
                e.wait_ge(sem_dve, dve_prev_reader)
            bbcc.tick(e.dma_start(
                out=bbc[:][0:1, :].rearrange("o (n t) -> o n t", t=TC),
                in_=bscb[0:NST, 0:TC]))
            k = 1
            while k < P:
                kk = min(k, P - k)
                e.wait_ge(sem_bbc, bbcc.n)
                bbcc.tick(e.dma_start(out=bbc[:][k:k + kk, :],
                                      in_=bbc[:][0:kk, :]))
                k += kk
        SY(_)
        head_state[(c, "gp_bbc")] = bbcc.n

        def _(e, act_bc=act_bc, dve_prev_reader=dve_prev_reader):
            e.wait_ge(sem_act, act_bc)
            if dve_prev_reader:
                e.wait_ge(sem_dve, dve_prev_reader)
            cbcc.tick(e.dma_start(
                out=cbc[:][0:1, :].rearrange("o (n t) -> o n t", t=TC),
                in_=bscb[0:NST, TC:2 * TC]))
            k = 1
            while k < P:
                kk = min(k, P - k)
                e.wait_ge(sem_cbc, cbcc.n)
                cbcc.tick(e.dma_start(out=cbc[:][k:k + kk, :],
                                      in_=cbc[:][0:kk, :]))
                k += kk
        GP(_)
        head_state[(c, "sy_cbc")] = cbcc.n

    emit_head_stats(0, 0)
    emit_head_proj(0, 0)

    stage = [hidt[:][:, 0:H], hidt[:][:, H:2 * H]]
    cast_jobs = ([("wg", m) for m in range(MC)] + [("wu", m) for m in range(MC)]
                 + [("wd", m) for m in range(DC)])
    wc_marks = {}
    for j, (kind, m) in enumerate(cast_jobs):
        par = j % 2
        wch = wcc[par]
        src = {"wg": wg_in, "wu": wu_in, "wd": wd_in}[kind][m]
        width = H if kind == "wd" else DM
        sbuf = stage[par][:, 0:width]

        def _(e, j=j, wch=wch, src=src, sbuf=sbuf):
            if j >= 2:
                e.wait_ge(sem_wsc, (j - 1) * 16)
            wch.tick(e.dma_start(out=sbuf, in_=src))
        GP(_)
        wc_marks[j] = wch.n

    def emit_cast_stores(j0, j1):
        for j in range(j0, min(j1, len(cast_jobs))):
            kind, m = cast_jobs[j]
            par = j % 2
            wch = wcc[par]
            dst = {"wg": wg_scr, "wu": wu_scr, "wd": wd_scr}[kind][m]
            width = H if kind == "wd" else DM
            sbuf = stage[par][:, 0:width]

            def _(e, wch=wch, wc_mark=wc_marks[j], dst=dst, sbuf=sbuf):
                e.wait_ge(wch.sem, wc_mark)
                wscc.tick(e.dma_start(out=dst, in_=sbuf))
            SY(_)
    WSC_ALL_N = len(cast_jobs) * 16
    wsc_all = WSC_ALL_N

    dve_chunk_end = 0
    for c in range(NCH):
        c0 = c * TC
        slot = slotv(c)
        ybfv = slot
        slot_guard_pe = max(gu_state.get(("gudone", c - 2), 0),
                            gu_state.get(("ddone", c - 2), 0))
        slot_guard_out = 0
        dve_unorm = head_state[(c, "dve_unorm")]
        act_xdt = head_state[(c, "act_xdt")]
        gp_bbc = head_state[(c, "gp_bbc")]
        sy_cbc = head_state[(c, "sy_cbc")]
        dve_prev_reader = dve_chunk_end if c > 0 else 0

        t_scans_prev = 0
        t_ym0_prev = 0
        dve_scan_prev = 0
        for dc in range(DC):
            if c >= 1 and dc < 6:
                emit_gu_batch(c - 1, 4 * dc, 4 * dc + 4)
            dtsl = dtb[:][:, (dc % 2) * TC:(dc % 2 + 1) * TC]
            A0R = abar_h0 if dc % 2 == 0 else hbuf
            A1R = abar_h1
            OUT0R = hbuf if dc % 2 == 0 else abar_h0

            def _(e, dc=dc, act_xdt=act_xdt):
                e.wait_ge(sem_act, act_xdt)
                pwait(e, "ps_d")
                pe.tick(e.matmul(ps_d[:], wdtTb[:][:, dc * P:(dc + 1) * P],
                                 xdtb[:], start=True, stop=True))
            TEN(_)
            pe_dt = pe.n

            def _(e, dc=dc, c=c, pe_dt=pe_dt, dtsl=dtsl, A0R=A0R, A1R=A1R,
                  t_scans_prev=t_scans_prev, t_ym0_prev=t_ym0_prev,
                  dve_prev_reader=dve_prev_reader):
                e.wait_ge(sem_pe, pe_dt)
                if dc >= 1:
                    e.wait_ge(sem_dve, t_scans_prev)
                elif dve_prev_reader:
                    e.wait_ge(sem_dve, dve_prev_reader)
                e.activation(out=etx, in_=ps_d[:], func=AF.Exp,
                             bias=bdtp[:][:, dc:dc + 1])
                e.activation(out=dtsl, in_=etx, func=AF.Ln, bias=cst_one)
                for n in range(HNST, NST):
                    e.activation(
                        out=A1R[:, (n - HNST) * TC:(n - HNST + 1) * TC],
                        in_=dtsl, func=AF.Exp,
                        scale=aneg[:][:, dc * NST + n:dc * NST + n + 1])
                if dc >= 1:
                    e.wait_ge(sem_dve, t_ym0_prev)
                for n in range(HNST):
                    ins = e.activation(
                        out=A0R[:, n * TC:(n + 1) * TC],
                        in_=dtsl, func=AF.Exp,
                        scale=aneg[:][:, dc * NST + n:dc * NST + n + 1])
                    if n == HNST - 1:
                        act.tick(ins)
            ACT(_)
            pfree("ps_d", act)
            act_abar = act.n

            ysl = ybfv[:, dc * TC:(dc + 1) * TC]

            def _(e, dc=dc, c=c, act_abar=act_abar, gp_bbc=gp_bbc,
                  sy_cbc=sy_cbc, dtsl=dtsl, ysl=ysl, A0R=A0R, A1R=A1R,
                  OUT0R=OUT0R, slot_guard_pe=slot_guard_pe,
                  slot_guard_out=slot_guard_out):
                e.wait_ge(sem_act, act_abar)
                e.wait_ge(sem_bbc, gp_bbc)
                av0 = A0R.rearrange("p (n t) -> p n t", t=TC)
                av1 = A1R.rearrange("p (n t) -> p n t", t=TC)
                if c > 0:
                    hl = hlast[:][:, dc * NST:(dc + 1) * NST]
                    e.tensor_mul(fix16[:][:, 0:HNST], av0[:, :, 0],
                                 hl[:, 0:HNST])
                    e.tensor_mul(fix16[:][:, HNST:NST], av1[:, :, 0],
                                 hl[:, HNST:NST])
                dtu = t1s
                e.tensor_mul(dtu, dtsl, unb[:][:, dc * TC:(dc + 1) * TC])
                dtu_b = dtu.unsqueeze(1).broadcast_to([P, NST, TC])
                bbc_v = bbc[:].rearrange("p (n t) -> p n t", t=TC)
                xbuf_v = xbuf.rearrange("p (n t) -> p n t", t=TC)
                e.tensor_tensor(out=xbuf_v, in0=dtu_b, in1=bbc_v, op=ALU.mult)
                if c > 0:
                    e.tensor_tensor(out=xbuf_v[:, :, 0], in0=xbuf_v[:, :, 0],
                                    in1=fix16[:], op=ALU.add)
                e.memset(av0[:, :, 0:1], 0.0)
                dve.tick(e.memset(av1[:, :, 0:1], 0.0))
                e.wait_ge(sem_dve, dve.n)
                e.tensor_tensor_scan(
                    out=OUT0R, data0=A0R, data1=xbuf[:, 0:HNST * TC],
                    initial=0.0, op0=ALU.mult, op1=ALU.add)
                dve.tick(e.tensor_tensor_scan(
                    out=A0R, data0=A1R, data1=xbuf[:, HNST * TC:],
                    initial=0.0, op0=ALU.mult, op1=ALU.add))
                e.wait_ge(sem_cbc, sy_cbc)
                if dc == 0:
                    if slot_guard_pe:
                        e.wait_ge(sem_pe, slot_guard_pe)
                    if slot_guard_out:
                        e.wait_ge(sem_out, slot_guard_out)
                hv0 = OUT0R.rearrange("p (n t) -> p n t", t=TC)
                cb0 = cbc[:][:, 0:HNST * TC].rearrange("p (n t) -> p n t", t=TC)
                xb0 = xbuf[:, 0:HNST * TC]
                e.tensor_tensor(
                    out=xb0.rearrange("p (n t) -> p n t", t=TC),
                    in0=hv0, in1=cb0, op=ALU.mult)
                dve.tick(e.tensor_copy(
                    out=hlast[:][:, dc * NST:dc * NST + HNST],
                    in_=hv0[:, :, TC - 1]))
                hv1 = A0R.rearrange("p (n t) -> p n t", t=TC)
                cb1 = cbc[:][:, HNST * TC:].rearrange("p (n t) -> p n t", t=TC)
                xb1 = xbuf[:, HNST * TC:]
                e.tensor_tensor(out=xb1.rearrange("p (n t) -> p n t", t=TC),
                                in0=hv1, in1=cb1, op=ALU.mult)
                vin, width = xbuf, NST * TC
                while width > 2 * TC:
                    vv = vin.rearrange("p (j two t) -> p j two t", two=2, t=TC)
                    outw = width // 2
                    vout = xbuf[:, 0:outw]
                    vo = vout.rearrange("p (j t) -> p j t", t=TC)
                    e.tensor_tensor(out=vo, in0=vv[:, :, 0, :],
                                    in1=vv[:, :, 1, :], op=ALU.add)
                    vin, width = vout, outw
                vv = vin.rearrange("p (j two t) -> p j two t", two=2, t=TC)
                e.tensor_tensor(out=ysl, in0=vv[:, 0, 0, :],
                                in1=vv[:, 0, 1, :], op=ALU.add)
                e.tensor_copy(
                    out=hlast[:][:, dc * NST + HNST:(dc + 1) * NST],
                    in_=hv1[:, :, TC - 1])
                dve.tick(e.scalar_tensor_tensor(
                    out=ysl, in0=unb[:][:, dc * TC:(dc + 1) * TC],
                    scalar=dskp[:][:, dc:dc + 1], in1=ysl,
                    op0=ALU.mult, op1=ALU.add))
            DVE(_)
            dve_scan_prev = dve.n
            t_ym0_prev = dve.n - 1
            t_scans_prev = dve.n - 2
            if c == 0 and dc >= 2:
                emit_u_tile(2 * dc)
                emit_u_tile(2 * dc + 1)
                emit_cast_stores((dc - 2) * 10, (dc - 1) * 10)
            if c + 1 < NCH and dc == (7 if c == 0 else 2):
                emit_head_stats_mid(c + 1)
            if c >= 1 and dc == 6:
                emit_down_part1(c - 1, gu_state[("gudone", c - 1)])
            if c >= 1 and dc == 7:
                emit_down_part1b(c - 1)
        dve_mamba = dve.n

        if c >= 1:
            emit_down_part2a_pre(c - 1)
        if c + 1 < NCH:
            emit_head_bcast(c + 1, t_scans_prev)
            emit_head_proj(c + 1, dve_mamba)
        if c >= 1:
            emit_down_part2a(c - 1)

        bc2 = emit_stats_rows_bcast(
            lambda i: ybfv[:, i * TC:(i + 1) * TC],
            [(sem_dve, dve_mamba)], src_bf16=True)

        def _(e, bc2=bc2, c0=c0, ybfv=ybfv, slot=slot):
            e.wait_ge(sem_act, bc2)
            for dc in range(DC):
                e.tensor_mul(t1s, ybfv[:, dc * TC:(dc + 1) * TC], rstd_bc)
                e.tensor_sub(t1s, t1s, mr_bc)
                xsl = xT[:][:, dc * T + c0:dc * T + c0 + TC]
                e.tensor_tensor(out=xsl, in0=xsl, in1=t1s, op=ALU.add)
                ins = e.tensor_copy(out=slot[:, dc * TC:(dc + 1) * TC],
                                    in_=xsl)
                if dc == DC - 1:
                    dve.tick(ins)
        DVE(_)
        dve_x2 = dve.n

        bc3 = emit_stats_rows_bcast(
            lambda i: slot[:, i * TC:(i + 1) * TC],
            [(sem_dve, dve_x2)], src_bf16=True)

        def _(e, bc3=bc3, slot=slot):
            e.wait_ge(sem_act, bc3)
            for dc in range(DC):
                e.tensor_mul(t1s, slot[:, dc * TC:(dc + 1) * TC], rstd_bc)
                ins = e.tensor_sub(slot[:, dc * TC:(dc + 1) * TC], t1s, mr_bc)
                if dc == DC - 1:
                    dve.tick(ins)
        DVE(_)
        dve_chunk_end = dve.n
        gu_state["x2n_ready"] = dve.n

        if c >= 1:
            emit_down_part2out(c - 1)

    emit_gu_batch(NCH - 1, 0, MC)
    emit_down_part1(NCH - 1, gu_state[("gudone", NCH - 1)])
    emit_down_part1b(NCH - 1)
    emit_down_part2a_pre(NCH - 1)
    emit_down_part2a(NCH - 1)
    emit_down_part2out(NCH - 1)

    def _(e):
        e.wait_ge(sem_out, outc.n)
    SY(_)

    es.close()
    return nc


_PROG = None


def _get_prog():
    global _PROG
    if _PROG is None:
        _PROG = build_program()
    return _PROG


def _prep_host(inputs):
    W_xdt = inputs["L0_attn_W_xdt"]
    W_dt = inputs["L0_attn_W_dt"]
    W_B = inputs["L0_attn_W_B"]
    W_C = inputs["L0_attn_W_C"]
    A_log = inputs["L0_attn_A_log"]
    b_dt = inputs["L0_attn_b_dt"]
    D = inputs["L0_attn_D"]
    wg = inputs["L0_ffn_w_gate"]
    wu = inputs["L0_ffn_w_up"]
    wd = inputs["L0_ffn_w_down"]

    def dchunk_cols(a2d, inner):
        return np.ascontiguousarray(
            a2d.reshape(DC, P, inner).transpose(1, 0, 2).reshape(P, DC * inner))

    wxdtT = dchunk_cols(np.ascontiguousarray(W_xdt.T), R).astype(np.float32)
    wdtT = np.ascontiguousarray(W_dt.T).astype(np.float32)
    wbT = dchunk_cols(np.ascontiguousarray(W_B.T), NST).astype(np.float32)
    wcT = dchunk_cols(np.ascontiguousarray(W_C.T), NST).astype(np.float32)
    alog_p = dchunk_cols(A_log, NST).astype(np.float32)
    bdt_p = np.ascontiguousarray(b_dt.reshape(DC, P).T).astype(np.float32)
    d_p = np.ascontiguousarray(D.reshape(DC, P).T).astype(np.float32)

    def prep_lhsT(aT, kc, mc):
        a4 = aT.reshape(kc, P, mc, P)
        return np.ascontiguousarray(
            a4.transpose(2, 1, 0, 3).reshape(mc, P, kc * P))

    wg_prep = prep_lhsT(np.ascontiguousarray(wg.T), DC, MC).astype(np.int32)
    wu_prep = prep_lhsT(np.ascontiguousarray(wu.T), DC, MC).astype(np.int32)
    wd_prep = prep_lhsT(np.ascontiguousarray(wd.T), MC, DC).astype(np.int32)

    return dict(
        wg_prep=wg_prep, wu_prep=wu_prep, wd_prep=wd_prep,
        wxdtT=wxdtT, wdtT=wdtT, wbT=wbT, wcT=wcT, alog_p=alog_p,
        bdt_p=bdt_p, d_p=d_p,
        bxdt=np.ascontiguousarray(
            inputs["L0_attn_b_xdt"].reshape(R, 1)).astype(np.float32),
        bb=np.ascontiguousarray(
            inputs["L0_attn_b_B"].reshape(NST, 1)).astype(np.float32),
        bc=np.ascontiguousarray(
            inputs["L0_attn_b_C"].reshape(NST, 1)).astype(np.float32),
        idf=np.eye(P, dtype=np.float32),
        ones_b=np.ones((P, 1), dtype=BF),
        ones_row=np.ones((1, P), dtype=BF),
    )


def kernel(**inputs):
    u = np.asarray(inputs["L0_attn_u"], dtype=np.float32)
    common = _prep_host({k: np.asarray(v) for k, v in inputs.items()})
    nc = _get_prog()
    in_maps = [dict(common, u=np.ascontiguousarray(u[b])) for b in range(8)]
    res = run_bass_kernel_spmd(nc, in_maps, core_ids=list(range(8)))
    return np.stack([np.asarray(r["y_out"], dtype=np.float32)
                     for r in res.results], axis=0)


# version 21
# speedup vs baseline: 1.0788x; 1.0364x over previous
import contextlib
import numpy as np
import ml_dtypes
import concourse.bass as bass
import concourse.mybir as mybir
from concourse.bass_utils import run_bass_kernel_spmd

F32 = mybir.dt.float32
BF16 = mybir.dt.bfloat16
I32 = mybir.dt.int32
AF = mybir.ActivationFunctionType
ALU = mybir.AluOpType
BF = ml_dtypes.bfloat16

P = 128
T = 2048
DM = 1024
DC = 8
NST = 16
HNST = NST // 2
R = 64
H = 3072
MC = 24
TC = 512
NCH = T // TC
FH = TC // 2
S127 = float(np.float32(1.0) / np.float32(127.0))
EPS = 1e-5


class Chain:
    def __init__(self, sem, step=1):
        self.sem, self.step, self.n = sem, step, 0

    def tick(self, ins):
        ins.then_inc(self.sem, self.step)
        self.n += self.step
        return self.n


def build_program():
    nc = bass.Bass("TRN2", target_bir_lowering=False)

    u_in = nc.dram_tensor("u", [T, DM], F32, kind="ExternalInput").ap()
    wg_in = nc.dram_tensor("wg_prep", [MC, P, DM], I32, kind="ExternalInput").ap()
    wu_in = nc.dram_tensor("wu_prep", [MC, P, DM], I32, kind="ExternalInput").ap()
    wd_in = nc.dram_tensor("wd_prep", [DC, P, H], I32, kind="ExternalInput").ap()
    wxdtT_in = nc.dram_tensor("wxdtT", [P, DC * R], F32, kind="ExternalInput").ap()
    wdtT_in = nc.dram_tensor("wdtT", [R, DM], F32, kind="ExternalInput").ap()
    wbT_in = nc.dram_tensor("wbT", [P, DC * NST], F32, kind="ExternalInput").ap()
    wcT_in = nc.dram_tensor("wcT", [P, DC * NST], F32, kind="ExternalInput").ap()
    alog_in = nc.dram_tensor("alog_p", [P, DC * NST], F32, kind="ExternalInput").ap()
    bdt_in = nc.dram_tensor("bdt_p", [P, DC], F32, kind="ExternalInput").ap()
    dsk_in = nc.dram_tensor("d_p", [P, DC], F32, kind="ExternalInput").ap()
    bxdt_in = nc.dram_tensor("bxdt", [R, 1], F32, kind="ExternalInput").ap()
    bb_in = nc.dram_tensor("bb", [NST, 1], F32, kind="ExternalInput").ap()
    bc_in = nc.dram_tensor("bc", [NST, 1], F32, kind="ExternalInput").ap()
    idf_in = nc.dram_tensor("idf", [P, P], F32, kind="ExternalInput").ap()
    onesb_in = nc.dram_tensor("ones_b", [P, 1], BF16, kind="ExternalInput").ap()
    onesrow_in = nc.dram_tensor("ones_row", [1, P], BF16, kind="ExternalInput").ap()

    wg_scr = nc.dram_tensor("wg_scr", [MC, P, DM], BF16, kind="Internal").ap()
    wu_scr = nc.dram_tensor("wu_scr", [MC, P, DM], BF16, kind="Internal").ap()
    wd_scr = nc.dram_tensor("wd_scr", [DC, P, H], BF16, kind="Internal").ap()
    hid_hbm = nc.dram_tensor("hid_scr", [MC, P, TC], BF16, kind="Internal").ap()
    y_out = nc.dram_tensor("y_out", [T, DM], F32, kind="ExternalOutput").ap()

    es = contextlib.ExitStack()

    def sb(name, shape, dtype):
        return es.enter_context(nc.sbuf_tensor(name, shape, dtype))

    def psum(name, shape, dtype=F32):
        return es.enter_context(nc.psum_tensor(name, shape, dtype))

    xT = sb("xT", [P, DC * T], F32)
    x2n = sb("x2n", [P, 2 * DC * TC], BF16)
    dtb = sb("dtb", [P, 2 * TC], BF16)
    unb = sb("unb", [P, DC * TC], BF16)
    mam = sb("mam", [P, (2 * NST + HNST) * TC], BF16)
    bbc = sb("bbc", [P, NST * TC], BF16)
    cbc = sb("cbc", [P, NST * TC], BF16)
    aux = sb("aux", [P, 2 * TC], BF16)
    rows = sb("rows", [1, 3 * TC], F32)
    rowb = sb("rowb", [1, TC], BF16)
    rbc = sb("rbc", [P, 2 * TC], BF16)
    smw = sb("smw", [P, 2 * DC * NST + DC * R], BF16)
    wdtTb = sb("wdtTb", [R, DM], BF16)
    xdtb = sb("xdtb", [R, TC], BF16)
    bscb = sb("bscb", [NST, 2 * TC], BF16)
    aneg = sb("aneg", [P, DC * NST], F32)
    bdtp = sb("bdtp", [P, DC], F32)
    dskp = sb("dskp", [P, DC], F32)
    bxdtc = sb("bxdtc", [R, 1], F32)
    bbcol = sb("bbcol", [NST, 1], F32)
    bccol = sb("bccol", [NST, 1], F32)
    idf = sb("idf_sb", [P, P], F32)
    onesb = sb("onesb_sb", [P, 1], BF16)
    onesrow = sb("onesrow_sb", [1, P], BF16)
    hlast = sb("hlast", [P, DC * NST], BF16)
    fix16 = sb("fix16", [P, NST], BF16)
    cst = sb("cst", [P, 2], F32)
    hidt = sb("hidt", [P, MC * FH], BF16)
    wgu = sb("wgu", [P, 4 * DM], BF16)
    sgub = sb("sgub", [P, 4 * TC], BF16)
    hidm = sb("hidm", [P, 2 * TC], BF16)
    wdm2 = wgu[:][:, 0:H]

    auxv = aux[:]
    t1s = auxv[:, 0:TC]
    etx = auxv[:, TC:2 * TC]

    smwv = smw[:]
    wbTb = smwv[:, 0:DC * NST]
    wcTb = smwv[:, DC * NST:2 * DC * NST]
    wxdtTb = smwv[:, 2 * DC * NST:2 * DC * NST + DC * R]

    mamx = mam[:]
    abar = mamx[:, 0:NST * TC]
    abar_h0 = mamx[:, 0:HNST * TC]
    abar_h1 = mamx[:, HNST * TC:NST * TC]
    xbuf = mamx[:, NST * TC:2 * NST * TC]
    hbuf = mamx[:, 2 * NST * TC:(2 * NST + HNST) * TC]
    rstd2_bc = mamx[:, HNST * TC:HNST * TC + TC]
    mr2_bc = mamx[:, HNST * TC + TC:HNST * TC + 2 * TC]
    uld = mam[:].bitcast(F32)[:, 0:2 * DM]

    def slotv(c):
        return x2n[:][:, (c % 2) * DC * TC:((c % 2) + 1) * DC * TC]


    ps_g = psum("ps_g", [P, TC])
    ps_u = psum("ps_u", [P, TC])
    ps_d = psum("ps_d", [P, TC])
    ps_s1 = psum("ps_s1", [P, TC])
    ps_s2 = psum("ps_s2", [P, TC])
    ps_t1f = psum("ps_t1", [P, TC])
    ps_t2f = psum("ps_t2", [P, TC])
    ps_p = psum("ps_p", [P, TC])
    ps_t1 = ps_t1f[:, 0:P]
    ps_t2 = ps_t2f[:, 0:P]

    def sem(name):
        return es.enter_context(nc.semaphore(name))

    sem_pe, sem_act, sem_dve, sem_po = sem("s_pe"), sem("s_act"), sem("s_dve"), sem("s_po")
    sem_sy, sem_gp = sem("s_sy"), sem("s_gp")
    sem_uld0, sem_uld1 = sem("s_uld0"), sem("s_uld1")
    sem_bbc, sem_cbc = sem("s_bbc"), sem("s_cbc")
    sem_wc0, sem_wc1 = sem("s_wc0"), sem("s_wc1")
    sem_wsc = sem("s_wsc")
    sem_wv0, sem_wv1 = sem("s_wv0"), sem("s_wv1")
    sem_wd = sem("s_wd")
    sem_wd2 = sem("s_wd2")
    sem_sgub = sem("s_sgub")
    sem_hidt = sem("s_hidt")
    sem_out = sem("s_out")
    blk = es.enter_context(nc.Block())

    pe, act, dve, po = Chain(sem_pe), Chain(sem_act), Chain(sem_dve), Chain(sem_po)
    sy, gp = Chain(sem_sy, 16), Chain(sem_gp, 16)
    uldc = [Chain(sem_uld0, 16), Chain(sem_uld1, 16)]
    bbcc, cbcc = Chain(sem_bbc, 16), Chain(sem_cbc, 16)
    wcc = [Chain(sem_wc0, 16), Chain(sem_wc1, 16)]
    wscc = Chain(sem_wsc, 16)
    wvc = [Chain(sem_wv0, 16), Chain(sem_wv1, 16)]
    wdc = Chain(sem_wd, 16)
    wdc2 = Chain(sem_wd2, 16)
    sguc = Chain(sem_sgub, 16)
    hidtc = Chain(sem_hidt, 16)
    outc = Chain(sem_out, 16)

    TEN, ACT, DVE, SY, GP = (blk.tensor, blk.scalar, blk.vector, blk.sync,
                             blk.gpsimd)
    POOL = blk.gpsimd

    def load_act_set(e, set_id):
        e.add_instruction(mybir.InstLoadActFuncSet(
            name=nc.get_next_instruction_name(), act_func_set_id=set_id,
            ins=[], outs=[]))

    psum_free = {}

    def pwait(e, name):
        ch_m = psum_free.get(name)
        if ch_m:
            e.wait_ge(ch_m[0].sem, ch_m[1])

    def pfree(name, chain):
        psum_free[name] = (chain, chain.n)

    rows_s = rows[:][:, 0:TC]
    rows_q = rows[:][:, TC:2 * TC]
    rows_a1 = rows[:][:, 2 * TC:3 * TC]
    rows_a2 = rows_q
    rowb_1 = rowb[:][:, 0:TC]
    rowb2 = hidm[:].bitcast(BF16)[0:1, TC:2 * TC]
    rstd_bc = rbc[:][:, 0:TC]
    mr_bc = rbc[:][:, TC:2 * TC]
    cst_eps = cst[:][:, 0:1]
    cst_one = cst[:][:, 1:2]

    def _(e):
        for ap_out, ap_in in [(idf[:], idf_in), (onesb[:], onesb_in),
                              (onesrow[:], onesrow_in), (bdtp[:], bdt_in),
                              (dskp[:], dsk_in), (bxdtc[:], bxdt_in),
                              (bbcol[:], bb_in), (bccol[:], bc_in)]:
            sy.tick(e.dma_start(out=ap_out, in_=ap_in))
    SY(_)
    sy_setup = sy.n

    def _(e):
        gp.tick(e.dma_start(out=wbTb, in_=wbT_in))
        gp.tick(e.dma_start(out=wcTb, in_=wcT_in))
        gp.tick(e.dma_start(out=wxdtTb, in_=wxdtT_in))
        gp.tick(e.dma_start(out=wdtTb[:], in_=wdtT_in))
        gp.tick(e.dma_start(out=aneg[:], in_=alog_in))
    GP(_)
    gp_small = gp.n

    def _(e):
        load_act_set(e, 6)
        e.wait_ge(sem_gp, gp_small)
        act.tick(e.activation(out=aneg[:], in_=aneg[:], func=AF.Exp))
    ACT(_)
    act_aexp = act.n

    def _(e):
        e.wait_ge(sem_act, act_aexp)
        e.memset(cst_eps, EPS)
        e.memset(cst_one, 1.0)
        dve.tick(e.tensor_scalar_mul(aneg[:], aneg[:], -1.0))
    DVE(_)

    u_v = u_in.rearrange("(tt p) d -> p tt d", p=P)
    pe_tile_marks = {}
    act_u_chunk = {}
    uld2 = wgu[:].bitcast(F32)

    def emit_u_tile(tt):
        par = tt % 2
        base = uld if tt < 4 else uld2
        buf = base[:, par * DM:(par + 1) * DM]
        uch = uldc[par]

        def _(e, tt=tt, buf=buf, uch=uch):
            e.wait_ge(sem_sy, sy_setup)
            m = pe_tile_marks.get(tt - 2, 0)
            if m:
                e.wait_ge(sem_pe, m)
            uch.tick(e.dma_start(out=buf, in_=u_v[:, tt, :]))
        SY(_)
        u_mark = uch.n
        act_base = act.n

        def _(e, tt=tt, buf=buf, u_mark=u_mark, uch=uch, act_base=act_base):
            e.wait_ge(uch.sem, u_mark)
            if tt > 0:
                e.wait_ge(sem_act, act_base)
            for dc in range(DC):
                pst = ps_t1 if dc % 2 == 0 else ps_t2
                if dc >= 2:
                    e.wait_ge(sem_act, act_base + dc - 1)
                pe.tick(e.transpose(pst[:], buf[:, dc * P:(dc + 1) * P], idf[:]))
        TEN(_)
        pe_tile_marks[tt] = pe.n
        pe_base = pe.n - DC

        def _(e, tt=tt, pe_base=pe_base):
            for dc in range(DC):
                pst = ps_t1 if dc % 2 == 0 else ps_t2
                e.wait_ge(sem_pe, pe_base + dc + 1)
                act.tick(e.copy(
                    out=xT[:][:, dc * T + tt * P:dc * T + (tt + 1) * P],
                    in_=pst[:]))
        ACT(_)
        if tt % 4 == 3:
            act_u_chunk[tt // 4] = act.n
        pfree("ps_t1", act)
        pfree("ps_t2", act)

    for tt in range(4):
        emit_u_tile(tt)

    def emit_stats_rows_bcast(src_getter, extra_marks, src_bf16=False):
        pe_slot = {}
        for i in range(DC):
            src = src_getter(i)
            qslot = dtb[:][:, (i % 2) * TC:(i % 2 + 1) * TC]

            def _(e, i=i, src=src, qslot=qslot):
                if i == 0:
                    for s_, v in extra_marks:
                        if v > 0:
                            e.wait_ge(s_, v)
                w = pe_slot.get(i - 2 if src_bf16 else i - 1)
                if w:
                    e.wait_ge(sem_pe, w)
                if not src_bf16:
                    e.activation(out=etx, in_=src, func=AF.Copy)
                act.tick(e.activation(out=qslot, in_=src, func=AF.Square))
            ACT(_)
            a_i = act.n

            def _(e, i=i, src=src, qslot=qslot, a_i=a_i):
                e.wait_ge(sem_act, a_i)
                if i == 0:
                    pwait(e, "ps_s1")
                    pwait(e, "ps_s2")
                e.matmul(ps_s1[0:1, :], onesb[:], src if src_bf16 else etx,
                         start=(i == 0), stop=(i == DC - 1))
                pe.tick(e.matmul(ps_s2[0:1, :], onesb[:], qslot,
                                 start=(i == 0), stop=(i == DC - 1)))
            TEN(_)
            pe_slot[i] = pe.n
        pe_stats = pe.n

        def _(e, pe_stats=pe_stats):
            e.wait_ge(sem_pe, pe_stats)
            e.copy(out=rows_s, in_=ps_s1[0:1, :])
            e.copy(out=rows_q, in_=ps_s2[0:1, :])
            e.mul(out=rows_s, in_=rows_s, mul=1.0 / DM)
            e.mul(out=rows_q, in_=rows_q, mul=1.0 / DM)
            act.tick(e.activation(out=rows_a1, in_=rows_s, func=AF.Square))
        ACT(_)
        a_m = act.n

        def _(e, a_m=a_m):
            e.wait_ge(sem_act, a_m)
            dve.tick(e.tensor_sub(rows_a2, rows_q, rows_a1))
        DVE(_)
        d_v = dve.n

        def _(e, d_v=d_v):
            e.wait_ge(sem_dve, d_v)
            e.activation(out=rows_a1, in_=rows_a2, func=AF.Ln,
                         bias=cst_eps[0:1, :])
            act.tick(e.activation(out=rows_a1, in_=rows_a1, func=AF.Exp,
                                  scale=-0.5))
        ACT(_)
        a_r = act.n

        def _(e, a_r=a_r):
            e.wait_ge(sem_act, a_r)
            dve.tick(e.tensor_mul(rows_a2, rows_s, rows_a1))
        DVE(_)
        d_mr = dve.n
        dve_now = dve.n

        def _(e, d_mr=d_mr):
            e.wait_ge(sem_dve, d_mr)
            e.wait_ge(sem_sgub, sguc.n)
            e.activation(out=rowb_1, in_=rows_a1, func=AF.Copy)
            act.tick(e.activation(out=rowb2, in_=rows_a2, func=AF.Copy))
        ACT(_)
        a_rb = act.n

        def _(e, a_rb=a_rb):
            e.wait_ge(sem_act, a_rb)
            pwait(e, "ps_p")
            pwait(e, "ps_s2")
            e.matmul(ps_p[:], onesrow[:], rowb_1, start=True, stop=True)
            pe.tick(e.matmul(ps_s2[:], onesrow[:], rowb2, start=True,
                             stop=True))
        TEN(_)
        pe_b1 = pe.n

        def _(e, pe_b1=pe_b1, dve_now=dve_now):
            e.wait_ge(sem_pe, pe_b1)
            e.wait_ge(sem_dve, dve_now)
            e.copy(out=rstd_bc, in_=ps_p[:])
            act.tick(e.copy(out=mr_bc, in_=ps_s2[:]))
        ACT(_)
        pfree("ps_p", act)
        pfree("ps_s2", act)
        return act.n

    gu_state = {"x2n_ready": 0}
    DNPS = [(ps_g, "ps_g"), (ps_u, "ps_u"), (ps_t1f, "ps_t1"),
            (ps_t2f, "ps_t2")]

    def emit_gu_batch(wc, m0, m1):
        for m in range(m0, m1):
            par = m % 2
            wch = wvc[par]
            wg_sb = wgu[:][:, par * 2 * DM:par * 2 * DM + DM]
            wu_sb = wgu[:][:, par * 2 * DM + DM:(par + 1) * 2 * DM]
            psg, psg_n = (ps_g, "ps_g") if par == 0 else (ps_t1f, "ps_t1")
            psu, psu_n = (ps_u, "ps_u") if par == 0 else (ps_t2f, "ps_t2")
            sg_sb = sgub[:][:, par * 2 * TC:par * 2 * TC + TC]
            ub_sb = sgub[:][:, par * 2 * TC + TC:(par + 1) * 2 * TC]

            pre = gu_state.get(("wvmark", wc, m))
            if pre is None:
                def _(e, m=m, wc=wc, wch=wch, wg_sb=wg_sb, wu_sb=wu_sb):
                    e.wait_ge(sem_wsc, wsc_all)
                    if wc == 0 and m < 2:
                        e.wait_ge(sem_act, act_u_chunk[3])
                    mm = gu_state.get(("pe_w", m - 2))
                    if mm:
                        e.wait_ge(sem_pe, mm)
                    mm2 = gu_state.get("down_done")
                    if mm2 and m < 2:
                        e.wait_ge(sem_pe, mm2)
                    wch.tick(e.dma_start(out=wg_sb, in_=wg_scr[m]))
                    wch.tick(e.dma_start(out=wu_sb, in_=wu_scr[m]))
                SY(_)
                wv_mark = wch.n
            else:
                wv_mark = pre

            def _(e, m=m, wc=wc, wch=wch, wv_mark=wv_mark, wg_sb=wg_sb,
                  wu_sb=wu_sb, psg=psg, psu=psu, psg_n=psg_n, psu_n=psu_n):
                e.wait_ge(wch.sem, wv_mark)
                e.wait_ge(sem_dve, gu_state["x2n_ready"])
                pwait(e, psg_n)
                pwait(e, psu_n)
                sl = slotv(wc)
                for k in range(DC):
                    e.matmul(psg[:], wg_sb[:, k * P:(k + 1) * P],
                             sl[:, k * TC:(k + 1) * TC],
                             start=(k == 0), stop=(k == DC - 1))
                for k in range(DC):
                    ins = e.matmul(psu[:], wu_sb[:, k * P:(k + 1) * P],
                                   sl[:, k * TC:(k + 1) * TC],
                                   start=(k == 0), stop=(k == DC - 1))
                    if k == DC - 1:
                        pe.tick(ins)
            TEN(_)
            gu_state[("pe_w", m)] = pe.n
            pe_gu = pe.n

            def _(e, m=m, m0=m0, m1=m1, wc=wc, pe_gu=pe_gu, psg=psg, psu=psu,
                  sg_sb=sg_sb, ub_sb=ub_sb):
                if m == m0:
                    load_act_set(e, 18)
                e.wait_ge(sem_pe, pe_gu)
                if m < 2:
                    oo = gu_state.get(("otok_done", 0 if wc < 1 else wc - 1), 0) \
                        if wc >= 1 else 0
                    if oo:
                        e.wait_ge(sem_out, oo)
                mm = gu_state.get(("hidtt", m - 2))
                if mm:
                    e.wait_ge(sem_po, mm)
                e.activation(out=sg_sb, in_=psg[:], func=AF.Silu, scale=S127)
                act.tick(e.activation(out=ub_sb, in_=psu[:], func=AF.Copy,
                                      scale=S127))
                if m == m1 - 1:
                    load_act_set(e, 6)
            ACT(_)
            pfree(psg_n, act)
            pfree(psu_n, act)
            a_gu = act.n

            def _(e, m=m, a_gu=a_gu, sg_sb=sg_sb, ub_sb=ub_sb):
                e.wait_ge(sem_act, a_gu)
                mm = gu_state.get(("hidst", m - 2))
                if mm:
                    e.wait_ge(sem_sgub, mm)
                par = m % 2
                po.tick(e.tensor_mul(hidm[:][:, par * TC:(par + 1) * TC],
                                     sg_sb, ub_sb))
            POOL(_)
            gu_state[("hidtt", m)] = po.n
            po_h = po.n

            def _(e, m=m, po_h=po_h):
                e.wait_ge(sem_po, po_h)
                par = m % 2
                sguc.tick(e.dma_start(out=hid_hbm[m],
                                      in_=hidm[:][:, par * TC:(par + 1) * TC]))
            GP(_)
            gu_state[("sgst", m)] = sguc.n
            gu_state[("hidst", m)] = sguc.n
        gu_state[("gudone", wc)] = pe.n

    def emit_down_mgroup(dn, fh, m2):
        wc = dn["wc"]
        dwnb, sq1 = dn["dwnb"], dn["sq1"]
        sy_hidt = dn["sy_hidt"]
        half_marks, st_marks, a_dn_marks = (dn["half_marks"], dn["st_marks"],
                                            dn["a_dn_marks"])
        psd, psd_n = DNPS[m2 % 4]
        HC = MC // 2
        HCW = HC * P
        for h in range(2):
            wch2 = wdc if h == 0 else wdc2
            prev = half_marks.get((fh, m2 - 1, h))
            pe_now = pe.n

            def _(e, m2=m2, fh=fh, h=h, wch2=wch2, prev=prev, pe_now=pe_now):
                if prev is not None:
                    e.wait_ge(sem_pe, prev)
                elif fh > 0 or gu_state.get("wd_used"):
                    e.wait_ge(sem_pe, pe_now)
                wch2.tick(e.dma_start(
                    out=wdm2[:, h * HCW:(h + 1) * HCW],
                    in_=wd_scr[m2][:, h * HCW:(h + 1) * HCW]))
            GP(_)
        gu_state["wd_used"] = True
        for h in range(2):
            wch2 = wdc if h == 0 else wdc2
            gp_wd = wch2.n

            def _(e, m2=m2, fh=fh, h=h, sy_hidt=sy_hidt, gp_wd=gp_wd,
                  psd=psd, psd_n=psd_n, wch2=wch2):
                if h == 0:
                    e.wait_ge(sem_hidt, sy_hidt)
                    pwait(e, psd_n)
                e.wait_ge(wch2.sem, gp_wd)
                for k2 in range(HC):
                    kk = h * HC + k2
                    ins = e.matmul(psd[:, 0:FH],
                                   wdm2[:, kk * P:(kk + 1) * P],
                                   hidt[:][:, kk * FH:(kk + 1) * FH],
                                   start=(kk == 0), stop=(kk == MC - 1))
                    if k2 == HC - 1:
                        pe.tick(ins)
            TEN(_)
            half_marks[(fh, m2, h)] = pe.n
        pe_dn = pe.n
        if m2 >= 2:
            emit_dn_stats(dn, fh, m2 - 2)
        st_guard = st_marks.get((fh, m2 - 2), 0)
        po_guard = dn["po_guard"]
        sg_done = dn["sg_done"]

        def _(e, m2=m2, fh=fh, pe_dn=pe_dn, psd=psd,
              po_guard=po_guard, st_guard=st_guard, sg_done=sg_done):
            e.wait_ge(sem_pe, pe_dn)
            if m2 == 0 and fh == 0:
                if po_guard:
                    e.wait_ge(sem_pe, po_guard)
                e.wait_ge(sem_sgub, sg_done)
            if st_guard:
                e.wait_ge(sem_pe, st_guard)
            dsl = dwnb[:, m2 * TC + fh * FH:m2 * TC + fh * FH + FH]
            e.activation(out=dsl, in_=psd[:, 0:FH], func=AF.Copy,
                         scale=S127)
            act.tick(e.activation(
                out=sq1[:, (m2 % 2) * FH:(m2 % 2 + 1) * FH],
                in_=psd[:, 0:FH], func=AF.Square, scale=S127))
        ACT(_)
        pfree(psd_n, act)
        a_dn_marks[(fh, m2)] = act.n

    def emit_dn_stats(dn, fh, k):
        dwnb, sq1 = dn["dwnb"], dn["sq1"]

        def _(e, k=k, fh=fh, a_dn=dn["a_dn_marks"][(fh, k)]):
            e.wait_ge(sem_act, a_dn)
            if k == 0 and fh == 0:
                pwait(e, "ps_s1")
                pwait(e, "ps_s2")
            e.matmul(ps_s1[0:1, fh * FH:(fh + 1) * FH], onesb[:],
                     dwnb[:, k * TC + fh * FH:k * TC + fh * FH + FH],
                     start=(k == 0), stop=(k == DC - 1))
            pe.tick(e.matmul(
                ps_s2[0:1, fh * FH:(fh + 1) * FH], onesb[:],
                sq1[:, (k % 2) * FH:(k % 2 + 1) * FH],
                start=(k == 0), stop=(k == DC - 1)))
        TEN(_)
        dn["st_marks"][(fh, k)] = pe.n

    def emit_dn_hidt(dn, fh):
        sg_done = dn["sg_done"]
        pe_hidt_guard = pe.n if fh == 1 else max(
            gu_state.get("down_done", 0),
            gu_state.get(("midstats_pe", dn["wc"] + 2), 0))

        def _(e, fh=fh, sg_done=sg_done, pe_hidt_guard=pe_hidt_guard):
            e.wait_ge(sem_sgub, sg_done)
            if pe_hidt_guard:
                e.wait_ge(sem_pe, pe_hidt_guard)
            hidtc.tick(e.dma_start(
                out=hidt[:].rearrange("p (a t) -> p a t", t=FH),
                in_=hid_hbm[:, :, fh * FH:(fh + 1) * FH].transpose([1, 0, 2])))
        GP(_)
        dn["sy_hidt"] = hidtc.n

    def emit_down_part1(wc, po_guard):
        dn = {"wc": wc, "po_guard": po_guard,
              "dwnb": slotv(wc), "sq1": hidm[:],
              "sg_done": gu_state[("sgst", MC - 1)],
              "half_marks": {}, "st_marks": {}, "a_dn_marks": {}}
        gu_state[("dn", wc)] = dn
        emit_dn_hidt(dn, 0)
        for m2 in range(4):
            emit_down_mgroup(dn, 0, m2)

    def emit_down_part1b(wc):
        dn = gu_state[("dn", wc)]
        for m2 in range(4, DC):
            emit_down_mgroup(dn, 0, m2)
        emit_dn_stats(dn, 0, DC - 2)
        emit_dn_stats(dn, 0, DC - 1)
        emit_dn_hidt(dn, 1)

    def emit_down_part2a_pre(wc):
        dn = gu_state[("dn", wc)]
        for m2 in range(2):
            emit_down_mgroup(dn, 1, m2)

    def emit_down_part2a(wc):
        dn = gu_state[("dn", wc)]
        dwnb = dn["dwnb"]
        for m2 in range(2, DC):
            emit_down_mgroup(dn, 1, m2)
        emit_dn_stats(dn, 1, DC - 2)
        emit_dn_stats(dn, 1, DC - 1)
        gu_state["down_done"] = pe.n
        gu_state[("ddone", wc)] = pe.n
        nwc = wc + 1
        if nwc < NCH:
            for m in (0, 1):
                par = m % 2
                wch = wvc[par]
                wg_sb = wgu[:][:, par * 2 * DM:par * 2 * DM + DM]
                wu_sb = wgu[:][:, par * 2 * DM + DM:(par + 1) * 2 * DM]

                def _(e, m=m, wch=wch, wg_sb=wg_sb, wu_sb=wu_sb,
                      dd=gu_state["down_done"]):
                    e.wait_ge(sem_wsc, wsc_all)
                    e.wait_ge(sem_pe, dd)
                    wch.tick(e.dma_start(out=wg_sb, in_=wg_scr[m]))
                    wch.tick(e.dma_start(out=wu_sb, in_=wu_scr[m]))
                SY(_)
                gu_state[("wvmark", nwc, m)] = wch.n
        pe_stats_dn = pe.n
        def _(e, pe_stats_dn=pe_stats_dn):
            e.wait_ge(sem_pe, pe_stats_dn)
            e.copy(out=rows_s, in_=ps_s1[0:1, :])
            e.copy(out=rows_q, in_=ps_s2[0:1, :])
            e.mul(out=rows_s, in_=rows_s, mul=1.0 / DM)
            e.mul(out=rows_q, in_=rows_q, mul=1.0 / DM)
            act.tick(e.activation(out=rows_a1, in_=rows_s, func=AF.Square))
        ACT(_)
        a_m4 = act.n

        def _(e, a_m4=a_m4):
            e.wait_ge(sem_act, a_m4)
            dve.tick(e.tensor_sub(rows_a2, rows_q, rows_a1))
        DVE(_)
        d_v4 = dve.n

        def _(e, d_v4=d_v4):
            e.wait_ge(sem_dve, d_v4)
            e.activation(out=rows_a1, in_=rows_a2, func=AF.Ln,
                         bias=cst_eps[0:1, :])
            act.tick(e.activation(out=rows_a1, in_=rows_a1, func=AF.Exp,
                                  scale=-0.5))
        ACT(_)
        a_r4 = act.n

        def _(e, a_r4=a_r4):
            e.wait_ge(sem_act, a_r4)
            dve.tick(e.tensor_mul(rows_a2, rows_s, rows_a1))
        DVE(_)
        d_mr4 = dve.n
        dve_now4 = dve.n

        def _(e, d_mr4=d_mr4):
            e.wait_ge(sem_dve, d_mr4)
            e.wait_ge(sem_sgub, sguc.n)
            e.activation(out=rowb_1, in_=rows_a1, func=AF.Copy)
            act.tick(e.activation(out=rowb2, in_=rows_a2, func=AF.Copy))
        ACT(_)
        a_rb4 = act.n

        def _(e, a_rb4=a_rb4):
            e.wait_ge(sem_act, a_rb4)
            pwait(e, "ps_p")
            pwait(e, "ps_s2")
            e.matmul(ps_p[:], onesrow[:], rowb_1, start=True, stop=True)
            pe.tick(e.matmul(ps_s2[:], onesrow[:], rowb2, start=True,
                             stop=True))
        TEN(_)
        pe_b14 = pe.n

        def _(e, pe_b14=pe_b14, dve_now4=dve_now4):
            e.wait_ge(sem_pe, pe_b14)
            e.wait_ge(sem_dve, dve_now4)
            e.copy(out=rstd_bc, in_=ps_p[:])
            act.tick(e.copy(out=mr_bc, in_=ps_s2[:]))
        ACT(_)
        pfree("ps_p", act)
        pfree("ps_s2", act)
        bc4 = act.n

        def _(e, bc4=bc4, wc=wc):
            e.wait_ge(sem_act, bc4)
            for dc in range(DC):
                e.tensor_mul(t1s, dwnb[:, dc * TC:(dc + 1) * TC], rstd_bc)
                e.tensor_sub(t1s, t1s, mr_bc)
                xsl = xT[:][:, dc * T + wc * TC:dc * T + (wc + 1) * TC]
                ins = e.tensor_tensor(out=xsl, in0=xsl, in1=t1s, op=ALU.add)
                if dc == DC - 1:
                    dve.tick(ins)
        DVE(_)
        dve_x3 = dve.n
        gu_state["x3_dve"] = dve_x3


    def emit_down_part2out(wc):
        dn = gu_state[("dn", wc)]
        dve_x3 = gu_state["x3_dve"]
        otok = sgub[:].bitcast(F32)[:, 0:DM]
        for tt in range(TC // P):
            act_base = act.n
            sy_now = outc.n

            def _(e, tt=tt, wc=wc, dve_x3=dve_x3, act_base=act_base):
                e.wait_ge(sem_dve, dve_x3)
                pwait(e, "ps_t1")
                pwait(e, "ps_t2")
                if tt > 0:
                    e.wait_ge(sem_act, act_base)
                for dc in range(DC):
                    pst = ps_t1 if dc % 2 == 0 else ps_t2
                    if dc >= 2:
                        e.wait_ge(sem_act, act_base + dc - 1)
                    pe.tick(e.transpose(
                        pst[:],
                        xT[:][:, dc * T + wc * TC + tt * P:
                              dc * T + wc * TC + (tt + 1) * P],
                        idf[:]))
            TEN(_)
            pe_base = pe.n - DC

            def _(e, tt=tt, pe_base=pe_base, sy_now=sy_now, otok=otok,
                  tt_guard=gu_state[("hidtt", MC - 1)]):
                for dc in range(DC):
                    pst = ps_t1 if dc % 2 == 0 else ps_t2
                    e.wait_ge(sem_pe, pe_base + dc + 1)
                    if dc == 0:
                        e.wait_ge(sem_out, sy_now)
                        if tt == 0:
                            e.wait_ge(sem_po, tt_guard)
                    act.tick(e.copy(out=otok[:, dc * P:(dc + 1) * P],
                                    in_=pst[:]))
            ACT(_)
            a_ot = act.n

            def _(e, tt=tt, wc=wc, a_ot=a_ot, otok=otok):
                e.wait_ge(sem_act, a_ot)
                outc.tick(e.dma_start(
                    out=y_out[wc * TC + tt * P:wc * TC + (tt + 1) * P, :],
                    in_=otok))
            SY(_)
        pfree("ps_t1", act)
        pfree("ps_t2", act)
        gu_state[("otok_done", wc)] = outc.n

    head_state = {}
    rbs = bscb[:][0:1, 0:2 * TC]

    def emit_head_stats_mid(cn):
        hA = [hidt[:][:, 0:TC], hidt[:][:, TC:2 * TC]]
        hB = [hidt[:][:, 2 * TC:3 * TC], hidt[:][:, 3 * TC:4 * TC]]
        c0 = cn * TC
        dd_guard = gu_state.get(("ddone", cn - 3), 0)
        pe_slot = {}
        for i in range(DC):
            src = xT[:][:, i * T + c0:i * T + c0 + TC]

            def _(e, i=i, src=src, dd_guard=dd_guard):
                if i == 0:
                    e.wait_ge(sem_act, act_u_chunk[cn])
                    e.wait_ge(sem_wsc, wsc_all)
                    if dd_guard:
                        e.wait_ge(sem_pe, dd_guard)
                w = pe_slot.get(i - 2)
                if w:
                    e.wait_ge(sem_pe, w)
                e.activation(out=hA[i % 2], in_=src, func=AF.Copy)
                act.tick(e.activation(out=hB[i % 2], in_=src, func=AF.Square))
            ACT(_)
            a_i = act.n

            def _(e, i=i, a_i=a_i):
                e.wait_ge(sem_act, a_i)
                if i == 0:
                    pwait(e, "ps_s1")
                    pwait(e, "ps_s2")
                e.matmul(ps_s1[0:1, :], onesb[:], hA[i % 2],
                         start=(i == 0), stop=(i == DC - 1))
                pe.tick(e.matmul(ps_s2[0:1, :], onesb[:], hB[i % 2],
                                 start=(i == 0), stop=(i == DC - 1)))
            TEN(_)
            pe_slot[i] = pe.n
        pe_stats = pe.n
        gu_state[("midstats_pe", cn)] = pe.n

        def _(e, pe_stats=pe_stats):
            e.wait_ge(sem_pe, pe_stats)
            e.copy(out=rows_s, in_=ps_s1[0:1, :])
            e.copy(out=rows_q, in_=ps_s2[0:1, :])
            e.mul(out=rows_s, in_=rows_s, mul=1.0 / DM)
            e.mul(out=rows_q, in_=rows_q, mul=1.0 / DM)
            act.tick(e.activation(out=rows_a1, in_=rows_s, func=AF.Square))
        ACT(_)
        pfree("ps_s1", act)
        pfree("ps_s2", act)
        a_m = act.n

        def _(e, a_m=a_m):
            e.wait_ge(sem_act, a_m)
            dve.tick(e.tensor_sub(rows_a2, rows_q, rows_a1))
        DVE(_)
        d_v = dve.n

        def _(e, d_v=d_v):
            e.wait_ge(sem_dve, d_v)
            e.activation(out=rows_a1, in_=rows_a2, func=AF.Ln,
                         bias=cst_eps[0:1, :])
            act.tick(e.activation(out=rows_a1, in_=rows_a1, func=AF.Exp,
                                  scale=-0.5))
        ACT(_)
        a_r = act.n

        def _(e, a_r=a_r):
            e.wait_ge(sem_act, a_r)
            dve.tick(e.tensor_mul(rows_a2, rows_s, rows_a1))
        DVE(_)
        d_mr = dve.n
        bb_g = head_state.get((cn - 1, "gp_bbc"), 0)
        cb_g = head_state.get((cn - 1, "sy_cbc"), 0)

        def _(e, d_mr=d_mr, bb_g=bb_g, cb_g=cb_g):
            e.wait_ge(sem_dve, d_mr)
            if bb_g:
                e.wait_ge(sem_bbc, bb_g)
            if cb_g:
                e.wait_ge(sem_cbc, cb_g)
            e.activation(out=rbs[:, 0:TC], in_=rows_a1, func=AF.Copy)
            act.tick(e.activation(out=rbs[:, TC:2 * TC], in_=rows_a2,
                                  func=AF.Copy))
        ACT(_)
        gu_state[("midrows_act", cn)] = act.n

    def emit_head_bcast(cn, scan_mark):
        mid_a = gu_state[("midrows_act", cn)]

        def _(e, mid_a=mid_a, scan_mark=scan_mark):
            e.wait_ge(sem_act, mid_a)
            e.wait_ge(sem_dve, scan_mark)
            pwait(e, "ps_p")
            pwait(e, "ps_d")
            e.matmul(ps_p[:], onesrow[:], rbs[:, 0:TC], start=True, stop=True)
            pe.tick(e.matmul(ps_d[:], onesrow[:], rbs[:, TC:2 * TC],
                             start=True, stop=True))
        TEN(_)
        pe_b = pe.n

        def _(e, pe_b=pe_b):
            e.wait_ge(sem_pe, pe_b)
            e.copy(out=rstd2_bc, in_=ps_p[:])
            act.tick(e.copy(out=mr2_bc, in_=ps_d[:]))
        ACT(_)
        pfree("ps_p", act)
        pfree("ps_d", act)
        bc1 = act.n
        c0 = cn * TC

        def _(e, bc1=bc1, c0=c0):
            e.wait_ge(sem_act, bc1)
            for dc in range(DC):
                e.tensor_mul(t1s, xT[:][:, dc * T + c0:dc * T + c0 + TC],
                             rstd2_bc)
                ins = e.tensor_sub(unb[:][:, dc * TC:(dc + 1) * TC],
                                   t1s, mr2_bc)
                if dc == DC - 1:
                    dve.tick(ins)
        DVE(_)
        head_state[(cn, "dve_unorm")] = dve.n


    def emit_head_stats(c, dve_chunk_end):
        c0 = c * TC
        bc1 = emit_stats_rows_bcast(
            lambda i: xT[:][:, i * T + c0:i * T + c0 + TC],
            [(sem_act, act_u_chunk[c]), (sem_dve, dve_chunk_end)])

        def _(e, bc1=bc1, c0=c0):
            e.wait_ge(sem_act, bc1)
            for dc in range(DC):
                e.tensor_mul(t1s, xT[:][:, dc * T + c0:dc * T + c0 + TC],
                             rstd_bc)
                ins = e.tensor_sub(unb[:][:, dc * TC:(dc + 1) * TC],
                                   t1s, mr_bc)
                if dc == DC - 1:
                    dve.tick(ins)
        DVE(_)
        head_state[(c, "dve_unorm")] = dve.n

    def emit_head_proj(c, dve_chunk_end):
        dve_unorm = head_state[(c, "dve_unorm")]
        act_now = act.n

        def _(e, dve_unorm=dve_unorm, act_now=act_now):
            e.wait_ge(sem_dve, dve_unorm)
            e.wait_ge(sem_gp, gp_small)
            e.wait_ge(sem_act, act_now)
            pwait(e, "ps_p")
            for dc in range(DC):
                ins = e.matmul(ps_p[0:R, :], wxdtTb[:, dc * R:(dc + 1) * R],
                               unb[:][:, dc * TC:(dc + 1) * TC],
                               start=(dc == 0), stop=(dc == DC - 1))
                if dc == DC - 1:
                    pe.tick(ins)
        TEN(_)
        pe_xdt = pe.n

        def _(e, pe_xdt=pe_xdt):
            e.wait_ge(sem_pe, pe_xdt)
            act.tick(e.activation(out=xdtb[:], in_=ps_p[0:R, :],
                                  func=AF.Identity, bias=bxdtc[:]))
        ACT(_)
        pfree("ps_p", act)
        head_state[(c, "act_xdt")] = act.n

        def _(e, dve_unorm=dve_unorm):
            e.wait_ge(sem_dve, dve_unorm)
            pwait(e, "ps_p")
            pwait(e, "ps_d")
            for dc in range(DC):
                e.matmul(ps_p[0:NST, :], wbTb[:, dc * NST:(dc + 1) * NST],
                         unb[:][:, dc * TC:(dc + 1) * TC],
                         start=(dc == 0), stop=(dc == DC - 1))
            for dc in range(DC):
                ins = e.matmul(ps_d[0:NST, :], wcTb[:, dc * NST:(dc + 1) * NST],
                               unb[:][:, dc * TC:(dc + 1) * TC],
                               start=(dc == 0), stop=(dc == DC - 1))
                if dc == DC - 1:
                    pe.tick(ins)
        TEN(_)
        pe_bcproj = pe.n

        def _(e, pe_bcproj=pe_bcproj):
            e.wait_ge(sem_pe, pe_bcproj)
            e.activation(out=bscb[:, 0:TC], in_=ps_p[0:NST, :],
                         func=AF.Identity, bias=bbcol[:])
            act.tick(e.activation(out=bscb[:, TC:2 * TC], in_=ps_d[0:NST, :],
                                  func=AF.Identity, bias=bccol[:]))
        ACT(_)
        pfree("ps_p", act)
        pfree("ps_d", act)
        act_bc = act.n

        dve_prev_reader = dve_chunk_end if c > 0 else 0

        def _(e, act_bc=act_bc, dve_prev_reader=dve_prev_reader):
            e.wait_ge(sem_act, act_bc)
            if dve_prev_reader:
                e.wait_ge(sem_dve, dve_prev_reader)
            bbcc.tick(e.dma_start(
                out=bbc[:][0:1, :].rearrange("o (n t) -> o n t", t=TC),
                in_=bscb[0:NST, 0:TC]))
            k = 1
            while k < P:
                kk = min(k, P - k)
                e.wait_ge(sem_bbc, bbcc.n)
                bbcc.tick(e.dma_start(out=bbc[:][k:k + kk, :],
                                      in_=bbc[:][0:kk, :]))
                k += kk
        SY(_)
        head_state[(c, "gp_bbc")] = bbcc.n

        def _(e, act_bc=act_bc, dve_prev_reader=dve_prev_reader):
            e.wait_ge(sem_act, act_bc)
            if dve_prev_reader:
                e.wait_ge(sem_dve, dve_prev_reader)
            cbcc.tick(e.dma_start(
                out=cbc[:][0:1, :].rearrange("o (n t) -> o n t", t=TC),
                in_=bscb[0:NST, TC:2 * TC]))
            k = 1
            while k < P:
                kk = min(k, P - k)
                e.wait_ge(sem_cbc, cbcc.n)
                cbcc.tick(e.dma_start(out=cbc[:][k:k + kk, :],
                                      in_=cbc[:][0:kk, :]))
                k += kk
        GP(_)
        head_state[(c, "sy_cbc")] = cbcc.n

    emit_head_stats(0, 0)
    emit_head_proj(0, 0)

    stage = [hidt[:][:, 0:H], hidt[:][:, H:2 * H]]
    cast_jobs = ([("wg", m) for m in range(MC)] + [("wu", m) for m in range(MC)]
                 + [("wd", m) for m in range(DC)])
    wc_marks = {}
    for j, (kind, m) in enumerate(cast_jobs):
        par = j % 2
        wch = wcc[par]
        src = {"wg": wg_in, "wu": wu_in, "wd": wd_in}[kind][m]
        width = H if kind == "wd" else DM
        sbuf = stage[par][:, 0:width]

        def _(e, j=j, wch=wch, src=src, sbuf=sbuf):
            if j >= 2:
                e.wait_ge(sem_wsc, (j - 1) * 16)
            wch.tick(e.dma_start(out=sbuf, in_=src))
        GP(_)
        wc_marks[j] = wch.n

    def emit_cast_stores(j0, j1):
        for j in range(j0, min(j1, len(cast_jobs))):
            kind, m = cast_jobs[j]
            par = j % 2
            wch = wcc[par]
            dst = {"wg": wg_scr, "wu": wu_scr, "wd": wd_scr}[kind][m]
            width = H if kind == "wd" else DM
            sbuf = stage[par][:, 0:width]

            def _(e, wch=wch, wc_mark=wc_marks[j], dst=dst, sbuf=sbuf):
                e.wait_ge(wch.sem, wc_mark)
                wscc.tick(e.dma_start(out=dst, in_=sbuf))
            SY(_)
    WSC_ALL_N = len(cast_jobs) * 16
    wsc_all = WSC_ALL_N

    dve_chunk_end = 0
    for c in range(NCH):
        c0 = c * TC
        slot = slotv(c)
        ybfv = slot
        slot_guard_pe = max(gu_state.get(("gudone", c - 2), 0),
                            gu_state.get(("ddone", c - 2), 0))
        slot_guard_out = 0
        dve_unorm = head_state[(c, "dve_unorm")]
        act_xdt = head_state[(c, "act_xdt")]
        gp_bbc = head_state[(c, "gp_bbc")]
        sy_cbc = head_state[(c, "sy_cbc")]
        dve_prev_reader = dve_chunk_end if c > 0 else 0

        t_scans_prev = 0
        t_ym0_prev = 0
        dve_scan_prev = 0
        for dc in range(DC):
            if c >= 1 and dc < 6:
                emit_gu_batch(c - 1, 4 * dc, 4 * dc + 4)
            dtsl = dtb[:][:, (dc % 2) * TC:(dc % 2 + 1) * TC]
            A0R = abar_h0 if dc % 2 == 0 else hbuf
            A1R = abar_h1
            OUT0R = hbuf if dc % 2 == 0 else abar_h0

            def _(e, dc=dc, act_xdt=act_xdt):
                e.wait_ge(sem_act, act_xdt)
                pwait(e, "ps_d")
                pe.tick(e.matmul(ps_d[:], wdtTb[:][:, dc * P:(dc + 1) * P],
                                 xdtb[:], start=True, stop=True))
            TEN(_)
            pe_dt = pe.n

            def _(e, dc=dc, c=c, pe_dt=pe_dt, dtsl=dtsl, A0R=A0R, A1R=A1R,
                  t_scans_prev=t_scans_prev, t_ym0_prev=t_ym0_prev,
                  dve_prev_reader=dve_prev_reader):
                e.wait_ge(sem_pe, pe_dt)
                if dc >= 1:
                    e.wait_ge(sem_dve, t_scans_prev)
                elif dve_prev_reader:
                    e.wait_ge(sem_dve, dve_prev_reader)
                e.activation(out=etx, in_=ps_d[:], func=AF.Exp,
                             bias=bdtp[:][:, dc:dc + 1])
                e.activation(out=dtsl, in_=etx, func=AF.Ln, bias=cst_one)
                for n in range(HNST, NST):
                    e.activation(
                        out=A1R[:, (n - HNST) * TC:(n - HNST + 1) * TC],
                        in_=dtsl, func=AF.Exp,
                        scale=aneg[:][:, dc * NST + n:dc * NST + n + 1])
                if dc >= 1:
                    e.wait_ge(sem_dve, t_ym0_prev)
                for n in range(HNST):
                    ins = e.activation(
                        out=A0R[:, n * TC:(n + 1) * TC],
                        in_=dtsl, func=AF.Exp,
                        scale=aneg[:][:, dc * NST + n:dc * NST + n + 1])
                    if n == HNST - 1:
                        act.tick(ins)
            ACT(_)
            pfree("ps_d", act)
            act_abar = act.n

            ysl = ybfv[:, dc * TC:(dc + 1) * TC]

            def _(e, dc=dc, c=c, act_abar=act_abar, gp_bbc=gp_bbc,
                  sy_cbc=sy_cbc, dtsl=dtsl, ysl=ysl, A0R=A0R, A1R=A1R,
                  OUT0R=OUT0R, slot_guard_pe=slot_guard_pe,
                  slot_guard_out=slot_guard_out):
                e.wait_ge(sem_act, act_abar)
                e.wait_ge(sem_bbc, gp_bbc)
                av0 = A0R.rearrange("p (n t) -> p n t", t=TC)
                av1 = A1R.rearrange("p (n t) -> p n t", t=TC)
                if c > 0:
                    hl = hlast[:][:, dc * NST:(dc + 1) * NST]
                    e.tensor_mul(fix16[:][:, 0:HNST], av0[:, :, 0],
                                 hl[:, 0:HNST])
                    e.tensor_mul(fix16[:][:, HNST:NST], av1[:, :, 0],
                                 hl[:, HNST:NST])
                dtu = t1s
                e.tensor_mul(dtu, dtsl, unb[:][:, dc * TC:(dc + 1) * TC])
                dtu_b = dtu.unsqueeze(1).broadcast_to([P, NST, TC])
                bbc_v = bbc[:].rearrange("p (n t) -> p n t", t=TC)
                xbuf_v = xbuf.rearrange("p (n t) -> p n t", t=TC)
                e.tensor_tensor(out=xbuf_v, in0=dtu_b, in1=bbc_v, op=ALU.mult)
                if c > 0:
                    e.tensor_tensor(out=xbuf_v[:, :, 0], in0=xbuf_v[:, :, 0],
                                    in1=fix16[:], op=ALU.add)
                e.memset(av0[:, :, 0:1], 0.0)
                dve.tick(e.memset(av1[:, :, 0:1], 0.0))
                e.wait_ge(sem_dve, dve.n)
                e.tensor_tensor_scan(
                    out=OUT0R, data0=A0R, data1=xbuf[:, 0:HNST * TC],
                    initial=0.0, op0=ALU.mult, op1=ALU.add)
                dve.tick(e.tensor_tensor_scan(
                    out=A0R, data0=A1R, data1=xbuf[:, HNST * TC:],
                    initial=0.0, op0=ALU.mult, op1=ALU.add))
                e.wait_ge(sem_cbc, sy_cbc)
                if dc == 0:
                    if slot_guard_pe:
                        e.wait_ge(sem_pe, slot_guard_pe)
                    if slot_guard_out:
                        e.wait_ge(sem_out, slot_guard_out)
                hv0 = OUT0R.rearrange("p (n t) -> p n t", t=TC)
                cb0 = cbc[:][:, 0:HNST * TC].rearrange("p (n t) -> p n t", t=TC)
                xb0 = xbuf[:, 0:HNST * TC]
                e.tensor_tensor(
                    out=xb0.rearrange("p (n t) -> p n t", t=TC),
                    in0=hv0, in1=cb0, op=ALU.mult)
                dve.tick(e.tensor_copy(
                    out=hlast[:][:, dc * NST:dc * NST + HNST],
                    in_=hv0[:, :, TC - 1]))
                hv1 = A0R.rearrange("p (n t) -> p n t", t=TC)
                cb1 = cbc[:][:, HNST * TC:].rearrange("p (n t) -> p n t", t=TC)
                xb1 = xbuf[:, HNST * TC:]
                e.tensor_tensor(out=xb1.rearrange("p (n t) -> p n t", t=TC),
                                in0=hv1, in1=cb1, op=ALU.mult)
                vin, width = xbuf, NST * TC
                while width > 2 * TC:
                    vv = vin.rearrange("p (j two t) -> p j two t", two=2, t=TC)
                    outw = width // 2
                    vout = xbuf[:, 0:outw]
                    vo = vout.rearrange("p (j t) -> p j t", t=TC)
                    e.tensor_tensor(out=vo, in0=vv[:, :, 0, :],
                                    in1=vv[:, :, 1, :], op=ALU.add)
                    vin, width = vout, outw
                vv = vin.rearrange("p (j two t) -> p j two t", two=2, t=TC)
                e.tensor_tensor(out=ysl, in0=vv[:, 0, 0, :],
                                in1=vv[:, 0, 1, :], op=ALU.add)
                e.tensor_copy(
                    out=hlast[:][:, dc * NST + HNST:(dc + 1) * NST],
                    in_=hv1[:, :, TC - 1])
                dve.tick(e.scalar_tensor_tensor(
                    out=ysl, in0=unb[:][:, dc * TC:(dc + 1) * TC],
                    scalar=dskp[:][:, dc:dc + 1], in1=ysl,
                    op0=ALU.mult, op1=ALU.add))
            DVE(_)
            dve_scan_prev = dve.n
            t_ym0_prev = dve.n - 1
            t_scans_prev = dve.n - 2
            if c == 0 and dc >= 2:
                emit_u_tile(2 * dc)
                emit_u_tile(2 * dc + 1)
                emit_cast_stores((dc - 2) * 10, (dc - 1) * 10)
            if c + 1 < NCH and dc == (7 if c == 0 else 2):
                emit_head_stats_mid(c + 1)
            if c >= 1 and dc == 6:
                emit_down_part1(c - 1, gu_state[("gudone", c - 1)])
            if c >= 1 and dc == 7:
                emit_down_part1b(c - 1)
        dve_mamba = dve.n

        if c >= 1:
            emit_down_part2a_pre(c - 1)
            emit_down_part2a(c - 1)
        if c + 1 < NCH:
            emit_head_bcast(c + 1, t_scans_prev)
            emit_head_proj(c + 1, dve_mamba)

        bc2 = emit_stats_rows_bcast(
            lambda i: ybfv[:, i * TC:(i + 1) * TC],
            [(sem_dve, dve_mamba)], src_bf16=True)

        def _(e, bc2=bc2, c0=c0, ybfv=ybfv, slot=slot):
            e.wait_ge(sem_act, bc2)
            for dc in range(DC):
                e.tensor_mul(t1s, ybfv[:, dc * TC:(dc + 1) * TC], rstd_bc)
                e.tensor_sub(t1s, t1s, mr_bc)
                xsl = xT[:][:, dc * T + c0:dc * T + c0 + TC]
                e.tensor_tensor(out=xsl, in0=xsl, in1=t1s, op=ALU.add)
                ins = e.tensor_copy(out=slot[:, dc * TC:(dc + 1) * TC],
                                    in_=xsl)
                if dc == DC - 1:
                    dve.tick(ins)
        DVE(_)
        dve_x2 = dve.n

        bc3 = emit_stats_rows_bcast(
            lambda i: slot[:, i * TC:(i + 1) * TC],
            [(sem_dve, dve_x2)], src_bf16=True)

        def _(e, bc3=bc3, slot=slot):
            e.wait_ge(sem_act, bc3)
            for dc in range(DC):
                e.tensor_mul(t1s, slot[:, dc * TC:(dc + 1) * TC], rstd_bc)
                ins = e.tensor_sub(slot[:, dc * TC:(dc + 1) * TC], t1s, mr_bc)
                if dc == DC - 1:
                    dve.tick(ins)
        DVE(_)
        dve_chunk_end = dve.n
        gu_state["x2n_ready"] = dve.n

        if c >= 1:
            emit_down_part2out(c - 1)

    emit_gu_batch(NCH - 1, 0, MC)
    emit_down_part1(NCH - 1, gu_state[("gudone", NCH - 1)])
    emit_down_part1b(NCH - 1)
    emit_down_part2a_pre(NCH - 1)
    emit_down_part2a(NCH - 1)
    emit_down_part2out(NCH - 1)

    def _(e):
        e.wait_ge(sem_out, outc.n)
    SY(_)

    es.close()
    return nc


_PROG = None


def _get_prog():
    global _PROG
    if _PROG is None:
        _PROG = build_program()
    return _PROG


def _prep_host(inputs):
    W_xdt = inputs["L0_attn_W_xdt"]
    W_dt = inputs["L0_attn_W_dt"]
    W_B = inputs["L0_attn_W_B"]
    W_C = inputs["L0_attn_W_C"]
    A_log = inputs["L0_attn_A_log"]
    b_dt = inputs["L0_attn_b_dt"]
    D = inputs["L0_attn_D"]
    wg = inputs["L0_ffn_w_gate"]
    wu = inputs["L0_ffn_w_up"]
    wd = inputs["L0_ffn_w_down"]

    def dchunk_cols(a2d, inner):
        return np.ascontiguousarray(
            a2d.reshape(DC, P, inner).transpose(1, 0, 2).reshape(P, DC * inner))

    wxdtT = dchunk_cols(np.ascontiguousarray(W_xdt.T), R).astype(np.float32)
    wdtT = np.ascontiguousarray(W_dt.T).astype(np.float32)
    wbT = dchunk_cols(np.ascontiguousarray(W_B.T), NST).astype(np.float32)
    wcT = dchunk_cols(np.ascontiguousarray(W_C.T), NST).astype(np.float32)
    alog_p = dchunk_cols(A_log, NST).astype(np.float32)
    bdt_p = np.ascontiguousarray(b_dt.reshape(DC, P).T).astype(np.float32)
    d_p = np.ascontiguousarray(D.reshape(DC, P).T).astype(np.float32)

    def prep_lhsT(aT, kc, mc):
        a4 = aT.reshape(kc, P, mc, P)
        return np.ascontiguousarray(
            a4.transpose(2, 1, 0, 3).reshape(mc, P, kc * P))

    wg_prep = prep_lhsT(np.ascontiguousarray(wg.T), DC, MC).astype(np.int32)
    wu_prep = prep_lhsT(np.ascontiguousarray(wu.T), DC, MC).astype(np.int32)
    wd_prep = prep_lhsT(np.ascontiguousarray(wd.T), MC, DC).astype(np.int32)

    return dict(
        wg_prep=wg_prep, wu_prep=wu_prep, wd_prep=wd_prep,
        wxdtT=wxdtT, wdtT=wdtT, wbT=wbT, wcT=wcT, alog_p=alog_p,
        bdt_p=bdt_p, d_p=d_p,
        bxdt=np.ascontiguousarray(
            inputs["L0_attn_b_xdt"].reshape(R, 1)).astype(np.float32),
        bb=np.ascontiguousarray(
            inputs["L0_attn_b_B"].reshape(NST, 1)).astype(np.float32),
        bc=np.ascontiguousarray(
            inputs["L0_attn_b_C"].reshape(NST, 1)).astype(np.float32),
        idf=np.eye(P, dtype=np.float32),
        ones_b=np.ones((P, 1), dtype=BF),
        ones_row=np.ones((1, P), dtype=BF),
    )


def kernel(**inputs):
    u = np.asarray(inputs["L0_attn_u"], dtype=np.float32)
    common = _prep_host({k: np.asarray(v) for k, v in inputs.items()})
    nc = _get_prog()
    in_maps = [dict(common, u=np.ascontiguousarray(u[b])) for b in range(8)]
    res = run_bass_kernel_spmd(nc, in_maps, core_ids=list(range(8)))
    return np.stack([np.asarray(r["y_out"], dtype=np.float32)
                     for r in res.results], axis=0)


# version 22
# speedup vs baseline: 1.0880x; 1.0085x over previous
import contextlib
import numpy as np
import ml_dtypes
import concourse.bass as bass
import concourse.mybir as mybir
from concourse.bass_utils import run_bass_kernel_spmd

F32 = mybir.dt.float32
BF16 = mybir.dt.bfloat16
I32 = mybir.dt.int32
AF = mybir.ActivationFunctionType
ALU = mybir.AluOpType
BF = ml_dtypes.bfloat16

P = 128
T = 2048
DM = 1024
DC = 8
NST = 16
HNST = NST // 2
R = 64
H = 3072
MC = 24
TC = 512
NCH = T // TC
FH = TC // 2
S127 = float(np.float32(1.0) / np.float32(127.0))
EPS = 1e-5


class Chain:
    def __init__(self, sem, step=1):
        self.sem, self.step, self.n = sem, step, 0

    def tick(self, ins):
        ins.then_inc(self.sem, self.step)
        self.n += self.step
        return self.n


def build_program():
    nc = bass.Bass("TRN2", target_bir_lowering=False)

    u_in = nc.dram_tensor("u", [T, DM], F32, kind="ExternalInput").ap()
    wg_in = nc.dram_tensor("wg_prep", [MC, P, DM], I32, kind="ExternalInput").ap()
    wu_in = nc.dram_tensor("wu_prep", [MC, P, DM], I32, kind="ExternalInput").ap()
    wd_in = nc.dram_tensor("wd_prep", [DC, P, H], I32, kind="ExternalInput").ap()
    wxdtT_in = nc.dram_tensor("wxdtT", [P, DC * R], F32, kind="ExternalInput").ap()
    wdtT_in = nc.dram_tensor("wdtT", [R, DM], F32, kind="ExternalInput").ap()
    wbT_in = nc.dram_tensor("wbT", [P, DC * NST], F32, kind="ExternalInput").ap()
    wcT_in = nc.dram_tensor("wcT", [P, DC * NST], F32, kind="ExternalInput").ap()
    alog_in = nc.dram_tensor("alog_p", [P, DC * NST], F32, kind="ExternalInput").ap()
    bdt_in = nc.dram_tensor("bdt_p", [P, DC], F32, kind="ExternalInput").ap()
    dsk_in = nc.dram_tensor("d_p", [P, DC], F32, kind="ExternalInput").ap()
    bxdt_in = nc.dram_tensor("bxdt", [R, 1], F32, kind="ExternalInput").ap()
    bb_in = nc.dram_tensor("bb", [NST, 1], F32, kind="ExternalInput").ap()
    bc_in = nc.dram_tensor("bc", [NST, 1], F32, kind="ExternalInput").ap()
    idf_in = nc.dram_tensor("idf", [P, P], F32, kind="ExternalInput").ap()
    onesb_in = nc.dram_tensor("ones_b", [P, 1], BF16, kind="ExternalInput").ap()
    onesrow_in = nc.dram_tensor("ones_row", [1, P], BF16, kind="ExternalInput").ap()

    wg_scr = nc.dram_tensor("wg_scr", [MC, P, DM], BF16, kind="Internal").ap()
    wu_scr = nc.dram_tensor("wu_scr", [MC, P, DM], BF16, kind="Internal").ap()
    wd_scr = nc.dram_tensor("wd_scr", [DC, P, H], BF16, kind="Internal").ap()
    hid_hbm = nc.dram_tensor("hid_scr", [MC, P, TC], BF16, kind="Internal").ap()
    y_out = nc.dram_tensor("y_out", [T, DM], F32, kind="ExternalOutput").ap()

    es = contextlib.ExitStack()

    def sb(name, shape, dtype):
        return es.enter_context(nc.sbuf_tensor(name, shape, dtype))

    def psum(name, shape, dtype=F32):
        return es.enter_context(nc.psum_tensor(name, shape, dtype))

    xT = sb("xT", [P, DC * T], F32)
    x2n = sb("x2n", [P, 2 * DC * TC], BF16)
    dtb = sb("dtb", [P, 2 * TC], BF16)
    unb = sb("unb", [P, DC * TC], BF16)
    mam = sb("mam", [P, (2 * NST + HNST) * TC], BF16)
    bbc = sb("bbc", [P, NST * TC], BF16)
    cbc = sb("cbc", [P, NST * TC], BF16)
    aux = sb("aux", [P, 2 * TC], BF16)
    rows = sb("rows", [1, 3 * TC], F32)
    rowb = sb("rowb", [1, TC], BF16)
    rbc = sb("rbc", [P, 2 * TC], BF16)
    smw = sb("smw", [P, 2 * DC * NST + DC * R], BF16)
    wdtTb = sb("wdtTb", [R, DM], BF16)
    xdtb = sb("xdtb", [R, TC], BF16)
    bscb = sb("bscb", [NST, 2 * TC], BF16)
    aneg = sb("aneg", [P, DC * NST], F32)
    bdtp = sb("bdtp", [P, DC], F32)
    dskp = sb("dskp", [P, DC], F32)
    bxdtc = sb("bxdtc", [R, 1], F32)
    bbcol = sb("bbcol", [NST, 1], F32)
    bccol = sb("bccol", [NST, 1], F32)
    idf = sb("idf_sb", [P, P], F32)
    onesb = sb("onesb_sb", [P, 1], BF16)
    onesrow = sb("onesrow_sb", [1, P], BF16)
    hlast = sb("hlast", [P, DC * NST], BF16)
    fix16 = sb("fix16", [P, NST], BF16)
    cst = sb("cst", [P, 2], F32)
    hidt = sb("hidt", [P, MC * FH], BF16)
    wgu = sb("wgu", [P, 4 * DM], BF16)
    sgub = sb("sgub", [P, 4 * TC], BF16)
    hidm = sb("hidm", [P, 2 * TC], BF16)
    wdm2 = wgu[:][:, 0:H]

    auxv = aux[:]
    t1s = auxv[:, 0:TC]
    etx = auxv[:, TC:2 * TC]

    smwv = smw[:]
    wbTb = smwv[:, 0:DC * NST]
    wcTb = smwv[:, DC * NST:2 * DC * NST]
    wxdtTb = smwv[:, 2 * DC * NST:2 * DC * NST + DC * R]

    mamx = mam[:]
    abar = mamx[:, 0:NST * TC]
    abar_h0 = mamx[:, 0:HNST * TC]
    abar_h1 = mamx[:, HNST * TC:NST * TC]
    xbuf = mamx[:, NST * TC:2 * NST * TC]
    hbuf = mamx[:, 2 * NST * TC:(2 * NST + HNST) * TC]
    rstd2_bc = mamx[:, HNST * TC:HNST * TC + TC]
    mr2_bc = mamx[:, HNST * TC + TC:HNST * TC + 2 * TC]
    uld = mam[:].bitcast(F32)[:, 0:2 * DM]

    def slotv(c):
        return x2n[:][:, (c % 2) * DC * TC:((c % 2) + 1) * DC * TC]


    ps_g = psum("ps_g", [P, TC])
    ps_u = psum("ps_u", [P, TC])
    ps_d = psum("ps_d", [P, TC])
    ps_s1 = psum("ps_s1", [P, TC])
    ps_s2 = psum("ps_s2", [P, TC])
    ps_t1f = psum("ps_t1", [P, TC])
    ps_t2f = psum("ps_t2", [P, TC])
    ps_p = psum("ps_p", [P, TC])
    ps_t1 = ps_t1f[:, 0:P]
    ps_t2 = ps_t2f[:, 0:P]

    def sem(name):
        return es.enter_context(nc.semaphore(name))

    sem_pe, sem_act, sem_dve, sem_po = sem("s_pe"), sem("s_act"), sem("s_dve"), sem("s_po")
    sem_sy, sem_gp = sem("s_sy"), sem("s_gp")
    sem_uld0, sem_uld1 = sem("s_uld0"), sem("s_uld1")
    sem_bbc, sem_cbc = sem("s_bbc"), sem("s_cbc")
    sem_wc0, sem_wc1 = sem("s_wc0"), sem("s_wc1")
    sem_wsc = sem("s_wsc")
    sem_wv0, sem_wv1 = sem("s_wv0"), sem("s_wv1")
    sem_wd = sem("s_wd")
    sem_wd2 = sem("s_wd2")
    sem_sgub = sem("s_sgub")
    sem_hidt = sem("s_hidt")
    sem_out = sem("s_out")
    blk = es.enter_context(nc.Block())

    pe, act, dve, po = Chain(sem_pe), Chain(sem_act), Chain(sem_dve), Chain(sem_po)
    sy, gp = Chain(sem_sy, 16), Chain(sem_gp, 16)
    uldc = [Chain(sem_uld0, 16), Chain(sem_uld1, 16)]
    bbcc, cbcc = Chain(sem_bbc, 16), Chain(sem_cbc, 16)
    wcc = [Chain(sem_wc0, 16), Chain(sem_wc1, 16)]
    wscc = Chain(sem_wsc, 16)
    wvc = [Chain(sem_wv0, 16), Chain(sem_wv1, 16)]
    wdc = Chain(sem_wd, 16)
    wdc2 = Chain(sem_wd2, 16)
    sguc = Chain(sem_sgub, 16)
    hidtc = Chain(sem_hidt, 16)
    outc = Chain(sem_out, 16)

    TEN, ACT, DVE, SY, GP = (blk.tensor, blk.scalar, blk.vector, blk.sync,
                             blk.gpsimd)
    POOL = blk.gpsimd

    def load_act_set(e, set_id):
        e.add_instruction(mybir.InstLoadActFuncSet(
            name=nc.get_next_instruction_name(), act_func_set_id=set_id,
            ins=[], outs=[]))

    psum_free = {}

    def pwait(e, name):
        ch_m = psum_free.get(name)
        if ch_m:
            e.wait_ge(ch_m[0].sem, ch_m[1])

    def pfree(name, chain):
        psum_free[name] = (chain, chain.n)

    rows_s = rows[:][:, 0:TC]
    rows_q = rows[:][:, TC:2 * TC]
    rows_a1 = rows[:][:, 2 * TC:3 * TC]
    rows_a2 = rows_q
    rowb_1 = rowb[:][:, 0:TC]
    rowb2 = hidm[:].bitcast(BF16)[0:1, TC:2 * TC]
    rstd_bc = rbc[:][:, 0:TC]
    mr_bc = rbc[:][:, TC:2 * TC]
    cst_eps = cst[:][:, 0:1]
    cst_one = cst[:][:, 1:2]

    def _(e):
        for ap_out, ap_in in [(idf[:], idf_in), (onesb[:], onesb_in),
                              (onesrow[:], onesrow_in), (bdtp[:], bdt_in),
                              (dskp[:], dsk_in), (bxdtc[:], bxdt_in),
                              (bbcol[:], bb_in), (bccol[:], bc_in)]:
            sy.tick(e.dma_start(out=ap_out, in_=ap_in))
    SY(_)
    sy_setup = sy.n

    def _(e):
        gp.tick(e.dma_start(out=wbTb, in_=wbT_in))
        gp.tick(e.dma_start(out=wcTb, in_=wcT_in))
        gp.tick(e.dma_start(out=wxdtTb, in_=wxdtT_in))
        gp.tick(e.dma_start(out=wdtTb[:], in_=wdtT_in))
        gp.tick(e.dma_start(out=aneg[:], in_=alog_in))
    GP(_)
    gp_small = gp.n

    def _(e):
        load_act_set(e, 6)
        e.wait_ge(sem_gp, gp_small)
        act.tick(e.activation(out=aneg[:], in_=aneg[:], func=AF.Exp))
    ACT(_)
    act_aexp = act.n

    def _(e):
        e.wait_ge(sem_act, act_aexp)
        e.memset(cst_eps, EPS)
        e.memset(cst_one, 1.0)
        dve.tick(e.tensor_scalar_mul(aneg[:], aneg[:], -1.0))
    DVE(_)

    u_v = u_in.rearrange("(tt p) d -> p tt d", p=P)
    pe_tile_marks = {}
    act_u_chunk = {}
    uld2 = wgu[:].bitcast(F32)

    def emit_u_tile(tt):
        par = tt % 2
        base = uld if tt < 4 else uld2
        buf = base[:, par * DM:(par + 1) * DM]
        uch = uldc[par]

        def _(e, tt=tt, buf=buf, uch=uch):
            e.wait_ge(sem_sy, sy_setup)
            m = pe_tile_marks.get(tt - 2, 0)
            if m:
                e.wait_ge(sem_pe, m)
            uch.tick(e.dma_start(out=buf, in_=u_v[:, tt, :]))
        SY(_)
        u_mark = uch.n
        act_base = act.n

        def _(e, tt=tt, buf=buf, u_mark=u_mark, uch=uch, act_base=act_base):
            e.wait_ge(uch.sem, u_mark)
            if tt > 0:
                e.wait_ge(sem_act, act_base)
            for dc in range(DC):
                pst = ps_t1 if dc % 2 == 0 else ps_t2
                if dc >= 2:
                    e.wait_ge(sem_act, act_base + dc - 1)
                pe.tick(e.transpose(pst[:], buf[:, dc * P:(dc + 1) * P], idf[:]))
        TEN(_)
        pe_tile_marks[tt] = pe.n
        pe_base = pe.n - DC

        def _(e, tt=tt, pe_base=pe_base):
            for dc in range(DC):
                pst = ps_t1 if dc % 2 == 0 else ps_t2
                e.wait_ge(sem_pe, pe_base + dc + 1)
                act.tick(e.copy(
                    out=xT[:][:, dc * T + tt * P:dc * T + (tt + 1) * P],
                    in_=pst[:]))
        ACT(_)
        if tt % 4 == 3:
            act_u_chunk[tt // 4] = act.n
        pfree("ps_t1", act)
        pfree("ps_t2", act)

    for tt in range(4):
        emit_u_tile(tt)

    def emit_stats_rows_bcast(src_getter, extra_marks, src_bf16=False):
        pe_slot = {}
        for i in range(DC):
            src = src_getter(i)
            qslot = dtb[:][:, (i % 2) * TC:(i % 2 + 1) * TC]

            def _(e, i=i, src=src, qslot=qslot):
                if i == 0:
                    for s_, v in extra_marks:
                        if v > 0:
                            e.wait_ge(s_, v)
                w = pe_slot.get(i - 2 if src_bf16 else i - 1)
                if w:
                    e.wait_ge(sem_pe, w)
                if not src_bf16:
                    e.activation(out=etx, in_=src, func=AF.Copy)
                act.tick(e.activation(out=qslot, in_=src, func=AF.Square))
            ACT(_)
            a_i = act.n

            def _(e, i=i, src=src, qslot=qslot, a_i=a_i):
                e.wait_ge(sem_act, a_i)
                if i == 0:
                    pwait(e, "ps_s1")
                    pwait(e, "ps_s2")
                e.matmul(ps_s1[0:1, :], onesb[:], src if src_bf16 else etx,
                         start=(i == 0), stop=(i == DC - 1))
                pe.tick(e.matmul(ps_s2[0:1, :], onesb[:], qslot,
                                 start=(i == 0), stop=(i == DC - 1)))
            TEN(_)
            pe_slot[i] = pe.n
        pe_stats = pe.n

        def _(e, pe_stats=pe_stats):
            e.wait_ge(sem_pe, pe_stats)
            e.copy(out=rows_s, in_=ps_s1[0:1, :])
            e.copy(out=rows_q, in_=ps_s2[0:1, :])
            e.mul(out=rows_s, in_=rows_s, mul=1.0 / DM)
            e.mul(out=rows_q, in_=rows_q, mul=1.0 / DM)
            act.tick(e.activation(out=rows_a1, in_=rows_s, func=AF.Square))
        ACT(_)
        a_m = act.n

        def _(e, a_m=a_m):
            e.wait_ge(sem_act, a_m)
            dve.tick(e.tensor_sub(rows_a2, rows_q, rows_a1))
        DVE(_)
        d_v = dve.n

        def _(e, d_v=d_v):
            e.wait_ge(sem_dve, d_v)
            e.activation(out=rows_a1, in_=rows_a2, func=AF.Ln,
                         bias=cst_eps[0:1, :])
            act.tick(e.activation(out=rows_a1, in_=rows_a1, func=AF.Exp,
                                  scale=-0.5))
        ACT(_)
        a_r = act.n

        def _(e, a_r=a_r):
            e.wait_ge(sem_act, a_r)
            dve.tick(e.tensor_mul(rows_a2, rows_s, rows_a1))
        DVE(_)
        d_mr = dve.n
        dve_now = dve.n

        def _(e, d_mr=d_mr):
            e.wait_ge(sem_dve, d_mr)
            e.wait_ge(sem_sgub, sguc.n)
            e.activation(out=rowb_1, in_=rows_a1, func=AF.Copy)
            act.tick(e.activation(out=rowb2, in_=rows_a2, func=AF.Copy))
        ACT(_)
        a_rb = act.n

        def _(e, a_rb=a_rb):
            e.wait_ge(sem_act, a_rb)
            pwait(e, "ps_p")
            pwait(e, "ps_s2")
            e.matmul(ps_p[:], onesrow[:], rowb_1, start=True, stop=True)
            pe.tick(e.matmul(ps_s2[:], onesrow[:], rowb2, start=True,
                             stop=True))
        TEN(_)
        pe_b1 = pe.n

        def _(e, pe_b1=pe_b1, dve_now=dve_now):
            e.wait_ge(sem_pe, pe_b1)
            e.wait_ge(sem_dve, dve_now)
            e.copy(out=rstd_bc, in_=ps_p[:])
            act.tick(e.copy(out=mr_bc, in_=ps_s2[:]))
        ACT(_)
        pfree("ps_p", act)
        pfree("ps_s2", act)
        return act.n

    gu_state = {"x2n_ready": 0}
    DNPS = [(ps_g, "ps_g"), (ps_u, "ps_u"), (ps_t1f, "ps_t1"),
            (ps_t2f, "ps_t2")]

    def emit_gu_batch(wc, m0, m1):
        for m in range(m0, m1):
            par = m % 2
            wch = wvc[par]
            wg_sb = wgu[:][:, par * 2 * DM:par * 2 * DM + DM]
            wu_sb = wgu[:][:, par * 2 * DM + DM:(par + 1) * 2 * DM]
            psg, psg_n = (ps_g, "ps_g") if par == 0 else (ps_t1f, "ps_t1")
            psu, psu_n = (ps_u, "ps_u") if par == 0 else (ps_t2f, "ps_t2")
            sg_sb = sgub[:][:, par * 2 * TC:par * 2 * TC + TC]
            ub_sb = sgub[:][:, par * 2 * TC + TC:(par + 1) * 2 * TC]

            pre = gu_state.get(("wvmark", wc, m))
            if pre is None:
                def _(e, m=m, wc=wc, wch=wch, wg_sb=wg_sb, wu_sb=wu_sb):
                    e.wait_ge(sem_wsc, wsc_all)
                    if wc == 0 and m < 2:
                        e.wait_ge(sem_act, act_u_chunk[3])
                    mm = gu_state.get(("pe_w", m - 2))
                    if mm:
                        e.wait_ge(sem_pe, mm)
                    mm2 = gu_state.get("down_done")
                    if mm2 and m < 2:
                        e.wait_ge(sem_pe, mm2)
                    wch.tick(e.dma_start(out=wg_sb, in_=wg_scr[m]))
                    wch.tick(e.dma_start(out=wu_sb, in_=wu_scr[m]))
                SY(_)
                wv_mark = wch.n
            else:
                wv_mark = pre

            def _(e, m=m, wc=wc, wch=wch, wv_mark=wv_mark, wg_sb=wg_sb,
                  wu_sb=wu_sb, psg=psg, psu=psu, psg_n=psg_n, psu_n=psu_n):
                e.wait_ge(wch.sem, wv_mark)
                e.wait_ge(sem_dve, gu_state["x2n_ready"])
                pwait(e, psg_n)
                pwait(e, psu_n)
                sl = slotv(wc)
                for k in range(DC):
                    e.matmul(psg[:], wg_sb[:, k * P:(k + 1) * P],
                             sl[:, k * TC:(k + 1) * TC],
                             start=(k == 0), stop=(k == DC - 1))
                for k in range(DC):
                    ins = e.matmul(psu[:], wu_sb[:, k * P:(k + 1) * P],
                                   sl[:, k * TC:(k + 1) * TC],
                                   start=(k == 0), stop=(k == DC - 1))
                    if k == DC - 1:
                        pe.tick(ins)
            TEN(_)
            gu_state[("pe_w", m)] = pe.n
            pe_gu = pe.n

            def _(e, m=m, m0=m0, m1=m1, wc=wc, pe_gu=pe_gu, psg=psg, psu=psu,
                  sg_sb=sg_sb, ub_sb=ub_sb):
                if m == m0:
                    load_act_set(e, 18)
                e.wait_ge(sem_pe, pe_gu)
                mm = gu_state.get(("hidtt", m - 2))
                if mm:
                    e.wait_ge(sem_po, mm)
                e.activation(out=sg_sb, in_=psg[:], func=AF.Silu, scale=S127)
                act.tick(e.activation(out=ub_sb, in_=psu[:], func=AF.Copy,
                                      scale=S127))
                if m == m1 - 1:
                    load_act_set(e, 6)
            ACT(_)
            pfree(psg_n, act)
            pfree(psu_n, act)
            a_gu = act.n

            def _(e, m=m, a_gu=a_gu, sg_sb=sg_sb, ub_sb=ub_sb):
                e.wait_ge(sem_act, a_gu)
                mm = gu_state.get(("hidst", m - 2))
                if mm:
                    e.wait_ge(sem_sgub, mm)
                par = m % 2
                po.tick(e.tensor_mul(hidm[:][:, par * TC:(par + 1) * TC],
                                     sg_sb, ub_sb))
            POOL(_)
            gu_state[("hidtt", m)] = po.n
            po_h = po.n

            def _(e, m=m, po_h=po_h):
                e.wait_ge(sem_po, po_h)
                par = m % 2
                sguc.tick(e.dma_start(out=hid_hbm[m],
                                      in_=hidm[:][:, par * TC:(par + 1) * TC]))
            GP(_)
            gu_state[("sgst", m)] = sguc.n
            gu_state[("hidst", m)] = sguc.n
        gu_state[("gudone", wc)] = pe.n

    def emit_down_mgroup(dn, fh, m2):
        wc = dn["wc"]
        dwnb, sq1 = dn["dwnb"], dn["sq1"]
        sy_hidt = dn["sy_hidt"]
        half_marks, st_marks, a_dn_marks = (dn["half_marks"], dn["st_marks"],
                                            dn["a_dn_marks"])
        psd, psd_n = DNPS[m2 % 4]
        HC = MC // 2
        HCW = HC * P
        for h in range(2):
            wch2 = wdc if h == 0 else wdc2
            prev = half_marks.get((fh, m2 - 1, h))
            pe_now = pe.n

            def _(e, m2=m2, fh=fh, h=h, wch2=wch2, prev=prev, pe_now=pe_now):
                if prev is not None:
                    e.wait_ge(sem_pe, prev)
                elif fh > 0 or gu_state.get("wd_used"):
                    e.wait_ge(sem_pe, pe_now)
                wch2.tick(e.dma_start(
                    out=wdm2[:, h * HCW:(h + 1) * HCW],
                    in_=wd_scr[m2][:, h * HCW:(h + 1) * HCW]))
            GP(_)
        gu_state["wd_used"] = True
        for h in range(2):
            wch2 = wdc if h == 0 else wdc2
            gp_wd = wch2.n

            def _(e, m2=m2, fh=fh, h=h, sy_hidt=sy_hidt, gp_wd=gp_wd,
                  psd=psd, psd_n=psd_n, wch2=wch2):
                if h == 0:
                    e.wait_ge(sem_hidt, sy_hidt)
                    pwait(e, psd_n)
                e.wait_ge(wch2.sem, gp_wd)
                for k2 in range(HC):
                    kk = h * HC + k2
                    ins = e.matmul(psd[:, 0:FH],
                                   wdm2[:, kk * P:(kk + 1) * P],
                                   hidt[:][:, kk * FH:(kk + 1) * FH],
                                   start=(kk == 0), stop=(kk == MC - 1))
                    if k2 == HC - 1:
                        pe.tick(ins)
            TEN(_)
            half_marks[(fh, m2, h)] = pe.n
        pe_dn = pe.n
        if m2 >= 2:
            emit_dn_stats(dn, fh, m2 - 2)
        st_guard = st_marks.get((fh, m2 - 2), 0)
        po_guard = dn["po_guard"]
        sg_done = dn["sg_done"]

        def _(e, m2=m2, fh=fh, pe_dn=pe_dn, psd=psd,
              po_guard=po_guard, st_guard=st_guard, sg_done=sg_done):
            e.wait_ge(sem_pe, pe_dn)
            if m2 == 0 and fh == 0:
                if po_guard:
                    e.wait_ge(sem_pe, po_guard)
                e.wait_ge(sem_sgub, sg_done)
            if st_guard:
                e.wait_ge(sem_pe, st_guard)
            dsl = dwnb[:, m2 * TC + fh * FH:m2 * TC + fh * FH + FH]
            e.activation(out=dsl, in_=psd[:, 0:FH], func=AF.Copy,
                         scale=S127)
            act.tick(e.activation(
                out=sq1[:, (m2 % 2) * FH:(m2 % 2 + 1) * FH],
                in_=psd[:, 0:FH], func=AF.Square, scale=S127))
        ACT(_)
        pfree(psd_n, act)
        a_dn_marks[(fh, m2)] = act.n

    def emit_dn_stats(dn, fh, k):
        dwnb, sq1 = dn["dwnb"], dn["sq1"]

        def _(e, k=k, fh=fh, a_dn=dn["a_dn_marks"][(fh, k)]):
            e.wait_ge(sem_act, a_dn)
            if k == 0 and fh == 0:
                pwait(e, "ps_s1")
                pwait(e, "ps_s2")
            e.matmul(ps_s1[0:1, fh * FH:(fh + 1) * FH], onesb[:],
                     dwnb[:, k * TC + fh * FH:k * TC + fh * FH + FH],
                     start=(k == 0), stop=(k == DC - 1))
            pe.tick(e.matmul(
                ps_s2[0:1, fh * FH:(fh + 1) * FH], onesb[:],
                sq1[:, (k % 2) * FH:(k % 2 + 1) * FH],
                start=(k == 0), stop=(k == DC - 1)))
        TEN(_)
        dn["st_marks"][(fh, k)] = pe.n

    def emit_dn_hidt(dn, fh):
        sg_done = dn["sg_done"]
        pe_hidt_guard = pe.n if fh == 1 else max(
            gu_state.get("down_done", 0),
            gu_state.get(("midstats_pe", dn["wc"] + 2), 0))
        out_guard = gu_state.get(("otok_done", dn["wc"] - 1), 0) if fh == 0 \
            else 0

        def _(e, fh=fh, sg_done=sg_done, pe_hidt_guard=pe_hidt_guard,
              out_guard=out_guard):
            e.wait_ge(sem_sgub, sg_done)
            if out_guard:
                e.wait_ge(sem_out, out_guard)
            if pe_hidt_guard:
                e.wait_ge(sem_pe, pe_hidt_guard)
            hidtc.tick(e.dma_start(
                out=hidt[:].rearrange("p (a t) -> p a t", t=FH),
                in_=hid_hbm[:, :, fh * FH:(fh + 1) * FH].transpose([1, 0, 2])))
        GP(_)
        dn["sy_hidt"] = hidtc.n

    def emit_down_part1(wc, po_guard):
        dn = {"wc": wc, "po_guard": po_guard,
              "dwnb": slotv(wc), "sq1": hidm[:],
              "sg_done": gu_state[("sgst", MC - 1)],
              "half_marks": {}, "st_marks": {}, "a_dn_marks": {}}
        gu_state[("dn", wc)] = dn
        emit_dn_hidt(dn, 0)
        for m2 in range(4):
            emit_down_mgroup(dn, 0, m2)

    def emit_down_part1b(wc):
        dn = gu_state[("dn", wc)]
        for m2 in range(4, DC):
            emit_down_mgroup(dn, 0, m2)
        emit_dn_stats(dn, 0, DC - 2)
        emit_dn_stats(dn, 0, DC - 1)
        emit_dn_hidt(dn, 1)

    def emit_down_part2a_pre(wc):
        dn = gu_state[("dn", wc)]
        for m2 in range(2):
            emit_down_mgroup(dn, 1, m2)

    def emit_down_part2a(wc):
        dn = gu_state[("dn", wc)]
        dwnb = dn["dwnb"]
        for m2 in range(2, DC):
            emit_down_mgroup(dn, 1, m2)
        emit_dn_stats(dn, 1, DC - 2)
        emit_dn_stats(dn, 1, DC - 1)
        gu_state["down_done"] = pe.n
        gu_state[("ddone", wc)] = pe.n
        nwc = wc + 1
        if nwc < NCH:
            for m in (0, 1):
                par = m % 2
                wch = wvc[par]
                wg_sb = wgu[:][:, par * 2 * DM:par * 2 * DM + DM]
                wu_sb = wgu[:][:, par * 2 * DM + DM:(par + 1) * 2 * DM]

                def _(e, m=m, wch=wch, wg_sb=wg_sb, wu_sb=wu_sb,
                      dd=gu_state["down_done"]):
                    e.wait_ge(sem_wsc, wsc_all)
                    e.wait_ge(sem_pe, dd)
                    wch.tick(e.dma_start(out=wg_sb, in_=wg_scr[m]))
                    wch.tick(e.dma_start(out=wu_sb, in_=wu_scr[m]))
                SY(_)
                gu_state[("wvmark", nwc, m)] = wch.n
        pe_stats_dn = pe.n
        def _(e, pe_stats_dn=pe_stats_dn):
            e.wait_ge(sem_pe, pe_stats_dn)
            e.copy(out=rows_s, in_=ps_s1[0:1, :])
            e.copy(out=rows_q, in_=ps_s2[0:1, :])
            e.mul(out=rows_s, in_=rows_s, mul=1.0 / DM)
            e.mul(out=rows_q, in_=rows_q, mul=1.0 / DM)
            act.tick(e.activation(out=rows_a1, in_=rows_s, func=AF.Square))
        ACT(_)
        a_m4 = act.n

        def _(e, a_m4=a_m4):
            e.wait_ge(sem_act, a_m4)
            dve.tick(e.tensor_sub(rows_a2, rows_q, rows_a1))
        DVE(_)
        d_v4 = dve.n

        def _(e, d_v4=d_v4):
            e.wait_ge(sem_dve, d_v4)
            e.activation(out=rows_a1, in_=rows_a2, func=AF.Ln,
                         bias=cst_eps[0:1, :])
            act.tick(e.activation(out=rows_a1, in_=rows_a1, func=AF.Exp,
                                  scale=-0.5))
        ACT(_)
        a_r4 = act.n

        def _(e, a_r4=a_r4):
            e.wait_ge(sem_act, a_r4)
            dve.tick(e.tensor_mul(rows_a2, rows_s, rows_a1))
        DVE(_)
        d_mr4 = dve.n
        dve_now4 = dve.n

        def _(e, d_mr4=d_mr4):
            e.wait_ge(sem_dve, d_mr4)
            e.wait_ge(sem_sgub, sguc.n)
            e.activation(out=rowb_1, in_=rows_a1, func=AF.Copy)
            act.tick(e.activation(out=rowb2, in_=rows_a2, func=AF.Copy))
        ACT(_)
        a_rb4 = act.n

        def _(e, a_rb4=a_rb4):
            e.wait_ge(sem_act, a_rb4)
            pwait(e, "ps_p")
            pwait(e, "ps_s2")
            e.matmul(ps_p[:], onesrow[:], rowb_1, start=True, stop=True)
            pe.tick(e.matmul(ps_s2[:], onesrow[:], rowb2, start=True,
                             stop=True))
        TEN(_)
        pe_b14 = pe.n

        def _(e, pe_b14=pe_b14, dve_now4=dve_now4):
            e.wait_ge(sem_pe, pe_b14)
            e.wait_ge(sem_dve, dve_now4)
            e.copy(out=rstd_bc, in_=ps_p[:])
            act.tick(e.copy(out=mr_bc, in_=ps_s2[:]))
        ACT(_)
        pfree("ps_p", act)
        pfree("ps_s2", act)
        bc4 = act.n

        def _(e, bc4=bc4, wc=wc):
            e.wait_ge(sem_act, bc4)
            for dc in range(DC):
                e.tensor_mul(t1s, dwnb[:, dc * TC:(dc + 1) * TC], rstd_bc)
                e.tensor_sub(t1s, t1s, mr_bc)
                xsl = xT[:][:, dc * T + wc * TC:dc * T + (wc + 1) * TC]
                ins = e.tensor_tensor(out=xsl, in0=xsl, in1=t1s, op=ALU.add)
                if dc == DC - 1:
                    dve.tick(ins)
        DVE(_)
        dve_x3 = dve.n
        gu_state["x3_dve"] = dve_x3


    def emit_down_part2out(wc):
        dn = gu_state[("dn", wc)]
        dve_x3 = gu_state["x3_dve"]
        otok = hidt[:].bitcast(F32)[:, DM:2 * DM]
        for tt in range(TC // P):
            act_base = act.n
            sy_now = outc.n

            def _(e, tt=tt, wc=wc, dve_x3=dve_x3, act_base=act_base):
                e.wait_ge(sem_dve, dve_x3)
                pwait(e, "ps_t1")
                pwait(e, "ps_t2")
                if tt > 0:
                    e.wait_ge(sem_act, act_base)
                for dc in range(DC):
                    pst = ps_t1 if dc % 2 == 0 else ps_t2
                    if dc >= 2:
                        e.wait_ge(sem_act, act_base + dc - 1)
                    pe.tick(e.transpose(
                        pst[:],
                        xT[:][:, dc * T + wc * TC + tt * P:
                              dc * T + wc * TC + (tt + 1) * P],
                        idf[:]))
            TEN(_)
            pe_base = pe.n - DC

            def _(e, tt=tt, pe_base=pe_base, sy_now=sy_now, otok=otok,
                  dd_guard=gu_state[("ddone", wc)]):
                for dc in range(DC):
                    pst = ps_t1 if dc % 2 == 0 else ps_t2
                    e.wait_ge(sem_pe, pe_base + dc + 1)
                    if dc == 0:
                        e.wait_ge(sem_out, sy_now)
                        if tt == 0:
                            e.wait_ge(sem_pe, dd_guard)
                    act.tick(e.copy(out=otok[:, dc * P:(dc + 1) * P],
                                    in_=pst[:]))
            ACT(_)
            a_ot = act.n

            def _(e, tt=tt, wc=wc, a_ot=a_ot, otok=otok):
                e.wait_ge(sem_act, a_ot)
                outc.tick(e.dma_start(
                    out=y_out[wc * TC + tt * P:wc * TC + (tt + 1) * P, :],
                    in_=otok))
            SY(_)
        pfree("ps_t1", act)
        pfree("ps_t2", act)
        gu_state[("otok_done", wc)] = outc.n

    head_state = {}
    rbs = bscb[:][0:1, 0:2 * TC]

    def emit_head_stats_mid(cn):
        hA = [hidt[:][:, 0:TC], hidt[:][:, TC:2 * TC]]
        hB = [hidt[:][:, 2 * TC:3 * TC], hidt[:][:, 3 * TC:4 * TC]]
        c0 = cn * TC
        dd_guard = gu_state.get(("ddone", cn - 3), 0)
        pe_slot = {}
        for i in range(DC):
            src = xT[:][:, i * T + c0:i * T + c0 + TC]

            def _(e, i=i, src=src, dd_guard=dd_guard):
                if i == 0:
                    e.wait_ge(sem_act, act_u_chunk[cn])
                    e.wait_ge(sem_wsc, wsc_all)
                    if dd_guard:
                        e.wait_ge(sem_pe, dd_guard)
                w = pe_slot.get(i - 2)
                if w:
                    e.wait_ge(sem_pe, w)
                e.activation(out=hA[i % 2], in_=src, func=AF.Copy)
                act.tick(e.activation(out=hB[i % 2], in_=src, func=AF.Square))
            ACT(_)
            a_i = act.n

            def _(e, i=i, a_i=a_i):
                e.wait_ge(sem_act, a_i)
                if i == 0:
                    pwait(e, "ps_s1")
                    pwait(e, "ps_s2")
                e.matmul(ps_s1[0:1, :], onesb[:], hA[i % 2],
                         start=(i == 0), stop=(i == DC - 1))
                pe.tick(e.matmul(ps_s2[0:1, :], onesb[:], hB[i % 2],
                                 start=(i == 0), stop=(i == DC - 1)))
            TEN(_)
            pe_slot[i] = pe.n
        pe_stats = pe.n
        gu_state[("midstats_pe", cn)] = pe.n

        def _(e, pe_stats=pe_stats):
            e.wait_ge(sem_pe, pe_stats)
            e.copy(out=rows_s, in_=ps_s1[0:1, :])
            e.copy(out=rows_q, in_=ps_s2[0:1, :])
            e.mul(out=rows_s, in_=rows_s, mul=1.0 / DM)
            e.mul(out=rows_q, in_=rows_q, mul=1.0 / DM)
            act.tick(e.activation(out=rows_a1, in_=rows_s, func=AF.Square))
        ACT(_)
        pfree("ps_s1", act)
        pfree("ps_s2", act)
        a_m = act.n

        def _(e, a_m=a_m):
            e.wait_ge(sem_act, a_m)
            dve.tick(e.tensor_sub(rows_a2, rows_q, rows_a1))
        DVE(_)
        d_v = dve.n

        def _(e, d_v=d_v):
            e.wait_ge(sem_dve, d_v)
            e.activation(out=rows_a1, in_=rows_a2, func=AF.Ln,
                         bias=cst_eps[0:1, :])
            act.tick(e.activation(out=rows_a1, in_=rows_a1, func=AF.Exp,
                                  scale=-0.5))
        ACT(_)
        a_r = act.n

        def _(e, a_r=a_r):
            e.wait_ge(sem_act, a_r)
            dve.tick(e.tensor_mul(rows_a2, rows_s, rows_a1))
        DVE(_)
        d_mr = dve.n
        bb_g = head_state.get((cn - 1, "gp_bbc"), 0)
        cb_g = head_state.get((cn - 1, "sy_cbc"), 0)

        def _(e, d_mr=d_mr, bb_g=bb_g, cb_g=cb_g):
            e.wait_ge(sem_dve, d_mr)
            if bb_g:
                e.wait_ge(sem_bbc, bb_g)
            if cb_g:
                e.wait_ge(sem_cbc, cb_g)
            e.activation(out=rbs[:, 0:TC], in_=rows_a1, func=AF.Copy)
            act.tick(e.activation(out=rbs[:, TC:2 * TC], in_=rows_a2,
                                  func=AF.Copy))
        ACT(_)
        gu_state[("midrows_act", cn)] = act.n

    def emit_head_bcast(cn, scan_mark):
        mid_a = gu_state[("midrows_act", cn)]

        def _(e, mid_a=mid_a, scan_mark=scan_mark):
            e.wait_ge(sem_act, mid_a)
            e.wait_ge(sem_dve, scan_mark)
            pwait(e, "ps_p")
            pwait(e, "ps_d")
            e.matmul(ps_p[:], onesrow[:], rbs[:, 0:TC], start=True, stop=True)
            pe.tick(e.matmul(ps_d[:], onesrow[:], rbs[:, TC:2 * TC],
                             start=True, stop=True))
        TEN(_)
        pe_b = pe.n

        def _(e, pe_b=pe_b):
            e.wait_ge(sem_pe, pe_b)
            e.copy(out=rstd2_bc, in_=ps_p[:])
            act.tick(e.copy(out=mr2_bc, in_=ps_d[:]))
        ACT(_)
        pfree("ps_p", act)
        pfree("ps_d", act)
        bc1 = act.n
        c0 = cn * TC

        def _(e, bc1=bc1, c0=c0):
            e.wait_ge(sem_act, bc1)
            for dc in range(DC):
                e.tensor_mul(t1s, xT[:][:, dc * T + c0:dc * T + c0 + TC],
                             rstd2_bc)
                ins = e.tensor_sub(unb[:][:, dc * TC:(dc + 1) * TC],
                                   t1s, mr2_bc)
                if dc == DC - 1:
                    dve.tick(ins)
        DVE(_)
        head_state[(cn, "dve_unorm")] = dve.n


    def emit_head_stats(c, dve_chunk_end):
        c0 = c * TC
        bc1 = emit_stats_rows_bcast(
            lambda i: xT[:][:, i * T + c0:i * T + c0 + TC],
            [(sem_act, act_u_chunk[c]), (sem_dve, dve_chunk_end)])

        def _(e, bc1=bc1, c0=c0):
            e.wait_ge(sem_act, bc1)
            for dc in range(DC):
                e.tensor_mul(t1s, xT[:][:, dc * T + c0:dc * T + c0 + TC],
                             rstd_bc)
                ins = e.tensor_sub(unb[:][:, dc * TC:(dc + 1) * TC],
                                   t1s, mr_bc)
                if dc == DC - 1:
                    dve.tick(ins)
        DVE(_)
        head_state[(c, "dve_unorm")] = dve.n

    def emit_head_proj(c, dve_chunk_end):
        dve_unorm = head_state[(c, "dve_unorm")]
        act_now = act.n

        def _(e, dve_unorm=dve_unorm, act_now=act_now):
            e.wait_ge(sem_dve, dve_unorm)
            e.wait_ge(sem_gp, gp_small)
            e.wait_ge(sem_act, act_now)
            pwait(e, "ps_p")
            for dc in range(DC):
                ins = e.matmul(ps_p[0:R, :], wxdtTb[:, dc * R:(dc + 1) * R],
                               unb[:][:, dc * TC:(dc + 1) * TC],
                               start=(dc == 0), stop=(dc == DC - 1))
                if dc == DC - 1:
                    pe.tick(ins)
        TEN(_)
        pe_xdt = pe.n

        def _(e, pe_xdt=pe_xdt):
            e.wait_ge(sem_pe, pe_xdt)
            act.tick(e.activation(out=xdtb[:], in_=ps_p[0:R, :],
                                  func=AF.Identity, bias=bxdtc[:]))
        ACT(_)
        pfree("ps_p", act)
        head_state[(c, "act_xdt")] = act.n

        def _(e, dve_unorm=dve_unorm):
            e.wait_ge(sem_dve, dve_unorm)
            pwait(e, "ps_p")
            pwait(e, "ps_d")
            for dc in range(DC):
                e.matmul(ps_p[0:NST, :], wbTb[:, dc * NST:(dc + 1) * NST],
                         unb[:][:, dc * TC:(dc + 1) * TC],
                         start=(dc == 0), stop=(dc == DC - 1))
            for dc in range(DC):
                ins = e.matmul(ps_d[0:NST, :], wcTb[:, dc * NST:(dc + 1) * NST],
                               unb[:][:, dc * TC:(dc + 1) * TC],
                               start=(dc == 0), stop=(dc == DC - 1))
                if dc == DC - 1:
                    pe.tick(ins)
        TEN(_)
        pe_bcproj = pe.n

        def _(e, pe_bcproj=pe_bcproj):
            e.wait_ge(sem_pe, pe_bcproj)
            e.activation(out=bscb[:, 0:TC], in_=ps_p[0:NST, :],
                         func=AF.Identity, bias=bbcol[:])
            act.tick(e.activation(out=bscb[:, TC:2 * TC], in_=ps_d[0:NST, :],
                                  func=AF.Identity, bias=bccol[:]))
        ACT(_)
        pfree("ps_p", act)
        pfree("ps_d", act)
        act_bc = act.n

        dve_prev_reader = dve_chunk_end if c > 0 else 0

        def _(e, act_bc=act_bc, dve_prev_reader=dve_prev_reader):
            e.wait_ge(sem_act, act_bc)
            if dve_prev_reader:
                e.wait_ge(sem_dve, dve_prev_reader)
            bbcc.tick(e.dma_start(
                out=bbc[:][0:1, :].rearrange("o (n t) -> o n t", t=TC),
                in_=bscb[0:NST, 0:TC]))
            k = 1
            while k < P:
                kk = min(k, P - k)
                e.wait_ge(sem_bbc, bbcc.n)
                bbcc.tick(e.dma_start(out=bbc[:][k:k + kk, :],
                                      in_=bbc[:][0:kk, :]))
                k += kk
        SY(_)
        head_state[(c, "gp_bbc")] = bbcc.n

        def _(e, act_bc=act_bc, dve_prev_reader=dve_prev_reader):
            e.wait_ge(sem_act, act_bc)
            if dve_prev_reader:
                e.wait_ge(sem_dve, dve_prev_reader)
            cbcc.tick(e.dma_start(
                out=cbc[:][0:1, :].rearrange("o (n t) -> o n t", t=TC),
                in_=bscb[0:NST, TC:2 * TC]))
            k = 1
            while k < P:
                kk = min(k, P - k)
                e.wait_ge(sem_cbc, cbcc.n)
                cbcc.tick(e.dma_start(out=cbc[:][k:k + kk, :],
                                      in_=cbc[:][0:kk, :]))
                k += kk
        GP(_)
        head_state[(c, "sy_cbc")] = cbcc.n

    emit_head_stats(0, 0)
    emit_head_proj(0, 0)

    stage = [hidt[:][:, 0:H], hidt[:][:, H:2 * H]]
    cast_jobs = ([("wg", m) for m in range(MC)] + [("wu", m) for m in range(MC)]
                 + [("wd", m) for m in range(DC)])
    wc_marks = {}
    for j, (kind, m) in enumerate(cast_jobs):
        par = j % 2
        wch = wcc[par]
        src = {"wg": wg_in, "wu": wu_in, "wd": wd_in}[kind][m]
        width = H if kind == "wd" else DM
        sbuf = stage[par][:, 0:width]

        def _(e, j=j, wch=wch, src=src, sbuf=sbuf):
            if j >= 2:
                e.wait_ge(sem_wsc, (j - 1) * 16)
            wch.tick(e.dma_start(out=sbuf, in_=src))
        GP(_)
        wc_marks[j] = wch.n

    def emit_cast_stores(j0, j1):
        for j in range(j0, min(j1, len(cast_jobs))):
            kind, m = cast_jobs[j]
            par = j % 2
            wch = wcc[par]
            dst = {"wg": wg_scr, "wu": wu_scr, "wd": wd_scr}[kind][m]
            width = H if kind == "wd" else DM
            sbuf = stage[par][:, 0:width]

            def _(e, wch=wch, wc_mark=wc_marks[j], dst=dst, sbuf=sbuf):
                e.wait_ge(wch.sem, wc_mark)
                wscc.tick(e.dma_start(out=dst, in_=sbuf))
            SY(_)
    WSC_ALL_N = len(cast_jobs) * 16
    wsc_all = WSC_ALL_N

    dve_chunk_end = 0
    for c in range(NCH):
        c0 = c * TC
        slot = slotv(c)
        ybfv = slot
        slot_guard_pe = max(gu_state.get(("gudone", c - 2), 0),
                            gu_state.get(("ddone", c - 2), 0))
        slot_guard_out = 0
        dve_unorm = head_state[(c, "dve_unorm")]
        act_xdt = head_state[(c, "act_xdt")]
        gp_bbc = head_state[(c, "gp_bbc")]
        sy_cbc = head_state[(c, "sy_cbc")]
        dve_prev_reader = dve_chunk_end if c > 0 else 0

        t_scans_prev = 0
        t_ym0_prev = 0
        dve_scan_prev = 0
        for dc in range(DC):
            dtsl = dtb[:][:, (dc % 2) * TC:(dc % 2 + 1) * TC]
            A0R = abar_h0 if dc % 2 == 0 else hbuf
            A1R = abar_h1
            OUT0R = hbuf if dc % 2 == 0 else abar_h0

            def _(e, dc=dc, act_xdt=act_xdt):
                e.wait_ge(sem_act, act_xdt)
                pwait(e, "ps_d")
                pe.tick(e.matmul(ps_d[:], wdtTb[:][:, dc * P:(dc + 1) * P],
                                 xdtb[:], start=True, stop=True))
            TEN(_)
            pe_dt = pe.n

            def _(e, dc=dc, c=c, pe_dt=pe_dt, dtsl=dtsl, A0R=A0R, A1R=A1R,
                  t_scans_prev=t_scans_prev, t_ym0_prev=t_ym0_prev,
                  dve_prev_reader=dve_prev_reader):
                e.wait_ge(sem_pe, pe_dt)
                if dc >= 1:
                    e.wait_ge(sem_dve, t_scans_prev)
                elif dve_prev_reader:
                    e.wait_ge(sem_dve, dve_prev_reader)
                e.activation(out=etx, in_=ps_d[:], func=AF.Exp,
                             bias=bdtp[:][:, dc:dc + 1])
                e.activation(out=dtsl, in_=etx, func=AF.Ln, bias=cst_one)
                for n in range(HNST, NST):
                    e.activation(
                        out=A1R[:, (n - HNST) * TC:(n - HNST + 1) * TC],
                        in_=dtsl, func=AF.Exp,
                        scale=aneg[:][:, dc * NST + n:dc * NST + n + 1])
                if dc >= 1:
                    e.wait_ge(sem_dve, t_ym0_prev)
                for n in range(HNST):
                    ins = e.activation(
                        out=A0R[:, n * TC:(n + 1) * TC],
                        in_=dtsl, func=AF.Exp,
                        scale=aneg[:][:, dc * NST + n:dc * NST + n + 1])
                    if n == HNST - 1:
                        act.tick(ins)
            ACT(_)
            pfree("ps_d", act)
            act_abar = act.n

            if c >= 1 and dc < 6:
                emit_gu_batch(c - 1, 4 * dc, 4 * dc + 4)
            if c >= 2 and dc == 0:
                emit_down_part2out(c - 2)

            ysl = ybfv[:, dc * TC:(dc + 1) * TC]

            def _(e, dc=dc, c=c, act_abar=act_abar, gp_bbc=gp_bbc,
                  sy_cbc=sy_cbc, dtsl=dtsl, ysl=ysl, A0R=A0R, A1R=A1R,
                  OUT0R=OUT0R, slot_guard_pe=slot_guard_pe,
                  slot_guard_out=slot_guard_out):
                e.wait_ge(sem_act, act_abar)
                e.wait_ge(sem_bbc, gp_bbc)
                av0 = A0R.rearrange("p (n t) -> p n t", t=TC)
                av1 = A1R.rearrange("p (n t) -> p n t", t=TC)
                if c > 0:
                    hl = hlast[:][:, dc * NST:(dc + 1) * NST]
                    e.tensor_mul(fix16[:][:, 0:HNST], av0[:, :, 0],
                                 hl[:, 0:HNST])
                    e.tensor_mul(fix16[:][:, HNST:NST], av1[:, :, 0],
                                 hl[:, HNST:NST])
                dtu = t1s
                e.tensor_mul(dtu, dtsl, unb[:][:, dc * TC:(dc + 1) * TC])
                dtu_b = dtu.unsqueeze(1).broadcast_to([P, NST, TC])
                bbc_v = bbc[:].rearrange("p (n t) -> p n t", t=TC)
                xbuf_v = xbuf.rearrange("p (n t) -> p n t", t=TC)
                e.tensor_tensor(out=xbuf_v, in0=dtu_b, in1=bbc_v, op=ALU.mult)
                if c > 0:
                    e.tensor_tensor(out=xbuf_v[:, :, 0], in0=xbuf_v[:, :, 0],
                                    in1=fix16[:], op=ALU.add)
                e.memset(av0[:, :, 0:1], 0.0)
                dve.tick(e.memset(av1[:, :, 0:1], 0.0))
                e.wait_ge(sem_dve, dve.n)
                e.tensor_tensor_scan(
                    out=OUT0R, data0=A0R, data1=xbuf[:, 0:HNST * TC],
                    initial=0.0, op0=ALU.mult, op1=ALU.add)
                dve.tick(e.tensor_tensor_scan(
                    out=A0R, data0=A1R, data1=xbuf[:, HNST * TC:],
                    initial=0.0, op0=ALU.mult, op1=ALU.add))
                e.wait_ge(sem_cbc, sy_cbc)
                if dc == 0:
                    if slot_guard_pe:
                        e.wait_ge(sem_pe, slot_guard_pe)
                    if slot_guard_out:
                        e.wait_ge(sem_out, slot_guard_out)
                hv0 = OUT0R.rearrange("p (n t) -> p n t", t=TC)
                cb0 = cbc[:][:, 0:HNST * TC].rearrange("p (n t) -> p n t", t=TC)
                xb0 = xbuf[:, 0:HNST * TC]
                e.tensor_tensor(
                    out=xb0.rearrange("p (n t) -> p n t", t=TC),
                    in0=hv0, in1=cb0, op=ALU.mult)
                dve.tick(e.tensor_copy(
                    out=hlast[:][:, dc * NST:dc * NST + HNST],
                    in_=hv0[:, :, TC - 1]))
                hv1 = A0R.rearrange("p (n t) -> p n t", t=TC)
                cb1 = cbc[:][:, HNST * TC:].rearrange("p (n t) -> p n t", t=TC)
                xb1 = xbuf[:, HNST * TC:]
                e.tensor_tensor(out=xb1.rearrange("p (n t) -> p n t", t=TC),
                                in0=hv1, in1=cb1, op=ALU.mult)
                vin, width = xbuf, NST * TC
                while width > 2 * TC:
                    vv = vin.rearrange("p (j two t) -> p j two t", two=2, t=TC)
                    outw = width // 2
                    vout = xbuf[:, 0:outw]
                    vo = vout.rearrange("p (j t) -> p j t", t=TC)
                    e.tensor_tensor(out=vo, in0=vv[:, :, 0, :],
                                    in1=vv[:, :, 1, :], op=ALU.add)
                    vin, width = vout, outw
                vv = vin.rearrange("p (j two t) -> p j two t", two=2, t=TC)
                e.tensor_tensor(out=ysl, in0=vv[:, 0, 0, :],
                                in1=vv[:, 0, 1, :], op=ALU.add)
                e.tensor_copy(
                    out=hlast[:][:, dc * NST + HNST:(dc + 1) * NST],
                    in_=hv1[:, :, TC - 1])
                dve.tick(e.scalar_tensor_tensor(
                    out=ysl, in0=unb[:][:, dc * TC:(dc + 1) * TC],
                    scalar=dskp[:][:, dc:dc + 1], in1=ysl,
                    op0=ALU.mult, op1=ALU.add))
            DVE(_)
            dve_scan_prev = dve.n
            t_ym0_prev = dve.n - 1
            t_scans_prev = dve.n - 2
            if c == 0 and dc >= 2:
                emit_u_tile(2 * dc)
                emit_u_tile(2 * dc + 1)
                emit_cast_stores((dc - 2) * 10, (dc - 1) * 10)
            if c + 1 < NCH and dc == (7 if c == 0 else 2):
                emit_head_stats_mid(c + 1)
            if c >= 1 and dc == 6:
                emit_down_part1(c - 1, gu_state[("gudone", c - 1)])
            if c >= 1 and dc == 7:
                emit_down_part1b(c - 1)
        dve_mamba = dve.n

        if c >= 1:
            emit_down_part2a_pre(c - 1)
            emit_down_part2a(c - 1)
        if c + 1 < NCH:
            emit_head_bcast(c + 1, t_scans_prev)
            emit_head_proj(c + 1, dve_mamba)

        bc2 = emit_stats_rows_bcast(
            lambda i: ybfv[:, i * TC:(i + 1) * TC],
            [(sem_dve, dve_mamba)], src_bf16=True)

        def _(e, bc2=bc2, c0=c0, ybfv=ybfv, slot=slot):
            e.wait_ge(sem_act, bc2)
            for dc in range(DC):
                e.tensor_mul(t1s, ybfv[:, dc * TC:(dc + 1) * TC], rstd_bc)
                e.tensor_sub(t1s, t1s, mr_bc)
                xsl = xT[:][:, dc * T + c0:dc * T + c0 + TC]
                e.tensor_tensor(out=xsl, in0=xsl, in1=t1s, op=ALU.add)
                ins = e.tensor_copy(out=slot[:, dc * TC:(dc + 1) * TC],
                                    in_=xsl)
                if dc == DC - 1:
                    dve.tick(ins)
        DVE(_)
        dve_x2 = dve.n

        bc3 = emit_stats_rows_bcast(
            lambda i: slot[:, i * TC:(i + 1) * TC],
            [(sem_dve, dve_x2)], src_bf16=True)

        def _(e, bc3=bc3, slot=slot):
            e.wait_ge(sem_act, bc3)
            for dc in range(DC):
                e.tensor_mul(t1s, slot[:, dc * TC:(dc + 1) * TC], rstd_bc)
                ins = e.tensor_sub(slot[:, dc * TC:(dc + 1) * TC], t1s, mr_bc)
                if dc == DC - 1:
                    dve.tick(ins)
        DVE(_)
        dve_chunk_end = dve.n
        gu_state["x2n_ready"] = dve.n

    emit_down_part2out(NCH - 2)
    emit_gu_batch(NCH - 1, 0, MC)
    emit_down_part1(NCH - 1, gu_state[("gudone", NCH - 1)])
    emit_down_part1b(NCH - 1)
    emit_down_part2a_pre(NCH - 1)
    emit_down_part2a(NCH - 1)
    emit_down_part2out(NCH - 1)

    def _(e):
        e.wait_ge(sem_out, outc.n)
    SY(_)

    es.close()
    return nc


_PROG = None


def _get_prog():
    global _PROG
    if _PROG is None:
        _PROG = build_program()
    return _PROG


def _prep_host(inputs):
    W_xdt = inputs["L0_attn_W_xdt"]
    W_dt = inputs["L0_attn_W_dt"]
    W_B = inputs["L0_attn_W_B"]
    W_C = inputs["L0_attn_W_C"]
    A_log = inputs["L0_attn_A_log"]
    b_dt = inputs["L0_attn_b_dt"]
    D = inputs["L0_attn_D"]
    wg = inputs["L0_ffn_w_gate"]
    wu = inputs["L0_ffn_w_up"]
    wd = inputs["L0_ffn_w_down"]

    def dchunk_cols(a2d, inner):
        return np.ascontiguousarray(
            a2d.reshape(DC, P, inner).transpose(1, 0, 2).reshape(P, DC * inner))

    wxdtT = dchunk_cols(np.ascontiguousarray(W_xdt.T), R).astype(np.float32)
    wdtT = np.ascontiguousarray(W_dt.T).astype(np.float32)
    wbT = dchunk_cols(np.ascontiguousarray(W_B.T), NST).astype(np.float32)
    wcT = dchunk_cols(np.ascontiguousarray(W_C.T), NST).astype(np.float32)
    alog_p = dchunk_cols(A_log, NST).astype(np.float32)
    bdt_p = np.ascontiguousarray(b_dt.reshape(DC, P).T).astype(np.float32)
    d_p = np.ascontiguousarray(D.reshape(DC, P).T).astype(np.float32)

    def prep_lhsT(aT, kc, mc):
        a4 = aT.reshape(kc, P, mc, P)
        return np.ascontiguousarray(
            a4.transpose(2, 1, 0, 3).reshape(mc, P, kc * P))

    wg_prep = prep_lhsT(np.ascontiguousarray(wg.T), DC, MC).astype(np.int32)
    wu_prep = prep_lhsT(np.ascontiguousarray(wu.T), DC, MC).astype(np.int32)
    wd_prep = prep_lhsT(np.ascontiguousarray(wd.T), MC, DC).astype(np.int32)

    return dict(
        wg_prep=wg_prep, wu_prep=wu_prep, wd_prep=wd_prep,
        wxdtT=wxdtT, wdtT=wdtT, wbT=wbT, wcT=wcT, alog_p=alog_p,
        bdt_p=bdt_p, d_p=d_p,
        bxdt=np.ascontiguousarray(
            inputs["L0_attn_b_xdt"].reshape(R, 1)).astype(np.float32),
        bb=np.ascontiguousarray(
            inputs["L0_attn_b_B"].reshape(NST, 1)).astype(np.float32),
        bc=np.ascontiguousarray(
            inputs["L0_attn_b_C"].reshape(NST, 1)).astype(np.float32),
        idf=np.eye(P, dtype=np.float32),
        ones_b=np.ones((P, 1), dtype=BF),
        ones_row=np.ones((1, P), dtype=BF),
    )


def kernel(**inputs):
    u = np.asarray(inputs["L0_attn_u"], dtype=np.float32)
    common = _prep_host({k: np.asarray(v) for k, v in inputs.items()})
    nc = _get_prog()
    in_maps = [dict(common, u=np.ascontiguousarray(u[b])) for b in range(8)]
    res = run_bass_kernel_spmd(nc, in_maps, core_ids=list(range(8)))
    return np.stack([np.asarray(r["y_out"], dtype=np.float32)
                     for r in res.results], axis=0)


# version 23
# speedup vs baseline: 1.1201x; 1.0295x over previous
import contextlib
import numpy as np
import ml_dtypes
import concourse.bass as bass
import concourse.mybir as mybir
from concourse.bass_utils import run_bass_kernel_spmd

F32 = mybir.dt.float32
BF16 = mybir.dt.bfloat16
I32 = mybir.dt.int32
AF = mybir.ActivationFunctionType
ALU = mybir.AluOpType
BF = ml_dtypes.bfloat16

P = 128
T = 2048
DM = 1024
DC = 8
NST = 16
HNST = NST // 2
R = 64
H = 3072
MC = 24
TC = 512
NCH = T // TC
FH = TC // 2
S127 = float(np.float32(1.0) / np.float32(127.0))
EPS = 1e-5


class Chain:
    def __init__(self, sem, step=1):
        self.sem, self.step, self.n = sem, step, 0

    def tick(self, ins):
        ins.then_inc(self.sem, self.step)
        self.n += self.step
        return self.n


def build_program():
    nc = bass.Bass("TRN2", target_bir_lowering=False)

    u_in = nc.dram_tensor("u", [T, DM], F32, kind="ExternalInput").ap()
    wg_in = nc.dram_tensor("wg_prep", [MC, P, DM], I32, kind="ExternalInput").ap()
    wu_in = nc.dram_tensor("wu_prep", [MC, P, DM], I32, kind="ExternalInput").ap()
    wd_in = nc.dram_tensor("wd_prep", [DC, P, H], I32, kind="ExternalInput").ap()
    wxdtT_in = nc.dram_tensor("wxdtT", [P, DC * R], F32, kind="ExternalInput").ap()
    wdtT_in = nc.dram_tensor("wdtT", [R, DM], F32, kind="ExternalInput").ap()
    wbT_in = nc.dram_tensor("wbT", [P, DC * NST], F32, kind="ExternalInput").ap()
    wcT_in = nc.dram_tensor("wcT", [P, DC * NST], F32, kind="ExternalInput").ap()
    alog_in = nc.dram_tensor("alog_p", [P, DC * NST], F32, kind="ExternalInput").ap()
    bdt_in = nc.dram_tensor("bdt_p", [P, DC], F32, kind="ExternalInput").ap()
    dsk_in = nc.dram_tensor("d_p", [P, DC], F32, kind="ExternalInput").ap()
    bxdt_in = nc.dram_tensor("bxdt", [R, 1], F32, kind="ExternalInput").ap()
    bb_in = nc.dram_tensor("bb", [NST, 1], F32, kind="ExternalInput").ap()
    bc_in = nc.dram_tensor("bc", [NST, 1], F32, kind="ExternalInput").ap()
    idf_in = nc.dram_tensor("idf", [P, P], F32, kind="ExternalInput").ap()
    onesb_in = nc.dram_tensor("ones_b", [P, 1], BF16, kind="ExternalInput").ap()
    onesrow_in = nc.dram_tensor("ones_row", [1, P], BF16, kind="ExternalInput").ap()

    wg_scr = nc.dram_tensor("wg_scr", [MC, P, DM], BF16, kind="Internal").ap()
    wu_scr = nc.dram_tensor("wu_scr", [MC, P, DM], BF16, kind="Internal").ap()
    wd_scr = nc.dram_tensor("wd_scr", [DC, P, H], BF16, kind="Internal").ap()
    hid_hbm = nc.dram_tensor("hid_scr", [MC, P, TC], BF16, kind="Internal").ap()
    y_out = nc.dram_tensor("y_out", [T, DM], F32, kind="ExternalOutput").ap()

    es = contextlib.ExitStack()

    def sb(name, shape, dtype):
        return es.enter_context(nc.sbuf_tensor(name, shape, dtype))

    def psum(name, shape, dtype=F32):
        return es.enter_context(nc.psum_tensor(name, shape, dtype))

    xT = sb("xT", [P, DC * T], F32)
    x2n = sb("x2n", [P, 2 * DC * TC], BF16)
    dtb = sb("dtb", [P, 2 * TC], BF16)
    unb = sb("unb", [P, DC * TC], BF16)
    mam = sb("mam", [P, (2 * NST + HNST) * TC], BF16)
    bbc = sb("bbc", [P, NST * TC], BF16)
    cbc = sb("cbc", [P, NST * TC], BF16)
    aux = sb("aux", [P, 2 * TC], BF16)
    rows = sb("rows", [1, 3 * TC], F32)
    rowb = sb("rowb", [1, TC], BF16)
    rbc = sb("rbc", [P, 2 * TC], BF16)
    smw = sb("smw", [P, 2 * DC * NST + DC * R], BF16)
    wdtTb = sb("wdtTb", [R, DM], BF16)
    xdtb = sb("xdtb", [R, TC], BF16)
    bscb = sb("bscb", [NST, 2 * TC], BF16)
    aneg = sb("aneg", [P, DC * NST], F32)
    bdtp = sb("bdtp", [P, DC], F32)
    dskp = sb("dskp", [P, DC], F32)
    bxdtc = sb("bxdtc", [R, 1], F32)
    bbcol = sb("bbcol", [NST, 1], F32)
    bccol = sb("bccol", [NST, 1], F32)
    idf = sb("idf_sb", [P, P], F32)
    onesb = sb("onesb_sb", [P, 1], BF16)
    onesrow = sb("onesrow_sb", [1, P], BF16)
    hlast = sb("hlast", [P, DC * NST], BF16)
    fix16 = sb("fix16", [P, NST], BF16)
    cst = sb("cst", [P, 2], F32)
    hidt = sb("hidt", [P, MC * FH], BF16)
    wgu = sb("wgu", [P, 4 * DM], BF16)
    sgub = sb("sgub", [P, 4 * TC], BF16)
    hidm = sb("hidm", [P, 2 * TC], BF16)
    wdm2 = wgu[:][:, 0:H]

    auxv = aux[:]
    t1s = auxv[:, 0:TC]
    etx = auxv[:, TC:2 * TC]

    smwv = smw[:]
    wbTb = smwv[:, 0:DC * NST]
    wcTb = smwv[:, DC * NST:2 * DC * NST]
    wxdtTb = smwv[:, 2 * DC * NST:2 * DC * NST + DC * R]

    mamx = mam[:]
    abar = mamx[:, 0:NST * TC]
    abar_h0 = mamx[:, 0:HNST * TC]
    abar_h1 = mamx[:, HNST * TC:NST * TC]
    xbuf = mamx[:, NST * TC:2 * NST * TC]
    hbuf = mamx[:, 2 * NST * TC:(2 * NST + HNST) * TC]
    rstd2_bc = mamx[:, HNST * TC:HNST * TC + TC]
    mr2_bc = mamx[:, HNST * TC + TC:HNST * TC + 2 * TC]
    uld = mam[:].bitcast(F32)[:, 0:2 * DM]

    def slotv(c):
        return x2n[:][:, (c % 2) * DC * TC:((c % 2) + 1) * DC * TC]


    ps_g = psum("ps_g", [P, TC])
    ps_u = psum("ps_u", [P, TC])
    ps_d = psum("ps_d", [P, TC])
    ps_s1 = psum("ps_s1", [P, TC])
    ps_s2 = psum("ps_s2", [P, TC])
    ps_t1f = psum("ps_t1", [P, TC])
    ps_t2f = psum("ps_t2", [P, TC])
    ps_p = psum("ps_p", [P, TC])
    ps_t1 = ps_t1f[:, 0:P]
    ps_t2 = ps_t2f[:, 0:P]

    def sem(name):
        return es.enter_context(nc.semaphore(name))

    sem_pe, sem_act, sem_dve, sem_po = sem("s_pe"), sem("s_act"), sem("s_dve"), sem("s_po")
    sem_sy, sem_gp = sem("s_sy"), sem("s_gp")
    sem_uld0, sem_uld1 = sem("s_uld0"), sem("s_uld1")
    sem_bbc, sem_cbc = sem("s_bbc"), sem("s_cbc")
    sem_wc0, sem_wc1 = sem("s_wc0"), sem("s_wc1")
    sem_wsc = sem("s_wsc")
    sem_wv0, sem_wv1 = sem("s_wv0"), sem("s_wv1")
    sem_wd = sem("s_wd")
    sem_wd2 = sem("s_wd2")
    sem_sgub = sem("s_sgub")
    sem_hidt = sem("s_hidt")
    sem_out = sem("s_out")
    blk = es.enter_context(nc.Block())

    pe, act, dve, po = Chain(sem_pe), Chain(sem_act), Chain(sem_dve), Chain(sem_po)
    sy, gp = Chain(sem_sy, 16), Chain(sem_gp, 16)
    uldc = [Chain(sem_uld0, 16), Chain(sem_uld1, 16)]
    bbcc, cbcc = Chain(sem_bbc, 16), Chain(sem_cbc, 16)
    wcc = [Chain(sem_wc0, 16), Chain(sem_wc1, 16)]
    wscc = Chain(sem_wsc, 16)
    wvc = [Chain(sem_wv0, 16), Chain(sem_wv1, 16)]
    wdc = Chain(sem_wd, 16)
    wdc2 = Chain(sem_wd2, 16)
    sguc = Chain(sem_sgub, 16)
    hidtc = Chain(sem_hidt, 16)
    outc = Chain(sem_out, 16)

    TEN, ACT, DVE, SY, GP = (blk.tensor, blk.scalar, blk.vector, blk.sync,
                             blk.gpsimd)
    POOL = blk.gpsimd

    def load_act_set(e, set_id):
        e.add_instruction(mybir.InstLoadActFuncSet(
            name=nc.get_next_instruction_name(), act_func_set_id=set_id,
            ins=[], outs=[]))

    psum_free = {}

    def pwait(e, name):
        ch_m = psum_free.get(name)
        if ch_m:
            e.wait_ge(ch_m[0].sem, ch_m[1])

    def pfree(name, chain):
        psum_free[name] = (chain, chain.n)

    rows_s = rows[:][:, 0:TC]
    rows_q = rows[:][:, TC:2 * TC]
    rows_a1 = rows[:][:, 2 * TC:3 * TC]
    rows_a2 = rows_q
    rowb_1 = rowb[:][:, 0:TC]
    rowb2 = hidm[:].bitcast(BF16)[0:1, TC:2 * TC]
    rstd_bc = rbc[:][:, 0:TC]
    mr_bc = rbc[:][:, TC:2 * TC]
    cst_eps = cst[:][:, 0:1]
    cst_one = cst[:][:, 1:2]

    def _(e):
        for ap_out, ap_in in [(idf[:], idf_in), (onesb[:], onesb_in),
                              (onesrow[:], onesrow_in), (bdtp[:], bdt_in),
                              (dskp[:], dsk_in), (bxdtc[:], bxdt_in),
                              (bbcol[:], bb_in), (bccol[:], bc_in)]:
            sy.tick(e.dma_start(out=ap_out, in_=ap_in))
    SY(_)
    sy_setup = sy.n

    def _(e):
        gp.tick(e.dma_start(out=wbTb, in_=wbT_in))
        gp.tick(e.dma_start(out=wcTb, in_=wcT_in))
        gp.tick(e.dma_start(out=wxdtTb, in_=wxdtT_in))
        gp.tick(e.dma_start(out=wdtTb[:], in_=wdtT_in))
        gp.tick(e.dma_start(out=aneg[:], in_=alog_in))
    GP(_)
    gp_small = gp.n

    def _(e):
        load_act_set(e, 6)
        e.wait_ge(sem_gp, gp_small)
        act.tick(e.activation(out=aneg[:], in_=aneg[:], func=AF.Exp))
    ACT(_)
    act_aexp = act.n

    def _(e):
        e.wait_ge(sem_act, act_aexp)
        e.memset(cst_eps, EPS)
        e.memset(cst_one, 1.0)
        dve.tick(e.tensor_scalar_mul(aneg[:], aneg[:], -1.0))
    DVE(_)

    u_v = u_in.rearrange("(tt p) d -> p tt d", p=P)
    pe_tile_marks = {}
    act_u_chunk = {}
    uld2 = wgu[:].bitcast(F32)

    def emit_u_tile(tt):
        par = tt % 2
        base = uld if tt < 4 else uld2
        buf = base[:, par * DM:(par + 1) * DM]
        uch = uldc[par]

        def _(e, tt=tt, buf=buf, uch=uch):
            e.wait_ge(sem_sy, sy_setup)
            m = pe_tile_marks.get(tt - 2, 0)
            if m:
                e.wait_ge(sem_pe, m)
            uch.tick(e.dma_start(out=buf, in_=u_v[:, tt, :]))
        SY(_)
        u_mark = uch.n
        act_base = act.n

        def _(e, tt=tt, buf=buf, u_mark=u_mark, uch=uch, act_base=act_base):
            e.wait_ge(uch.sem, u_mark)
            if tt > 0:
                e.wait_ge(sem_act, act_base)
            for dc in range(DC):
                pst = ps_t1 if dc % 2 == 0 else ps_t2
                if dc >= 2:
                    e.wait_ge(sem_act, act_base + dc - 1)
                pe.tick(e.transpose(pst[:], buf[:, dc * P:(dc + 1) * P], idf[:]))
        TEN(_)
        pe_tile_marks[tt] = pe.n
        pe_base = pe.n - DC

        def _(e, tt=tt, pe_base=pe_base):
            for dc in range(DC):
                pst = ps_t1 if dc % 2 == 0 else ps_t2
                e.wait_ge(sem_pe, pe_base + dc + 1)
                act.tick(e.copy(
                    out=xT[:][:, dc * T + tt * P:dc * T + (tt + 1) * P],
                    in_=pst[:]))
        ACT(_)
        if tt % 4 == 3:
            act_u_chunk[tt // 4] = act.n
        pfree("ps_t1", act)
        pfree("ps_t2", act)

    for tt in range(4):
        emit_u_tile(tt)

    def emit_stats_rows_bcast(src_getter, extra_marks, src_bf16=False):
        pe_slot = {}
        for i in range(DC):
            src = src_getter(i)
            qslot = dtb[:][:, (i % 2) * TC:(i % 2 + 1) * TC]

            def _(e, i=i, src=src, qslot=qslot):
                if i == 0:
                    for s_, v in extra_marks:
                        if v > 0:
                            e.wait_ge(s_, v)
                w = pe_slot.get(i - 2 if src_bf16 else i - 1)
                if w:
                    e.wait_ge(sem_pe, w)
                if not src_bf16:
                    e.activation(out=etx, in_=src, func=AF.Copy)
                act.tick(e.activation(out=qslot, in_=src, func=AF.Square))
            ACT(_)
            a_i = act.n

            def _(e, i=i, src=src, qslot=qslot, a_i=a_i):
                e.wait_ge(sem_act, a_i)
                if i == 0:
                    pwait(e, "ps_s1")
                    pwait(e, "ps_s2")
                e.matmul(ps_s1[0:1, :], onesb[:], src if src_bf16 else etx,
                         start=(i == 0), stop=(i == DC - 1))
                pe.tick(e.matmul(ps_s2[0:1, :], onesb[:], qslot,
                                 start=(i == 0), stop=(i == DC - 1)))
            TEN(_)
            pe_slot[i] = pe.n
        pe_stats = pe.n

        def _(e, pe_stats=pe_stats):
            e.wait_ge(sem_pe, pe_stats)
            e.copy(out=rows_s, in_=ps_s1[0:1, :])
            e.copy(out=rows_q, in_=ps_s2[0:1, :])
            e.mul(out=rows_s, in_=rows_s, mul=1.0 / DM)
            e.mul(out=rows_q, in_=rows_q, mul=1.0 / DM)
            act.tick(e.activation(out=rows_a1, in_=rows_s, func=AF.Square))
        ACT(_)
        a_m = act.n

        def _(e, a_m=a_m):
            e.wait_ge(sem_act, a_m)
            dve.tick(e.tensor_sub(rows_a2, rows_q, rows_a1))
        DVE(_)
        d_v = dve.n

        def _(e, d_v=d_v):
            e.wait_ge(sem_dve, d_v)
            e.activation(out=rows_a1, in_=rows_a2, func=AF.Ln,
                         bias=cst_eps[0:1, :])
            act.tick(e.activation(out=rows_a1, in_=rows_a1, func=AF.Exp,
                                  scale=-0.5))
        ACT(_)
        a_r = act.n

        def _(e, a_r=a_r):
            e.wait_ge(sem_act, a_r)
            dve.tick(e.tensor_mul(rows_a2, rows_s, rows_a1))
        DVE(_)
        d_mr = dve.n
        dve_now = dve.n

        def _(e, d_mr=d_mr):
            e.wait_ge(sem_dve, d_mr)
            e.wait_ge(sem_sgub, sguc.n)
            e.activation(out=rowb_1, in_=rows_a1, func=AF.Copy)
            act.tick(e.activation(out=rowb2, in_=rows_a2, func=AF.Copy))
        ACT(_)
        a_rb = act.n

        def _(e, a_rb=a_rb):
            e.wait_ge(sem_act, a_rb)
            pwait(e, "ps_p")
            pwait(e, "ps_s2")
            e.matmul(ps_p[:], onesrow[:], rowb_1, start=True, stop=True)
            pe.tick(e.matmul(ps_s2[:], onesrow[:], rowb2, start=True,
                             stop=True))
        TEN(_)
        pe_b1 = pe.n

        def _(e, pe_b1=pe_b1, dve_now=dve_now):
            e.wait_ge(sem_pe, pe_b1)
            e.wait_ge(sem_dve, dve_now)
            e.copy(out=rstd_bc, in_=ps_p[:])
            act.tick(e.copy(out=mr_bc, in_=ps_s2[:]))
        ACT(_)
        pfree("ps_p", act)
        pfree("ps_s2", act)
        return act.n

    gu_state = {"x2n_ready": 0}
    DNPS = [(ps_g, "ps_g"), (ps_u, "ps_u"), (ps_t1f, "ps_t1"),
            (ps_t2f, "ps_t2")]

    def emit_gu_batch(wc, m0, m1):
        for m in range(m0, m1):
            par = m % 2
            wch = wvc[par]
            wg_sb = wgu[:][:, par * 2 * DM:par * 2 * DM + DM]
            wu_sb = wgu[:][:, par * 2 * DM + DM:(par + 1) * 2 * DM]
            psg, psg_n = (ps_g, "ps_g") if par == 0 else (ps_t1f, "ps_t1")
            psu, psu_n = (ps_u, "ps_u") if par == 0 else (ps_t2f, "ps_t2")
            sg_sb = sgub[:][:, par * 2 * TC:par * 2 * TC + TC]
            ub_sb = sgub[:][:, par * 2 * TC + TC:(par + 1) * 2 * TC]

            pre = gu_state.get(("wvmark", wc, m))
            if pre is None:
                def _(e, m=m, wc=wc, wch=wch, wg_sb=wg_sb, wu_sb=wu_sb):
                    e.wait_ge(sem_wsc, wsc_all)
                    if wc == 0 and m < 2:
                        e.wait_ge(sem_act, act_u_chunk[3])
                    mm = gu_state.get(("pe_w", m - 2))
                    if mm:
                        e.wait_ge(sem_pe, mm)
                    mm2 = gu_state.get("down_done")
                    if mm2 and m < 2:
                        e.wait_ge(sem_pe, mm2)
                    wch.tick(e.dma_start(out=wg_sb, in_=wg_scr[m]))
                    wch.tick(e.dma_start(out=wu_sb, in_=wu_scr[m]))
                SY(_)
                wv_mark = wch.n
            else:
                wv_mark = pre

            def _(e, m=m, wc=wc, wch=wch, wv_mark=wv_mark, wg_sb=wg_sb,
                  wu_sb=wu_sb, psg=psg, psu=psu, psg_n=psg_n, psu_n=psu_n):
                e.wait_ge(wch.sem, wv_mark)
                e.wait_ge(sem_dve, gu_state["x2n_ready"])
                pwait(e, psg_n)
                pwait(e, psu_n)
                sl = slotv(wc)
                for k in range(DC):
                    e.matmul(psg[:], wg_sb[:, k * P:(k + 1) * P],
                             sl[:, k * TC:(k + 1) * TC],
                             start=(k == 0), stop=(k == DC - 1))
                for k in range(DC):
                    ins = e.matmul(psu[:], wu_sb[:, k * P:(k + 1) * P],
                                   sl[:, k * TC:(k + 1) * TC],
                                   start=(k == 0), stop=(k == DC - 1))
                    if k == DC - 1:
                        pe.tick(ins)
            TEN(_)
            gu_state[("pe_w", m)] = pe.n
            pe_gu = pe.n

            def _(e, m=m, m0=m0, m1=m1, wc=wc, pe_gu=pe_gu, psg=psg, psu=psu,
                  sg_sb=sg_sb, ub_sb=ub_sb):
                if m == m0:
                    load_act_set(e, 18)
                e.wait_ge(sem_pe, pe_gu)
                mm = gu_state.get(("hidtt", m - 2))
                if mm:
                    e.wait_ge(sem_po, mm)
                e.activation(out=sg_sb, in_=psg[:], func=AF.Silu, scale=S127)
                act.tick(e.activation(out=ub_sb, in_=psu[:], func=AF.Copy,
                                      scale=S127))
                if m == m1 - 1:
                    load_act_set(e, 6)
            ACT(_)
            pfree(psg_n, act)
            pfree(psu_n, act)
            a_gu = act.n

            def _(e, m=m, a_gu=a_gu, sg_sb=sg_sb, ub_sb=ub_sb):
                e.wait_ge(sem_act, a_gu)
                mm = gu_state.get(("hidst", m - 2))
                if mm:
                    e.wait_ge(sem_sgub, mm)
                par = m % 2
                po.tick(e.tensor_mul(hidm[:][:, par * TC:(par + 1) * TC],
                                     sg_sb, ub_sb))
            POOL(_)
            gu_state[("hidtt", m)] = po.n
            po_h = po.n

            def _(e, m=m, po_h=po_h):
                e.wait_ge(sem_po, po_h)
                par = m % 2
                sguc.tick(e.dma_start(out=hid_hbm[m],
                                      in_=hidm[:][:, par * TC:(par + 1) * TC]))
            GP(_)
            gu_state[("sgst", m)] = sguc.n
            gu_state[("hidst", m)] = sguc.n
        gu_state[("gudone", wc)] = pe.n

    def emit_down_mgroup(dn, fh, m2):
        wc = dn["wc"]
        dwnb, sq1 = dn["dwnb"], dn["sq1"]
        sy_hidt = dn["sy_hidt"]
        half_marks, st_marks, a_dn_marks = (dn["half_marks"], dn["st_marks"],
                                            dn["a_dn_marks"])
        psd, psd_n = DNPS[m2 % 4]
        HC = MC // 2
        HCW = HC * P
        for h in range(2):
            wch2 = wdc if h == 0 else wdc2
            prev = half_marks.get((fh, m2 - 1, h))
            pe_now = pe.n

            def _(e, m2=m2, fh=fh, h=h, wch2=wch2, prev=prev, pe_now=pe_now):
                if prev is not None:
                    e.wait_ge(sem_pe, prev)
                elif fh > 0 or gu_state.get("wd_used"):
                    e.wait_ge(sem_pe, pe_now)
                wch2.tick(e.dma_start(
                    out=wdm2[:, h * HCW:(h + 1) * HCW],
                    in_=wd_scr[m2][:, h * HCW:(h + 1) * HCW]))
            GP(_)
        gu_state["wd_used"] = True
        for h in range(2):
            wch2 = wdc if h == 0 else wdc2
            gp_wd = wch2.n

            def _(e, m2=m2, fh=fh, h=h, sy_hidt=sy_hidt, gp_wd=gp_wd,
                  psd=psd, psd_n=psd_n, wch2=wch2):
                if h == 0:
                    e.wait_ge(sem_hidt, sy_hidt)
                    pwait(e, psd_n)
                e.wait_ge(wch2.sem, gp_wd)
                for k2 in range(HC):
                    kk = h * HC + k2
                    ins = e.matmul(psd[:, 0:FH],
                                   wdm2[:, kk * P:(kk + 1) * P],
                                   hidt[:][:, kk * FH:(kk + 1) * FH],
                                   start=(kk == 0), stop=(kk == MC - 1))
                    if k2 == HC - 1:
                        pe.tick(ins)
            TEN(_)
            half_marks[(fh, m2, h)] = pe.n
        pe_dn = pe.n
        if m2 >= 2:
            emit_dn_stats(dn, fh, m2 - 2)
        st_guard = st_marks.get((fh, m2 - 2), 0)
        po_guard = dn["po_guard"]
        sg_done = dn["sg_done"]

        def _(e, m2=m2, fh=fh, pe_dn=pe_dn, psd=psd,
              po_guard=po_guard, st_guard=st_guard, sg_done=sg_done):
            e.wait_ge(sem_pe, pe_dn)
            if m2 == 0 and fh == 0:
                if po_guard:
                    e.wait_ge(sem_pe, po_guard)
                e.wait_ge(sem_sgub, sg_done)
            if st_guard:
                e.wait_ge(sem_pe, st_guard)
            dsl = dwnb[:, m2 * TC + fh * FH:m2 * TC + fh * FH + FH]
            e.activation(out=dsl, in_=psd[:, 0:FH], func=AF.Copy,
                         scale=S127)
            act.tick(e.activation(
                out=sq1[:, (m2 % 2) * FH:(m2 % 2 + 1) * FH],
                in_=psd[:, 0:FH], func=AF.Square, scale=S127))
        ACT(_)
        pfree(psd_n, act)
        a_dn_marks[(fh, m2)] = act.n

    def emit_dn_stats(dn, fh, k):
        dwnb, sq1 = dn["dwnb"], dn["sq1"]

        def _(e, k=k, fh=fh, a_dn=dn["a_dn_marks"][(fh, k)]):
            e.wait_ge(sem_act, a_dn)
            if k == 0 and fh == 0:
                pwait(e, "ps_s1")
                pwait(e, "ps_s2")
            e.matmul(ps_s1[0:1, fh * FH:(fh + 1) * FH], onesb[:],
                     dwnb[:, k * TC + fh * FH:k * TC + fh * FH + FH],
                     start=(k == 0), stop=(k == DC - 1))
            pe.tick(e.matmul(
                ps_s2[0:1, fh * FH:(fh + 1) * FH], onesb[:],
                sq1[:, (k % 2) * FH:(k % 2 + 1) * FH],
                start=(k == 0), stop=(k == DC - 1)))
        TEN(_)
        dn["st_marks"][(fh, k)] = pe.n

    def emit_dn_hidt(dn, fh):
        sg_done = dn["sg_done"]
        pe_hidt_guard = pe.n if fh == 1 else max(
            gu_state.get("down_done", 0),
            gu_state.get(("midstats_pe", dn["wc"] + 2), 0))
        out_guard = gu_state.get(("otok_done", dn["wc"] - 1), 0) if fh == 0 \
            else 0

        def _(e, fh=fh, sg_done=sg_done, pe_hidt_guard=pe_hidt_guard,
              out_guard=out_guard):
            e.wait_ge(sem_sgub, sg_done)
            if out_guard:
                e.wait_ge(sem_out, out_guard)
            if pe_hidt_guard:
                e.wait_ge(sem_pe, pe_hidt_guard)
            hidtc.tick(e.dma_start(
                out=hidt[:].rearrange("p (a t) -> p a t", t=FH),
                in_=hid_hbm[:, :, fh * FH:(fh + 1) * FH].transpose([1, 0, 2])))
        GP(_)
        dn["sy_hidt"] = hidtc.n

    def emit_down_part1(wc, po_guard):
        dn = {"wc": wc, "po_guard": po_guard,
              "dwnb": slotv(wc), "sq1": hidm[:],
              "sg_done": gu_state[("sgst", MC - 1)],
              "half_marks": {}, "st_marks": {}, "a_dn_marks": {}}
        gu_state[("dn", wc)] = dn
        emit_dn_hidt(dn, 0)
        for m2 in range(4):
            emit_down_mgroup(dn, 0, m2)

    def emit_down_part1b(wc):
        dn = gu_state[("dn", wc)]
        for m2 in range(4, DC):
            emit_down_mgroup(dn, 0, m2)
        emit_dn_stats(dn, 0, DC - 2)
        emit_dn_stats(dn, 0, DC - 1)
        emit_dn_hidt(dn, 1)

    def emit_down_part1c(wc):
        dn = gu_state[("dn", wc)]
        for m2 in range(DC):
            emit_down_mgroup(dn, 1, m2)
        emit_dn_stats(dn, 1, DC - 2)
        emit_dn_stats(dn, 1, DC - 1)

    def emit_down_part2a(wc):
        dn = gu_state[("dn", wc)]
        dwnb = dn["dwnb"]
        gu_state["down_done"] = pe.n
        gu_state[("ddone", wc)] = pe.n
        nwc = wc + 1
        if nwc < NCH:
            for m in (0, 1):
                par = m % 2
                wch = wvc[par]
                wg_sb = wgu[:][:, par * 2 * DM:par * 2 * DM + DM]
                wu_sb = wgu[:][:, par * 2 * DM + DM:(par + 1) * 2 * DM]

                def _(e, m=m, wch=wch, wg_sb=wg_sb, wu_sb=wu_sb,
                      dd=gu_state["down_done"]):
                    e.wait_ge(sem_wsc, wsc_all)
                    e.wait_ge(sem_pe, dd)
                    wch.tick(e.dma_start(out=wg_sb, in_=wg_scr[m]))
                    wch.tick(e.dma_start(out=wu_sb, in_=wu_scr[m]))
                SY(_)
                gu_state[("wvmark", nwc, m)] = wch.n
        pe_stats_dn = pe.n
        def _(e, pe_stats_dn=pe_stats_dn):
            e.wait_ge(sem_pe, pe_stats_dn)
            e.copy(out=rows_s, in_=ps_s1[0:1, :])
            e.copy(out=rows_q, in_=ps_s2[0:1, :])
            e.mul(out=rows_s, in_=rows_s, mul=1.0 / DM)
            e.mul(out=rows_q, in_=rows_q, mul=1.0 / DM)
            act.tick(e.activation(out=rows_a1, in_=rows_s, func=AF.Square))
        ACT(_)
        a_m4 = act.n

        def _(e, a_m4=a_m4):
            e.wait_ge(sem_act, a_m4)
            dve.tick(e.tensor_sub(rows_a2, rows_q, rows_a1))
        DVE(_)
        d_v4 = dve.n

        def _(e, d_v4=d_v4):
            e.wait_ge(sem_dve, d_v4)
            e.activation(out=rows_a1, in_=rows_a2, func=AF.Ln,
                         bias=cst_eps[0:1, :])
            act.tick(e.activation(out=rows_a1, in_=rows_a1, func=AF.Exp,
                                  scale=-0.5))
        ACT(_)
        a_r4 = act.n

        def _(e, a_r4=a_r4):
            e.wait_ge(sem_act, a_r4)
            dve.tick(e.tensor_mul(rows_a2, rows_s, rows_a1))
        DVE(_)
        d_mr4 = dve.n
        dve_now4 = dve.n

        def _(e, d_mr4=d_mr4):
            e.wait_ge(sem_dve, d_mr4)
            e.wait_ge(sem_sgub, sguc.n)
            e.activation(out=rowb_1, in_=rows_a1, func=AF.Copy)
            act.tick(e.activation(out=rowb2, in_=rows_a2, func=AF.Copy))
        ACT(_)
        a_rb4 = act.n

        def _(e, a_rb4=a_rb4):
            e.wait_ge(sem_act, a_rb4)
            pwait(e, "ps_p")
            pwait(e, "ps_s2")
            e.matmul(ps_p[:], onesrow[:], rowb_1, start=True, stop=True)
            pe.tick(e.matmul(ps_s2[:], onesrow[:], rowb2, start=True,
                             stop=True))
        TEN(_)
        pe_b14 = pe.n

        def _(e, pe_b14=pe_b14, dve_now4=dve_now4):
            e.wait_ge(sem_pe, pe_b14)
            e.wait_ge(sem_dve, dve_now4)
            e.copy(out=rstd_bc, in_=ps_p[:])
            act.tick(e.copy(out=mr_bc, in_=ps_s2[:]))
        ACT(_)
        pfree("ps_p", act)
        pfree("ps_s2", act)
        bc4 = act.n

        def _(e, bc4=bc4, wc=wc):
            e.wait_ge(sem_act, bc4)
            for dc in range(DC):
                e.tensor_mul(t1s, dwnb[:, dc * TC:(dc + 1) * TC], rstd_bc)
                e.tensor_sub(t1s, t1s, mr_bc)
                xsl = xT[:][:, dc * T + wc * TC:dc * T + (wc + 1) * TC]
                ins = e.tensor_tensor(out=xsl, in0=xsl, in1=t1s, op=ALU.add)
                if dc == DC - 1:
                    dve.tick(ins)
        DVE(_)
        dve_x3 = dve.n
        gu_state["x3_dve"] = dve_x3


    def emit_down_part2out(wc):
        dn = gu_state[("dn", wc)]
        dve_x3 = gu_state["x3_dve"]
        otok = hidt[:].bitcast(F32)[:, DM:2 * DM]
        for tt in range(TC // P):
            act_base = act.n
            sy_now = outc.n

            def _(e, tt=tt, wc=wc, dve_x3=dve_x3, act_base=act_base):
                e.wait_ge(sem_dve, dve_x3)
                pwait(e, "ps_t1")
                pwait(e, "ps_t2")
                if tt > 0:
                    e.wait_ge(sem_act, act_base)
                for dc in range(DC):
                    pst = ps_t1 if dc % 2 == 0 else ps_t2
                    if dc >= 2:
                        e.wait_ge(sem_act, act_base + dc - 1)
                    pe.tick(e.transpose(
                        pst[:],
                        xT[:][:, dc * T + wc * TC + tt * P:
                              dc * T + wc * TC + (tt + 1) * P],
                        idf[:]))
            TEN(_)
            pe_base = pe.n - DC

            def _(e, tt=tt, pe_base=pe_base, sy_now=sy_now, otok=otok,
                  dd_guard=gu_state[("ddone", wc)]):
                for dc in range(DC):
                    pst = ps_t1 if dc % 2 == 0 else ps_t2
                    e.wait_ge(sem_pe, pe_base + dc + 1)
                    if dc == 0:
                        e.wait_ge(sem_out, sy_now)
                        if tt == 0:
                            e.wait_ge(sem_pe, dd_guard)
                    act.tick(e.copy(out=otok[:, dc * P:(dc + 1) * P],
                                    in_=pst[:]))
            ACT(_)
            a_ot = act.n

            def _(e, tt=tt, wc=wc, a_ot=a_ot, otok=otok):
                e.wait_ge(sem_act, a_ot)
                outc.tick(e.dma_start(
                    out=y_out[wc * TC + tt * P:wc * TC + (tt + 1) * P, :],
                    in_=otok))
            SY(_)
        pfree("ps_t1", act)
        pfree("ps_t2", act)
        gu_state[("otok_done", wc)] = outc.n

    head_state = {}
    rbs = bscb[:][0:1, 0:2 * TC]

    def emit_head_stats_mid(cn):
        hA = [hidt[:][:, 0:TC], hidt[:][:, TC:2 * TC]]
        hB = [hidt[:][:, 2 * TC:3 * TC], hidt[:][:, 3 * TC:4 * TC]]
        c0 = cn * TC
        dd_guard = gu_state.get(("ddone", cn - 3), 0)
        pe_slot = {}
        for i in range(DC):
            src = xT[:][:, i * T + c0:i * T + c0 + TC]

            def _(e, i=i, src=src, dd_guard=dd_guard):
                if i == 0:
                    e.wait_ge(sem_act, act_u_chunk[cn])
                    e.wait_ge(sem_wsc, wsc_all)
                    if dd_guard:
                        e.wait_ge(sem_pe, dd_guard)
                w = pe_slot.get(i - 2)
                if w:
                    e.wait_ge(sem_pe, w)
                e.activation(out=hA[i % 2], in_=src, func=AF.Copy)
                act.tick(e.activation(out=hB[i % 2], in_=src, func=AF.Square))
            ACT(_)
            a_i = act.n

            def _(e, i=i, a_i=a_i):
                e.wait_ge(sem_act, a_i)
                if i == 0:
                    pwait(e, "ps_s1")
                    pwait(e, "ps_s2")
                e.matmul(ps_s1[0:1, :], onesb[:], hA[i % 2],
                         start=(i == 0), stop=(i == DC - 1))
                pe.tick(e.matmul(ps_s2[0:1, :], onesb[:], hB[i % 2],
                                 start=(i == 0), stop=(i == DC - 1)))
            TEN(_)
            pe_slot[i] = pe.n
        pe_stats = pe.n
        gu_state[("midstats_pe", cn)] = pe.n

        def _(e, pe_stats=pe_stats):
            e.wait_ge(sem_pe, pe_stats)
            e.copy(out=rows_s, in_=ps_s1[0:1, :])
            e.copy(out=rows_q, in_=ps_s2[0:1, :])
            e.mul(out=rows_s, in_=rows_s, mul=1.0 / DM)
            e.mul(out=rows_q, in_=rows_q, mul=1.0 / DM)
            act.tick(e.activation(out=rows_a1, in_=rows_s, func=AF.Square))
        ACT(_)
        pfree("ps_s1", act)
        pfree("ps_s2", act)
        a_m = act.n

        def _(e, a_m=a_m):
            e.wait_ge(sem_act, a_m)
            dve.tick(e.tensor_sub(rows_a2, rows_q, rows_a1))
        DVE(_)
        d_v = dve.n

        def _(e, d_v=d_v):
            e.wait_ge(sem_dve, d_v)
            e.activation(out=rows_a1, in_=rows_a2, func=AF.Ln,
                         bias=cst_eps[0:1, :])
            act.tick(e.activation(out=rows_a1, in_=rows_a1, func=AF.Exp,
                                  scale=-0.5))
        ACT(_)
        a_r = act.n

        def _(e, a_r=a_r):
            e.wait_ge(sem_act, a_r)
            dve.tick(e.tensor_mul(rows_a2, rows_s, rows_a1))
        DVE(_)
        d_mr = dve.n
        bb_g = head_state.get((cn - 1, "gp_bbc"), 0)
        cb_g = head_state.get((cn - 1, "sy_cbc"), 0)

        def _(e, d_mr=d_mr, bb_g=bb_g, cb_g=cb_g):
            e.wait_ge(sem_dve, d_mr)
            if bb_g:
                e.wait_ge(sem_bbc, bb_g)
            if cb_g:
                e.wait_ge(sem_cbc, cb_g)
            e.activation(out=rbs[:, 0:TC], in_=rows_a1, func=AF.Copy)
            act.tick(e.activation(out=rbs[:, TC:2 * TC], in_=rows_a2,
                                  func=AF.Copy))
        ACT(_)
        gu_state[("midrows_act", cn)] = act.n

    def emit_head_bcast(cn, scan_mark):
        mid_a = gu_state[("midrows_act", cn)]

        def _(e, mid_a=mid_a, scan_mark=scan_mark):
            e.wait_ge(sem_act, mid_a)
            e.wait_ge(sem_dve, scan_mark)
            pwait(e, "ps_p")
            pwait(e, "ps_d")
            e.matmul(ps_p[:], onesrow[:], rbs[:, 0:TC], start=True, stop=True)
            pe.tick(e.matmul(ps_d[:], onesrow[:], rbs[:, TC:2 * TC],
                             start=True, stop=True))
        TEN(_)
        pe_b = pe.n

        def _(e, pe_b=pe_b):
            e.wait_ge(sem_pe, pe_b)
            e.copy(out=rstd2_bc, in_=ps_p[:])
            act.tick(e.copy(out=mr2_bc, in_=ps_d[:]))
        ACT(_)
        pfree("ps_p", act)
        pfree("ps_d", act)
        bc1 = act.n
        c0 = cn * TC

        def _(e, bc1=bc1, c0=c0):
            e.wait_ge(sem_act, bc1)
            for dc in range(DC):
                e.tensor_mul(t1s, xT[:][:, dc * T + c0:dc * T + c0 + TC],
                             rstd2_bc)
                ins = e.tensor_sub(unb[:][:, dc * TC:(dc + 1) * TC],
                                   t1s, mr2_bc)
                if dc == DC - 1:
                    dve.tick(ins)
        DVE(_)
        head_state[(cn, "dve_unorm")] = dve.n


    def emit_head_stats(c, dve_chunk_end):
        c0 = c * TC
        bc1 = emit_stats_rows_bcast(
            lambda i: xT[:][:, i * T + c0:i * T + c0 + TC],
            [(sem_act, act_u_chunk[c]), (sem_dve, dve_chunk_end)])

        def _(e, bc1=bc1, c0=c0):
            e.wait_ge(sem_act, bc1)
            for dc in range(DC):
                e.tensor_mul(t1s, xT[:][:, dc * T + c0:dc * T + c0 + TC],
                             rstd_bc)
                ins = e.tensor_sub(unb[:][:, dc * TC:(dc + 1) * TC],
                                   t1s, mr_bc)
                if dc == DC - 1:
                    dve.tick(ins)
        DVE(_)
        head_state[(c, "dve_unorm")] = dve.n

    def emit_head_proj(c, dve_chunk_end):
        dve_unorm = head_state[(c, "dve_unorm")]
        act_now = act.n

        def _(e, dve_unorm=dve_unorm, act_now=act_now):
            e.wait_ge(sem_dve, dve_unorm)
            e.wait_ge(sem_gp, gp_small)
            e.wait_ge(sem_act, act_now)
            pwait(e, "ps_p")
            for dc in range(DC):
                ins = e.matmul(ps_p[0:R, :], wxdtTb[:, dc * R:(dc + 1) * R],
                               unb[:][:, dc * TC:(dc + 1) * TC],
                               start=(dc == 0), stop=(dc == DC - 1))
                if dc == DC - 1:
                    pe.tick(ins)
        TEN(_)
        pe_xdt = pe.n

        def _(e, pe_xdt=pe_xdt):
            e.wait_ge(sem_pe, pe_xdt)
            act.tick(e.activation(out=xdtb[:], in_=ps_p[0:R, :],
                                  func=AF.Identity, bias=bxdtc[:]))
        ACT(_)
        pfree("ps_p", act)
        head_state[(c, "act_xdt")] = act.n

        def _(e, dve_unorm=dve_unorm):
            e.wait_ge(sem_dve, dve_unorm)
            pwait(e, "ps_p")
            pwait(e, "ps_d")
            for dc in range(DC):
                e.matmul(ps_p[0:NST, :], wbTb[:, dc * NST:(dc + 1) * NST],
                         unb[:][:, dc * TC:(dc + 1) * TC],
                         start=(dc == 0), stop=(dc == DC - 1))
            for dc in range(DC):
                ins = e.matmul(ps_d[0:NST, :], wcTb[:, dc * NST:(dc + 1) * NST],
                               unb[:][:, dc * TC:(dc + 1) * TC],
                               start=(dc == 0), stop=(dc == DC - 1))
                if dc == DC - 1:
                    pe.tick(ins)
        TEN(_)
        pe_bcproj = pe.n

        def _(e, pe_bcproj=pe_bcproj):
            e.wait_ge(sem_pe, pe_bcproj)
            e.activation(out=bscb[:, 0:TC], in_=ps_p[0:NST, :],
                         func=AF.Identity, bias=bbcol[:])
            act.tick(e.activation(out=bscb[:, TC:2 * TC], in_=ps_d[0:NST, :],
                                  func=AF.Identity, bias=bccol[:]))
        ACT(_)
        pfree("ps_p", act)
        pfree("ps_d", act)
        act_bc = act.n

        dve_prev_reader = dve_chunk_end if c > 0 else 0

        def _(e, act_bc=act_bc, dve_prev_reader=dve_prev_reader):
            e.wait_ge(sem_act, act_bc)
            if dve_prev_reader:
                e.wait_ge(sem_dve, dve_prev_reader)
            bbcc.tick(e.dma_start(
                out=bbc[:][0:1, :].rearrange("o (n t) -> o n t", t=TC),
                in_=bscb[0:NST, 0:TC]))
            k = 1
            while k < P:
                kk = min(k, P - k)
                e.wait_ge(sem_bbc, bbcc.n)
                bbcc.tick(e.dma_start(out=bbc[:][k:k + kk, :],
                                      in_=bbc[:][0:kk, :]))
                k += kk
        SY(_)
        head_state[(c, "gp_bbc")] = bbcc.n

        def _(e, act_bc=act_bc, dve_prev_reader=dve_prev_reader):
            e.wait_ge(sem_act, act_bc)
            if dve_prev_reader:
                e.wait_ge(sem_dve, dve_prev_reader)
            cbcc.tick(e.dma_start(
                out=cbc[:][0:1, :].rearrange("o (n t) -> o n t", t=TC),
                in_=bscb[0:NST, TC:2 * TC]))
            k = 1
            while k < P:
                kk = min(k, P - k)
                e.wait_ge(sem_cbc, cbcc.n)
                cbcc.tick(e.dma_start(out=cbc[:][k:k + kk, :],
                                      in_=cbc[:][0:kk, :]))
                k += kk
        GP(_)
        head_state[(c, "sy_cbc")] = cbcc.n

    emit_head_stats(0, 0)
    emit_head_proj(0, 0)

    stage = [hidt[:][:, 0:H], hidt[:][:, H:2 * H]]
    cast_jobs = ([("wg", m) for m in range(MC)] + [("wu", m) for m in range(MC)]
                 + [("wd", m) for m in range(DC)])
    wc_marks = {}
    for j, (kind, m) in enumerate(cast_jobs):
        par = j % 2
        wch = wcc[par]
        src = {"wg": wg_in, "wu": wu_in, "wd": wd_in}[kind][m]
        width = H if kind == "wd" else DM
        sbuf = stage[par][:, 0:width]

        def _(e, j=j, wch=wch, src=src, sbuf=sbuf):
            if j >= 2:
                e.wait_ge(sem_wsc, (j - 1) * 16)
            wch.tick(e.dma_start(out=sbuf, in_=src))
        GP(_)
        wc_marks[j] = wch.n

    def emit_cast_stores(j0, j1):
        for j in range(j0, min(j1, len(cast_jobs))):
            kind, m = cast_jobs[j]
            par = j % 2
            wch = wcc[par]
            dst = {"wg": wg_scr, "wu": wu_scr, "wd": wd_scr}[kind][m]
            width = H if kind == "wd" else DM
            sbuf = stage[par][:, 0:width]

            def _(e, wch=wch, wc_mark=wc_marks[j], dst=dst, sbuf=sbuf):
                e.wait_ge(wch.sem, wc_mark)
                wscc.tick(e.dma_start(out=dst, in_=sbuf))
            SY(_)
    WSC_ALL_N = len(cast_jobs) * 16
    wsc_all = WSC_ALL_N

    dve_chunk_end = 0
    for c in range(NCH):
        c0 = c * TC
        slot = slotv(c)
        ybfv = slot
        slot_guard_pe = max(gu_state.get(("gudone", c - 2), 0),
                            gu_state.get(("ddone", c - 2), 0))
        slot_guard_out = 0
        dve_unorm = head_state[(c, "dve_unorm")]
        act_xdt = head_state[(c, "act_xdt")]
        gp_bbc = head_state[(c, "gp_bbc")]
        sy_cbc = head_state[(c, "sy_cbc")]
        dve_prev_reader = dve_chunk_end if c > 0 else 0

        t_scans_prev = 0
        t_ym0_prev = 0
        dve_scan_prev = 0
        for dc in range(DC):
            dtsl = dtb[:][:, (dc % 2) * TC:(dc % 2 + 1) * TC]
            A0R = abar_h0 if dc % 2 == 0 else hbuf
            A1R = abar_h1
            OUT0R = hbuf if dc % 2 == 0 else abar_h0

            def _(e, dc=dc, act_xdt=act_xdt):
                e.wait_ge(sem_act, act_xdt)
                pwait(e, "ps_d")
                pe.tick(e.matmul(ps_d[:], wdtTb[:][:, dc * P:(dc + 1) * P],
                                 xdtb[:], start=True, stop=True))
            TEN(_)
            pe_dt = pe.n

            def _(e, dc=dc, c=c, pe_dt=pe_dt, dtsl=dtsl, A0R=A0R, A1R=A1R,
                  t_scans_prev=t_scans_prev, t_ym0_prev=t_ym0_prev,
                  dve_prev_reader=dve_prev_reader):
                e.wait_ge(sem_pe, pe_dt)
                if dc >= 1:
                    e.wait_ge(sem_dve, t_scans_prev)
                elif dve_prev_reader:
                    e.wait_ge(sem_dve, dve_prev_reader)
                e.activation(out=etx, in_=ps_d[:], func=AF.Exp,
                             bias=bdtp[:][:, dc:dc + 1])
                e.activation(out=dtsl, in_=etx, func=AF.Ln, bias=cst_one)
                for n in range(HNST, NST):
                    e.activation(
                        out=A1R[:, (n - HNST) * TC:(n - HNST + 1) * TC],
                        in_=dtsl, func=AF.Exp,
                        scale=aneg[:][:, dc * NST + n:dc * NST + n + 1])
                if dc >= 1:
                    e.wait_ge(sem_dve, t_ym0_prev)
                for n in range(HNST):
                    ins = e.activation(
                        out=A0R[:, n * TC:(n + 1) * TC],
                        in_=dtsl, func=AF.Exp,
                        scale=aneg[:][:, dc * NST + n:dc * NST + n + 1])
                    if n == HNST - 1:
                        act.tick(ins)
            ACT(_)
            pfree("ps_d", act)
            act_abar = act.n

            if c >= 1 and dc < 5:
                emit_gu_batch(c - 1, 5 * dc, min(MC, 5 * dc + 5))
            if c >= 2 and dc == 0:
                emit_down_part2out(c - 2)

            ysl = ybfv[:, dc * TC:(dc + 1) * TC]

            def _(e, dc=dc, c=c, act_abar=act_abar, gp_bbc=gp_bbc,
                  sy_cbc=sy_cbc, dtsl=dtsl, ysl=ysl, A0R=A0R, A1R=A1R,
                  OUT0R=OUT0R, slot_guard_pe=slot_guard_pe,
                  slot_guard_out=slot_guard_out):
                e.wait_ge(sem_act, act_abar)
                e.wait_ge(sem_bbc, gp_bbc)
                av0 = A0R.rearrange("p (n t) -> p n t", t=TC)
                av1 = A1R.rearrange("p (n t) -> p n t", t=TC)
                if c > 0:
                    hl = hlast[:][:, dc * NST:(dc + 1) * NST]
                    e.tensor_mul(fix16[:][:, 0:HNST], av0[:, :, 0],
                                 hl[:, 0:HNST])
                    e.tensor_mul(fix16[:][:, HNST:NST], av1[:, :, 0],
                                 hl[:, HNST:NST])
                dtu = t1s
                e.tensor_mul(dtu, dtsl, unb[:][:, dc * TC:(dc + 1) * TC])
                dtu_b = dtu.unsqueeze(1).broadcast_to([P, NST, TC])
                bbc_v = bbc[:].rearrange("p (n t) -> p n t", t=TC)
                xbuf_v = xbuf.rearrange("p (n t) -> p n t", t=TC)
                e.tensor_tensor(out=xbuf_v, in0=dtu_b, in1=bbc_v, op=ALU.mult)
                if c > 0:
                    e.tensor_tensor(out=xbuf_v[:, :, 0], in0=xbuf_v[:, :, 0],
                                    in1=fix16[:], op=ALU.add)
                e.memset(av0[:, :, 0:1], 0.0)
                dve.tick(e.memset(av1[:, :, 0:1], 0.0))
                e.wait_ge(sem_dve, dve.n)
                e.tensor_tensor_scan(
                    out=OUT0R, data0=A0R, data1=xbuf[:, 0:HNST * TC],
                    initial=0.0, op0=ALU.mult, op1=ALU.add)
                dve.tick(e.tensor_tensor_scan(
                    out=A0R, data0=A1R, data1=xbuf[:, HNST * TC:],
                    initial=0.0, op0=ALU.mult, op1=ALU.add))
                e.wait_ge(sem_cbc, sy_cbc)
                if dc == 0:
                    if slot_guard_pe:
                        e.wait_ge(sem_pe, slot_guard_pe)
                    if slot_guard_out:
                        e.wait_ge(sem_out, slot_guard_out)
                hv0 = OUT0R.rearrange("p (n t) -> p n t", t=TC)
                cb0 = cbc[:][:, 0:HNST * TC].rearrange("p (n t) -> p n t", t=TC)
                xb0 = xbuf[:, 0:HNST * TC]
                e.tensor_tensor(
                    out=xb0.rearrange("p (n t) -> p n t", t=TC),
                    in0=hv0, in1=cb0, op=ALU.mult)
                dve.tick(e.tensor_copy(
                    out=hlast[:][:, dc * NST:dc * NST + HNST],
                    in_=hv0[:, :, TC - 1]))
                hv1 = A0R.rearrange("p (n t) -> p n t", t=TC)
                cb1 = cbc[:][:, HNST * TC:].rearrange("p (n t) -> p n t", t=TC)
                xb1 = xbuf[:, HNST * TC:]
                e.tensor_tensor(out=xb1.rearrange("p (n t) -> p n t", t=TC),
                                in0=hv1, in1=cb1, op=ALU.mult)
                vin, width = xbuf, NST * TC
                while width > 2 * TC:
                    vv = vin.rearrange("p (j two t) -> p j two t", two=2, t=TC)
                    outw = width // 2
                    vout = xbuf[:, 0:outw]
                    vo = vout.rearrange("p (j t) -> p j t", t=TC)
                    e.tensor_tensor(out=vo, in0=vv[:, :, 0, :],
                                    in1=vv[:, :, 1, :], op=ALU.add)
                    vin, width = vout, outw
                vv = vin.rearrange("p (j two t) -> p j two t", two=2, t=TC)
                e.tensor_tensor(out=ysl, in0=vv[:, 0, 0, :],
                                in1=vv[:, 0, 1, :], op=ALU.add)
                e.tensor_copy(
                    out=hlast[:][:, dc * NST + HNST:(dc + 1) * NST],
                    in_=hv1[:, :, TC - 1])
                dve.tick(e.scalar_tensor_tensor(
                    out=ysl, in0=unb[:][:, dc * TC:(dc + 1) * TC],
                    scalar=dskp[:][:, dc:dc + 1], in1=ysl,
                    op0=ALU.mult, op1=ALU.add))
            DVE(_)
            dve_scan_prev = dve.n
            t_ym0_prev = dve.n - 1
            t_scans_prev = dve.n - 2
            if c == 0 and dc >= 2:
                emit_u_tile(2 * dc)
                emit_u_tile(2 * dc + 1)
                emit_cast_stores((dc - 2) * 10, (dc - 1) * 10)
            if c + 1 < NCH and dc == (7 if c == 0 else 3):
                emit_head_stats_mid(c + 1)
            if c >= 1 and dc == 5:
                emit_down_part1(c - 1, gu_state[("gudone", c - 1)])
            if c >= 1 and dc == 6:
                emit_down_part1b(c - 1)
            if c >= 1 and dc == 7:
                emit_down_part1c(c - 1)
        dve_mamba = dve.n

        if c >= 1:
            emit_down_part2a(c - 1)
        if c + 1 < NCH:
            emit_head_bcast(c + 1, t_scans_prev)
            emit_head_proj(c + 1, dve_mamba)

        bc2 = emit_stats_rows_bcast(
            lambda i: ybfv[:, i * TC:(i + 1) * TC],
            [(sem_dve, dve_mamba)], src_bf16=True)

        def _(e, bc2=bc2, c0=c0, ybfv=ybfv, slot=slot):
            e.wait_ge(sem_act, bc2)
            for dc in range(DC):
                e.tensor_mul(t1s, ybfv[:, dc * TC:(dc + 1) * TC], rstd_bc)
                e.tensor_sub(t1s, t1s, mr_bc)
                xsl = xT[:][:, dc * T + c0:dc * T + c0 + TC]
                e.tensor_tensor(out=xsl, in0=xsl, in1=t1s, op=ALU.add)
                ins = e.tensor_copy(out=slot[:, dc * TC:(dc + 1) * TC],
                                    in_=xsl)
                if dc == DC - 1:
                    dve.tick(ins)
        DVE(_)
        dve_x2 = dve.n

        bc3 = emit_stats_rows_bcast(
            lambda i: slot[:, i * TC:(i + 1) * TC],
            [(sem_dve, dve_x2)], src_bf16=True)

        def _(e, bc3=bc3, slot=slot):
            e.wait_ge(sem_act, bc3)
            for dc in range(DC):
                e.tensor_mul(t1s, slot[:, dc * TC:(dc + 1) * TC], rstd_bc)
                ins = e.tensor_sub(slot[:, dc * TC:(dc + 1) * TC], t1s, mr_bc)
                if dc == DC - 1:
                    dve.tick(ins)
        DVE(_)
        dve_chunk_end = dve.n
        gu_state["x2n_ready"] = dve.n

    emit_down_part2out(NCH - 2)
    emit_gu_batch(NCH - 1, 0, MC)
    emit_down_part1(NCH - 1, gu_state[("gudone", NCH - 1)])
    emit_down_part1b(NCH - 1)
    emit_down_part1c(NCH - 1)
    emit_down_part2a(NCH - 1)
    emit_down_part2out(NCH - 1)

    def _(e):
        e.wait_ge(sem_out, outc.n)
    SY(_)

    es.close()
    return nc


_PROG = None


def _get_prog():
    global _PROG
    if _PROG is None:
        _PROG = build_program()
    return _PROG


def _prep_host(inputs):
    W_xdt = inputs["L0_attn_W_xdt"]
    W_dt = inputs["L0_attn_W_dt"]
    W_B = inputs["L0_attn_W_B"]
    W_C = inputs["L0_attn_W_C"]
    A_log = inputs["L0_attn_A_log"]
    b_dt = inputs["L0_attn_b_dt"]
    D = inputs["L0_attn_D"]
    wg = inputs["L0_ffn_w_gate"]
    wu = inputs["L0_ffn_w_up"]
    wd = inputs["L0_ffn_w_down"]

    def dchunk_cols(a2d, inner):
        return np.ascontiguousarray(
            a2d.reshape(DC, P, inner).transpose(1, 0, 2).reshape(P, DC * inner))

    wxdtT = dchunk_cols(np.ascontiguousarray(W_xdt.T), R).astype(np.float32)
    wdtT = np.ascontiguousarray(W_dt.T).astype(np.float32)
    wbT = dchunk_cols(np.ascontiguousarray(W_B.T), NST).astype(np.float32)
    wcT = dchunk_cols(np.ascontiguousarray(W_C.T), NST).astype(np.float32)
    alog_p = dchunk_cols(A_log, NST).astype(np.float32)
    bdt_p = np.ascontiguousarray(b_dt.reshape(DC, P).T).astype(np.float32)
    d_p = np.ascontiguousarray(D.reshape(DC, P).T).astype(np.float32)

    def prep_lhsT(aT, kc, mc):
        a4 = aT.reshape(kc, P, mc, P)
        return np.ascontiguousarray(
            a4.transpose(2, 1, 0, 3).reshape(mc, P, kc * P))

    wg_prep = prep_lhsT(np.ascontiguousarray(wg.T), DC, MC).astype(np.int32)
    wu_prep = prep_lhsT(np.ascontiguousarray(wu.T), DC, MC).astype(np.int32)
    wd_prep = prep_lhsT(np.ascontiguousarray(wd.T), MC, DC).astype(np.int32)

    return dict(
        wg_prep=wg_prep, wu_prep=wu_prep, wd_prep=wd_prep,
        wxdtT=wxdtT, wdtT=wdtT, wbT=wbT, wcT=wcT, alog_p=alog_p,
        bdt_p=bdt_p, d_p=d_p,
        bxdt=np.ascontiguousarray(
            inputs["L0_attn_b_xdt"].reshape(R, 1)).astype(np.float32),
        bb=np.ascontiguousarray(
            inputs["L0_attn_b_B"].reshape(NST, 1)).astype(np.float32),
        bc=np.ascontiguousarray(
            inputs["L0_attn_b_C"].reshape(NST, 1)).astype(np.float32),
        idf=np.eye(P, dtype=np.float32),
        ones_b=np.ones((P, 1), dtype=BF),
        ones_row=np.ones((1, P), dtype=BF),
    )


def kernel(**inputs):
    u = np.asarray(inputs["L0_attn_u"], dtype=np.float32)
    common = _prep_host({k: np.asarray(v) for k, v in inputs.items()})
    nc = _get_prog()
    in_maps = [dict(common, u=np.ascontiguousarray(u[b])) for b in range(8)]
    res = run_bass_kernel_spmd(nc, in_maps, core_ids=list(range(8)))
    return np.stack([np.asarray(r["y_out"], dtype=np.float32)
                     for r in res.results], axis=0)
